# version 06
# baseline (speedup 1.0000x reference)
import numpy as np
import ml_dtypes
from contextlib import ExitStack

import concourse.bass as bass
import concourse.mybir as mybir
from concourse.bass_utils import run_bass_kernel_spmd

F32 = mybir.dt.float32
BF16 = mybir.dt.bfloat16
AF = mybir.ActivationFunctionType
ALU = mybir.AluOpType

P = 128
T = 256
NT = 16
SEQ = 4096
D = 1024
NB = 8
CH = 64
CPT = T // CH
FF = 4096
NFB = 32
PROJ_M = 26
NEGBIG = -1.0e30

CC_ID = 0
CC_ONES1 = 128
CC_ONESR = 129
CCORE_W = 260
P1_LSE = 0
P1_WD = 128
P1_MASK = 256
P1_BLK = 2304
P1_BCH = 2560
CP1_W = 3584

K_RMS = 0
K_GB = 8
K_E6 = 16
K_E5 = 17
K_N1 = 18
K_NH = 19
CF_W = 24


class Sem:
    def __init__(self, nc, name):
        self.h = nc.alloc_semaphore(name)
        self.v = 0

    def inc(self, instr, n=1):
        instr.then_inc(self.h, n)
        self.v += n
        return self.v

    def wait(self, eng, v=None):
        eng.wait_ge(self.h, self.v if v is None else v)


def build(debug=False, n_tiles=NT, phase2=True):
    nc = bass.Bass("TRN2", target_bir_lowering=False)
    te, ve, sc, gp, sy = nc.tensor, nc.vector, nc.scalar, nc.gpsimd, nc.sync

    xT = nc.dram_tensor("xT", [D, SEQ], F32, kind="ExternalInput").ap()
    w_in = nc.dram_tensor("w_in", [D, 3328], F32, kind="ExternalInput").ap()
    w_out = nc.dram_tensor("w_out", [D, D], F32, kind="ExternalInput").ap()
    w_fcT = nc.dram_tensor("w_fcT", [D, FF], F32, kind="ExternalInput").ap()
    w_projT = nc.dram_tensor("w_projT", [FF, D], F32, kind="ExternalInput").ap()
    cbf_d = nc.dram_tensor("cbf", [P, CCORE_W], BF16, kind="ExternalInput").ap()
    cp1_d = nc.dram_tensor("cp1", [P, CP1_W], BF16, kind="ExternalInput").ap()
    cf32_d = nc.dram_tensor("cf32", [P, CF_W], F32, kind="ExternalInput").ap()
    outT = nc.dram_tensor("outT", [D, SEQ], F32, kind="ExternalOutput").ap()
    spill_kind = "ExternalOutput" if debug else "Internal"
    y1n_sp = nc.dram_tensor("y1n_sp", [D, SEQ], F32, kind=spill_kind).ap()
    dbg = {}
    if debug:
        for name in ["d_cx", "d_ck", "d_y1", "d_den"]:
            dbg[name] = nc.dram_tensor(name, [P, 2048], F32, kind="ExternalOutput").ap()
        for name in ["d_w", "d_af", "d_ltx", "d_ltk"]:
            dbg[name] = nc.dram_tensor(name, [P, 2048], BF16, kind="ExternalOutput").ap()

    xT_r = xT.rearrange("(jb p) t -> p jb t", p=P)
    sp_r = y1n_sp.rearrange("(jb p) t -> p jb t", p=P)
    out_r = outT.rearrange("(jb p) t -> p jb t", p=P)

    s_x = Sem(nc, "s_x")
    s_w = Sem(nc, "s_w")
    s_w2 = Sem(nc, "s_w2")
    s_c = Sem(nc, "s_c")
    s_sp = Sem(nc, "s_sp")
    s_spi = Sem(nc, "s_spi")
    s_out = Sem(nc, "s_out")
    s_ext = Sem(nc, "s_ext")
    s_pe = Sem(nc, "s_pe")
    s_v = Sem(nc, "s_v")
    s_sc = Sem(nc, "s_sc")
    s_gp = Sem(nc, "s_gp")

    cbf = nc.alloc_sbuf_tensor("cbf_sb", [P, CCORE_W], BF16)
    cf = nc.alloc_sbuf_tensor("cf32_sb", [P, CF_W], F32)
    sm = nc.alloc_sbuf_tensor("sm", [16, 6 * T], F32)
    smb = nc.alloc_sbuf_tensor("smb", [16, 3 * T], BF16)


    ph1_psum = ExitStack()
    bank = [ph1_psum.enter_context(nc.psum_tensor(f"pb{i}", [P, 512], F32)) for i in range(7)]
    ps7 = ph1_psum.enter_context(nc.psum_tensor("pb7", [P, 1024], BF16))

    ph1 = ExitStack()

    def sb1(name, shape, dt):
        return ph1.enter_context(nc.sbuf_tensor(name, shape, dt))

    w_in_bf = sb1("w_in_bf", [P, 8 * 3328], BF16)
    w_out_bf = sb1("w_out_bf", [P, 8 * 1024], BF16)
    x_sb = [sb1(f"x{i}", [P, NB * T], F32) for i in range(3)]
    h_bf = sb1("h_bf", [P, NB * T], BF16)
    g_sb = [sb1(f"g_sb{i}", [P, NB * T], F32) for i in range(2)]
    cx = [sb1(f"cx{i}", [P, NB * T], F32) for i in range(2)]
    ck = [sb1(f"ck{i}", [P, NB * T], F32) for i in range(2)]
    k_sb = [sb1(f"k_sb{i}", [P, T], F32) for i in range(2)]
    q_bf = [sb1(f"q_bf{i}", [P, T], BF16) for i in range(2)]
    z_bf = [sb1(f"z_bf{i}", [P, NB * T], BF16) for i in range(2)]
    sc_bf = sb1("sc_bf", [P, NB * T], BF16)
    w_sb = sb1("w_sb", [P, NB * T], BF16)
    af_bf = sb1("af_bf", [P, NB * T], BF16)
    t_f32 = sb1("t_f32", [P, T], F32)
    y1_sb = sb1("y1_sb", [P, NB * T], F32)
    y1b_bf = h_bf
    y1n_st = sb1("y1n_st", [P, NB * T], F32)
    lhsT_k = sb1("lhsT_k", [P, 8 * 128], BF16)
    lhsT_x = sb1("lhsT_x", [P, 8 * 128], BF16)
    stg4 = sb1("stg4", [P, 32], BF16)
    stgx = sb1("stgx", [4, 8 * 128], BF16)
    cp1 = sb1("cp1_sb", [P, CP1_W], BF16)
    iwb = sb1("iwb", [16, T], BF16)
    rdb_ = sb1("rdb", [16, T], BF16)

    print('PH1 sbuf remaining:', nc.sbuf_bytes_remaining)

    sy.dma_start(out=cbf[:], in_=cbf_d).then_inc(s_c.h, 16)
    sy.dma_start(out=cp1[:], in_=cp1_d).then_inc(s_c.h, 16)
    sy.dma_start(out=cf[:], in_=cf32_d).then_inc(s_c.h, 16)
    s_c.v += 48
    win_r = w_in.rearrange("(kb p) m -> p kb m", p=P)
    gp.dma_start(out=w_in_bf[:].rearrange("p (kb m) -> p kb m", kb=8), in_=win_r).then_inc(s_w.h, 16)
    wout_r = w_out.rearrange("(kb p) m -> p kb m", p=P)
    gp.dma_start(out=w_out_bf[:].rearrange("p (kb m) -> p kb m", kb=8), in_=wout_r).then_inc(s_w.h, 16)
    s_w.v += 32
    s_w.wait(ve)
    s_c.wait(ve)
    for kb in range(NB):
        ins = ve.tensor_scalar(w_in_bf[:, kb * 3328:(kb + 1) * 3328],
                               w_in_bf[:, kb * 3328:(kb + 1) * 3328],
                               cf[:, K_RMS + kb:K_RMS + kb + 1], None, ALU.mult)
    s_v.inc(ins)
    s_v.inc(ve.memset(lhsT_k[:], 0.0))
    s_v.inc(ve.memset(lhsT_x[:], 0.0))
    x_loaded = [None, None, None]
    for i in range(min(3, n_tiles)):
        sy.dma_start(out=x_sb[i][:].rearrange("p (jb t) -> p jb t", jb=NB),
                     in_=xT_r[:, :, i * T:(i + 1) * T]).then_inc(s_x.h, 16)
        s_x.v += 16
        x_loaded[i] = s_x.v

    spill_busy = [None]
    wv_after_rms = s_v.v

    st1 = {}
    for it in range(n_tiles + 2):
        f = it
        b = it - 1
        tb1 = it - 2
        have_f = f < n_tiles
        have_b = 0 <= b < n_tiles
        have_tb1 = 0 <= tb1 < n_tiles

        if have_f:
            fpp = f % 2
            fxp = f % 3
            xa = x_sb[fxp]
            s_x.wait(sc, x_loaded[fxp])
            s_c.wait(sc)
            if b >= 1:
                s_pe.wait(sc, st1[b - 1]["vst1"])
            vsq = s_sc.inc(sc.activation(sc_bf[:], xa[:], AF.Square))
            s_sc.wait(te, vsq)
            s_c.wait(te)
            if b >= 1:
                s_v.wait(te, st1[b - 1]["y1v"])
            for kb in range(NB):
                ins = te.matmul(bank[2][0:1, 0:T], cbf[:, CC_ONES1:CC_ONES1 + 1],
                                sc_bf[:, kb * T:(kb + 1) * T], start=(kb == 0), stop=(kb == NB - 1))
            vsumsq = s_pe.inc(ins)
            s_pe.wait(sc, vsumsq)
            sc.activation(sm[0:1, 0:T], bank[2][0:1, 0:T], AF.Ln,
                          bias=cf[0:1, K_E6:K_E6 + 1], scale=1.0 / D)
            vr = s_sc.inc(sc.activation(smb[0:1, 0:T], sm[0:1, 0:T], AF.Exp, scale=-0.5))
            s_sc.wait(te, vr)
            vrbc = s_pe.inc(te.matmul(bank[3][:, 0:T], cbf[0:1, CC_ONESR:CC_ONESR + P],
                                      smb[0:1, 0:T], start=True, stop=True))
            s_pe.wait(ve, vrbc)
            s_x.wait(ve, x_loaded[fxp])
            if b >= 1:
                s_pe.wait(ve, st1[b - 1]["vst1"])
            if it == 0:
                s_v.wait(ve, wv_after_rms)
            for kb in range(NB):
                ins = ve.tensor_tensor(h_bf[:, kb * T:(kb + 1) * T], xa[:, kb * T:(kb + 1) * T],
                                       bank[3][:, 0:T], ALU.mult)
            vh = s_v.inc(ins)
            st1[f] = {"vh": vh, "vsumsq": vsumsq}

        if have_b:
            bpp = b % 2
            bxp = b % 3
            sb_prev = st1[b]

        def back_vstep(stepname):
            if not have_b:
                return
            bs = st1[b]
            if stepname == "vx":
                for sem, val in bs["g_marks"]:
                    sem.wait(ve, val)
                for jb in range(NB):
                    ins = ve.scalar_tensor_tensor(cx[bpp][:, jb * T:(jb + 1) * T],
                                                  g_sb[bpp][:, jb * T:(jb + 1) * T], 1.0,
                                                  cx[bpp][:, jb * T:(jb + 1) * T],
                                                  ALU.subtract, ALU.mult)
                bs["vvx"] = s_v.inc(ins)
            elif stepname == "vk":
                for jb in range(NB):
                    ins = ve.scalar_tensor_tensor(ck[bpp][:, jb * T:(jb + 1) * T],
                                                  g_sb[bpp][:, jb * T:(jb + 1) * T], 1.0,
                                                  k_sb[bpp][:], ALU.subtract, ALU.mult)
                bs["vvk"] = s_v.inc(ins)
            elif stepname == "ext":
                C0b = b * CPT
                for jb in range(NB):
                    ve.tensor_scalar(lhsT_k[0:64, jb * 128 + C0b:jb * 128 + C0b + CPT],
                                     ck[bpp][0:64, jb * T:(jb + 1) * T:CH], -0.125, None, ALU.mult)
                    ve.tensor_scalar(lhsT_k[64:128, jb * 128 + 64 + C0b:jb * 128 + 64 + C0b + CPT],
                                     ck[bpp][64:128, jb * T:(jb + 1) * T:CH], -0.125, None, ALU.mult)
                    ins = ve.tensor_scalar(stg4[:, jb * CPT:(jb + 1) * CPT],
                                           cx[bpp][:, jb * T:(jb + 1) * T:CH], -1.0, None, ALU.mult)
                bs["vex"] = s_v.inc(ins)
                s_v.wait(gp, bs["vex"])
                bs["vgm"] = s_gp.inc(gp.memset(g_sb[bpp][:, 0:NB * T:CH], 0.0))
            elif stepname == "scan":
                s_gp.wait(ve, bs["vgm"])
                ve.tensor_tensor_scan(cx[bpp][:], g_sb[bpp][:], cx[bpp][:], 0.0, ALU.mult, ALU.add)
                ins = ve.tensor_tensor_scan(ck[bpp][:], g_sb[bpp][:], ck[bpp][:], 0.0,
                                            ALU.mult, ALU.add)
                bs["vscan"] = s_v.inc(ins)
            elif stepname == "pfull":
                if have_f:
                    s_pe.wait(ve, st1[f]["vsumsq"])
                elif b >= 1:
                    s_pe.wait(ve, st1[b - 1]["vst1"])
                for sem, val in bs["qk_marks"]:
                    sem.wait(ve, val)
                for jb in range(NB):
                    ins = ve.tensor_tensor(sc_bf[:, jb * T:(jb + 1) * T], q_bf[bpp][:],
                                           ck[bpp][:, jb * T:(jb + 1) * T], ALU.mult)
                bs["vP"] = s_v.inc(ins)

        def tail_part(part):
            if not have_tb1:
                return
            ts_ = st1[tb1]
            if part == 1:
                s_pe.wait(ve, ts_["vst1"])
                ve.tensor_scalar(sm[0:1, 2 * T:3 * T], bank[6][0:1, 0:T], 1.0 / D, None, ALU.mult)
                ve.tensor_scalar(sm[0:1, 3 * T:4 * T], bank[6][0:1, T:2 * T], 1.0 / D,
                                 None, ALU.mult)
                ve.tensor_tensor(sm[0:1, 0:T], sm[0:1, 2 * T:3 * T], sm[0:1, 2 * T:3 * T], ALU.mult)
                vvar = s_v.inc(ve.tensor_tensor(sm[0:1, 3 * T:4 * T], sm[0:1, 3 * T:4 * T],
                                                sm[0:1, 0:T], ALU.subtract))
                ts_["vvar"] = vvar
                s_v.wait(sc, vvar)
                sc.activation(sm[0:1, 0:T], sm[0:1, 3 * T:4 * T], AF.Ln,
                              bias=cf[0:1, K_E5:K_E5 + 1])
                vr1 = s_sc.inc(sc.activation(sm[0:1, T:2 * T], sm[0:1, 0:T], AF.Exp, scale=-0.5))
                s_sc.wait(ve, vr1)
                ve.tensor_copy(smb[0:1, T:2 * T], sm[0:1, 2 * T:3 * T])
                ts_["vmb"] = s_v.inc(ve.tensor_copy(smb[0:1, 2 * T:3 * T], sm[0:1, T:2 * T]))
            elif part == 2:
                s_v.wait(te, ts_["vmb"])
                te.matmul(bank[0][:, 0:T], cbf[0:1, CC_ONESR:CC_ONESR + P], smb[0:1, T:2 * T],
                          start=True, stop=True)
                ts_["vbc"] = s_pe.inc(te.matmul(bank[1][:, 0:T], cbf[0:1, CC_ONESR:CC_ONESR + P],
                                                smb[0:1, 2 * T:3 * T], start=True, stop=True))
            else:
                s_pe.wait(ve, ts_["vbc"])
                nonlocal_spill = spill_busy[0]
                if nonlocal_spill is not None:
                    s_sp.wait(ve, nonlocal_spill)
                for jb in range(NB):
                    ve.tensor_tensor(y1n_st[:, jb * T:(jb + 1) * T],
                                     y1_sb[:, jb * T:(jb + 1) * T], bank[0][:, 0:T], ALU.subtract)
                for jb in range(NB):
                    ins2 = ve.tensor_tensor(y1n_st[:, jb * T:(jb + 1) * T],
                                            y1n_st[:, jb * T:(jb + 1) * T], bank[1][:, 0:T],
                                            ALU.mult)
                vln = s_v.inc(ins2)
                ts_["vln"] = vln
                s_v.wait(sy, vln)
                sy.dma_start(out=sp_r[:, :, tb1 * T:(tb1 + 1) * T],
                             in_=y1n_st[:].rearrange("p (jb t) -> p jb t", jb=NB)
                             ).then_inc(s_sp.h, 16)
                s_sp.v += 16
                spill_busy[0] = s_sp.v

        if have_f:
            ms_order = (list(range(16, 24)) + list(range(0, 8)) + list(range(8, 16))
                        + [24, 25])
            groups = [ms_order[gg:gg + 4] for gg in range(0, PROJ_M, 4)]
            evac_done = {}
            g_marks = []
            qk_marks = []
            s_v.wait(te, st1[f]["vh"])
            if it == 0:
                s_w.wait(te)
                s_v.wait(te, wv_after_rms)
            back_steps = {1: "vx", 2: "vk", 3: "ext", 4: "scan", 5: "pfull"}
            for gi, grp in enumerate(groups):
                pb = [bank[2], bank[3]] if gi % 2 == 0 else [bank[4], bank[5]]
                if gi >= 2:
                    for sem, val in evac_done[gi - 2]:
                        sem.wait(te, val)
                for mi, m in enumerate(grp):
                    pt = pb[mi // 2][:, (mi % 2) * T:(mi % 2) * T + T]
                    for kb in range(NB):
                        ins = te.matmul(pt, w_in_bf[:, kb * 3328 + m * 128:kb * 3328 + (m + 1) * 128],
                                        h_bf[:, kb * T:(kb + 1) * T],
                                        start=(kb == 0 and mi in (0, 2)), stop=(kb == NB - 1))
                vpg = s_pe.inc(ins)
                s_pe.wait(sc, vpg)
                s_pe.wait(ve, vpg)
                last_sc = None
                last_v = None
                for mi, m in enumerate(grp):
                    pt = pb[mi // 2][:, (mi % 2) * T:(mi % 2) * T + T]
                    if m < 8:
                        last_sc = sc.activation(z_bf[fpp][:, m * T:(m + 1) * T], pt, AF.Sigmoid)
                    elif m < 16:
                        last_v = ve.tensor_copy(cx[fpp][:, (m - 8) * T:(m - 7) * T], pt)
                    elif m < 24:
                        last_sc = sc.activation(g_sb[fpp][:, (m - 16) * T:(m - 15) * T], pt,
                                                AF.Sigmoid,
                                                bias=cf[:, K_GB + (m - 16):K_GB + (m - 16) + 1])
                    elif m == 24:
                        last_sc = sc.activation(q_bf[fpp][:], pt, AF.Copy)
                    else:
                        vq = s_sc.inc(last_sc)
                        last_sc = None
                        s_sc.wait(ve, vq)
                        qk_marks.append((s_sc, vq))
                        last_v = ve.tensor_copy(k_sb[fpp][:], pt)
                marks = []
                if last_sc is not None:
                    marks.append((s_sc, s_sc.inc(last_sc)))
                if last_v is not None:
                    marks.append((s_v, s_v.inc(last_v)))
                evac_done[gi] = marks
                if gi in (0, 1):
                    g_marks.extend(marks)
                if gi == 6:
                    qk_marks.extend(marks)
                if gi in back_steps:
                    back_vstep(back_steps[gi])
                if gi == 1:
                    tail_part(1)
                elif gi == 3:
                    tail_part(2)
                elif gi == 5:
                    tail_part(3)
            st1[f]["g_marks"] = g_marks
            st1[f]["qk_marks"] = qk_marks
            st1[f]["evac56"] = evac_done[5] + evac_done[6]
            st1[f]["proj_done"] = s_pe.v
        else:
            for stepname in ("vx", "vk", "ext", "scan", "pfull"):
                back_vstep(stepname)
            for part_ in (1, 2, 3):
                tail_part(part_)

        if have_b:
            bs = st1[b]
            C0 = b * CPT
            t0b = b * T
            s_v.wait(te, bs["vex"])
            for jb in range(NB):
                ins = te.transpose(ps7[0:CPT, jb * 128:(jb + 1) * 128],
                                   stg4[:, jb * CPT:(jb + 1) * CPT], cbf[:, CC_ID:CC_ID + P])
            vtr = s_pe.inc(ins)
            s_pe.wait(ve, vtr)
            vstg = s_v.inc(ve.tensor_copy(stgx[0:CPT, :], ps7[0:CPT, 0:8 * 128]))
            s_v.wait(sy, vstg)
            stgx_r = stgx[:].rearrange("c (jb ff) -> c jb ff", jb=NB)
            ltx_r = lhsT_x[:].rearrange("p (jb ff) -> p jb ff", jb=NB)
            sy.dma_start(out=ltx_r[C0:C0 + CPT, :, 0:64],
                         in_=stgx_r[0:CPT, :, 0:64]).then_inc(s_ext.h, 16)
            sy.dma_start(out=ltx_r[64 + C0:64 + C0 + CPT, :, 64:128],
                         in_=stgx_r[0:CPT, :, 64:128]).then_inc(s_ext.h, 16)
            s_ext.v += 32
            s_v.wait(te, bs["vP"])
            if b >= 1:
                s_v.wait(te, st1[b - 1]["vvar"])
            for j in range(NB):
                ins = te.matmul(bank[6][0:16, 0:T], cp1[:, P1_LSE + j * 16:P1_LSE + (j + 1) * 16],
                                sc_bf[:, j * T:(j + 1) * T], start=(j == 0), stop=(j == NB - 1))
            vlse = s_pe.inc(ins)
            s_ext.wait(te)
            for sem, val in bs["evac56"]:
                sem.wait(te, val)
            exp_done = {}

            def emit_wd(j):
                s_sc.wait(te, exp_done[j])
                ins = te.matmul(bank[6][0:16, T:2 * T], cp1[:, P1_WD + j * 16:P1_WD + (j + 1) * 16],
                                w_sb[:, j * T:(j + 1) * T], start=False, stop=(j == NB - 1))
                s_pe.inc(ins)

            for j in range(NB):
                sb_ = bank[j % 4]
                if j < 2 and b >= 1 and "vln" in st1[b - 1]:
                    s_v.wait(te, st1[b - 1]["vln"])
                if j >= 4:
                    s_sc.wait(te, exp_done[j - 4])
                te.matmul(sb_[:, 0:T], lhsT_k[:, j * 128:(j + 1) * 128], q_bf[bpp][:],
                          start=True, stop=False)
                ins = te.matmul(sb_[:, 0:T], cp1[0:4, P1_MASK + b * 128:P1_MASK + (b + 1) * 128],
                                cp1[0:4, P1_BLK:P1_BLK + T], start=False, stop=True)
                vsc_ = s_pe.inc(ins)
                s_pe.wait(sc, vsc_)
                exp_done[j] = s_sc.inc(sc.activation(w_sb[:, j * T:(j + 1) * T], sb_[:, 0:T], AF.Exp))
                if j >= 3:
                    emit_wd(j - 3)
            for j in range(NB - 3, NB):
                emit_wd(j)
            vwd = s_pe.v
            s_pe.wait(sc, vwd)
            viw = s_sc.inc(sc.activation(sm[0:16, 4 * T:5 * T], bank[6][0:16, 0:T], AF.Exp,
                                         scale=0.125))
            s_sc.wait(ve, viw)
            ve.tensor_tensor(sm[0:16, 5 * T:6 * T], bank[6][0:16, T:2 * T],
                             sm[0:16, 4 * T:5 * T], ALU.add)
            ve.reciprocal(sm[0:16, 3 * T:4 * T], sm[0:16, 5 * T:6 * T])
            ve.tensor_copy(rdb_[0:16, :], sm[0:16, 3 * T:4 * T])
            vden = s_v.inc(ve.tensor_copy(iwb[0:16, :], sm[0:16, 4 * T:5 * T]))
            s_v.wait(te, vden)
            pair_v = {}
            for j in range(NB):
                ob = bank[2 + (j % 2)]
                ib = bank[4 + (j % 2)]
                rb = bank[j % 2]
                if j >= 2:
                    s_v.wait(te, pair_v[j - 2])
                te.matmul(ob[:, 0:T], lhsT_x[:, j * 128:(j + 1) * 128], w_sb[:, j * T:(j + 1) * T],
                          start=True, stop=True)
                sel = cp1[0:16, P1_BCH + j * 128:P1_BCH + (j + 1) * 128]
                te.matmul(ib[:, 0:T], sel, iwb[0:16, 0:T], start=True, stop=True)
                ins = te.matmul(rb[:, 0:T], sel, rdb_[0:16, 0:T], start=True, stop=True)
                vj = s_pe.inc(ins)
                s_pe.wait(ve, vj)
                ve.tensor_tensor(t_f32[:], ib[:, 0:T], cx[bpp][:, j * T:(j + 1) * T], ALU.mult)
                ve.tensor_tensor(t_f32[:], ob[:, 0:T], t_f32[:], ALU.subtract)
                ve.tensor_tensor(t_f32[:], t_f32[:], rb[:, 0:T], ALU.mult)
                ins = ve.tensor_tensor(af_bf[:, j * T:(j + 1) * T], t_f32[:],
                                       z_bf[bpp][:, j * T:(j + 1) * T], ALU.mult)
                pair_v[j] = s_v.inc(ins)
            vaf = s_v.v
            s_v.wait(te, vaf)
            if it == 1:
                s_w.wait(te)
            y1v = None
            vsq1 = None
            vyb = None
            for gi2 in range(2):
                pb2 = [bank[2], bank[3]] if gi2 == 0 else [bank[4], bank[5]]
                for mi in range(4):
                    m = gi2 * 4 + mi
                    pt = pb2[mi // 2][:, (mi % 2) * T:(mi % 2) * T + T]
                    for kb in range(NB):
                        ins = te.matmul(pt, w_out_bf[:, kb * 1024 + m * 128:kb * 1024 + (m + 1) * 128],
                                        af_bf[:, kb * T:(kb + 1) * T],
                                        start=(kb == 0 and mi in (0, 2)), stop=(kb == NB - 1))
                vwo = s_pe.inc(ins)
                s_pe.wait(ve, vwo)
                for mi in range(4):
                    m = gi2 * 4 + mi
                    pt = pb2[mi // 2][:, (mi % 2) * T:(mi % 2) * T + T]
                    ins = ve.tensor_tensor(y1_sb[:, m * T:(m + 1) * T], pt,
                                           x_sb[bxp][:, m * T:(m + 1) * T], ALU.add)
                y1v = s_v.inc(ins)
                half = slice(gi2 * 4 * T, (gi2 + 1) * 4 * T)
                s_v.wait(sc, y1v)
                if gi2 == 0:
                    s_pe.wait(sc, vlse)
                vsq1 = s_sc.inc(sc.activation(sc_bf[:, half], y1_sb[:, half], AF.Square))
                if gi2 == 0 and have_f:
                    s_pe.wait(ve, st1[f]["proj_done"])
                vyb = s_v.inc(ve.tensor_copy(y1b_bf[:, half], y1_sb[:, half]))
            bs["y1v"] = y1v
            if b + 3 < n_tiles:
                s_v.wait(sy, y1v)
                sy.dma_start(out=x_sb[b % 3][:].rearrange("p (jb t) -> p jb t", jb=NB),
                             in_=xT_r[:, :, (b + 3) * T:(b + 4) * T]).then_inc(s_x.h, 16)
                s_x.v += 16
                x_loaded[b % 3] = s_x.v
            s_sc.wait(te, vsq1)
            s_v.wait(te, vyb)
            s_v.wait(te, vden)
            for kb in range(NB):
                te.matmul(bank[6][0:1, 0:T], cbf[:, CC_ONES1:CC_ONES1 + 1],
                          y1b_bf[:, kb * T:(kb + 1) * T], start=(kb == 0), stop=False)
            for kb in range(NB):
                ins = te.matmul(bank[6][0:1, T:2 * T], cbf[:, CC_ONES1:CC_ONES1 + 1],
                                sc_bf[:, kb * T:(kb + 1) * T], start=False, stop=(kb == NB - 1))
            vst1 = s_pe.inc(ins)
            bs["vst1"] = vst1
            bs["pair_last"] = pair_v[NB - 1]

    if not phase2:
        s_sp.wait(sy)
        s_out.wait(sy)
        return nc

    pe_fin, v_fin, sc_fin, gp_fin = s_pe.v, s_v.v, s_sc.v, s_gp.v
    for eng in (gp, sy):
        s_pe.wait(eng, pe_fin)
        s_v.wait(eng, v_fin)
        s_sc.wait(eng, sc_fin)
        s_sp.wait(eng)
    ph1.close()
    ph1_psum.close()

    ph2 = ExitStack()

    def sb2(name, shape, dt):
        return ph2.enter_context(nc.sbuf_tensor(name, shape, dt))

    w_fc_bf = sb2("w_fc_bf", [P, 8 * FF], BF16)
    w_pj_bf = sb2("w_pj_bf", [P, NFB * 1024], BF16)
    yn_sb = [sb2(f"yn{i}", [P, NB * T], F32) for i in range(2)]
    yn_bf = sb2("yn_bf", [P, NB * T], BF16)
    h2_bf = sb2("h2_bf", [P, 6 * T], BF16)
    yc_sb = sb2("yc_sb", [P, NB * T], F32)
    mem_sb = sb2("mem_sb", [P, NB * T], F32)
    acc_bf = sb2("acc_bf", [P, NB * T], BF16)
    sig_bf = sb2("sig_bf", [P, NB * T], BF16)
    relu_f = sb2("relu_f", [P, T], F32)
    o_st = [sb2(f"o_st{i}", [P, NB * T], F32) for i in range(2)]
    bank2 = [ph2.enter_context(nc.psum_tensor(f"qb{i}", [P, 512], F32)) for i in range(8)]
    print('PH2 sbuf remaining:', nc.sbuf_bytes_remaining)

    wfc_r = w_fcT.rearrange("(kb p) f -> p kb f", p=P)
    wfc_sb_r = w_fc_bf[:].rearrange("p (kb f) -> p kb f", kb=8)
    fc_chunk_v = []
    for c in range(4):
        gp.dma_start(out=wfc_sb_r[:, :, c * 1024:(c + 1) * 1024],
                     in_=wfc_r[:, :, c * 1024:(c + 1) * 1024]).then_inc(s_w2.h, 16)
        s_w2.v += 16
        fc_chunk_v.append(s_w2.v)
    wpj_r = w_projT.rearrange("(fb p) d2 -> p fb d2", p=P)
    wpj_sb_r = w_pj_bf[:].rearrange("p (fb d2) -> p fb d2", fb=NFB)
    pj_chunk_v = []
    for c in range(4):
        gp.dma_start(out=wpj_sb_r[:, c * 8:(c + 1) * 8, :],
                     in_=wpj_r[:, c * 8:(c + 1) * 8, :]).then_inc(s_w2.h, 16)
        s_w2.v += 16
        pj_chunk_v.append(s_w2.v)

    s_sp.wait(sy)
    sy.dma_start(out=yn_sb[0][:].rearrange("p (jb t) -> p jb t", jb=NB),
                 in_=sp_r[:, :, 0:T]).then_inc(s_spi.h, 16)
    s_spi.v += 16
    yn_loaded = [s_spi.v, None]
    if n_tiles > 1:
        sy.dma_start(out=yn_sb[1][:].rearrange("p (jb t) -> p jb t", jb=NB),
                     in_=sp_r[:, :, T:2 * T]).then_inc(s_spi.h, 16)
        s_spi.v += 16
        yn_loaded[1] = s_spi.v
    yn_last_read = [None, None]
    out_busy = [None, None]

    st2 = {}
    for J in range(n_tiles + 2):
        f = J
        lb = J - 1
        tb = J - 2
        have_f = f < n_tiles
        have_lb = 0 <= lb < n_tiles
        have_tb = 0 <= tb < n_tiles

        if have_f:
            pp = f % 2
            if f >= 1 and "vynb_next" in st2.get(f - 1, {}):
                vynb = st2[f - 1]["vynb_next"]
            else:
                s_spi.wait(gp, yn_loaded[pp])
                if f >= 1:
                    s_pe.wait(gp, st2[f - 1]["vfc_last"])
                vynb = s_gp.inc(gp.tensor_copy(yn_bf[:], yn_sb[pp][:]))

        if have_lb:
            ls = st2[lb]
            lpp = lb % 2
            s_pe.wait(ve, ls["vpj"])
            for mb in range(NB):
                pt = bank2[4 + mb // 2][:, (mb % 2) * T:(mb % 2) * T + T]
                ins = ve.tensor_scalar(yc_sb[:, mb * T:(mb + 1) * T], pt, -0.5, None, ALU.add)
            vyc = s_v.inc(ins)
            s_v.wait(sc, vyc)
            if lb >= 1 and "vst2" in st2.get(lb - 1, {}):
                s_pe.wait(sc, st2[lb - 1]["vst2"])
            vs1 = s_sc.inc(sc.activation(sig_bf[:], yc_sb[:], AF.Sign, bias=cf[:, K_NH:K_NH + 1]))
            s_sc.wait(ve, vs1)
            vacc = s_v.inc(ve.tensor_copy(acc_bf[:], sig_bf[:]))
            ve.tensor_scalar(mem_sb[:], yc_sb[:], 1.9, 0.45, ALU.mult, ALU.add)
            vm = s_v.inc(ve.scalar_tensor_tensor(mem_sb[:], sig_bf[:], -0.5, mem_sb[:],
                                                 ALU.mult, ALU.add))
            lstate = {"vm": vm, "vacc": vacc, "step": 2}
        else:
            lstate = None

        def ladder_step():
            if lstate is None or lstate["step"] > 5:
                return
            k = lstate["step"]
            s_v.wait(sc, lstate["vm"])
            vsg = s_sc.inc(sc.activation(sig_bf[:], mem_sb[:], AF.Sign,
                                         bias=cf[:, K_N1:K_N1 + 1]))
            s_sc.wait(ve, vsg)
            lstate["vacc"] = s_v.inc(ve.tensor_tensor(acc_bf[:], acc_bf[:], sig_bf[:], ALU.add))
            if k < 5:
                s_sc.wait(ve, vsg)
                ve.scalar_tensor_tensor(mem_sb[:], mem_sb[:], 0.9, yc_sb[:], ALU.mult, ALU.add)
                lstate["vm"] = s_v.inc(ve.scalar_tensor_tensor(mem_sb[:], sig_bf[:], -0.5,
                                                               mem_sb[:], ALU.mult, ALU.add))
            lstate["step"] = k + 1

        def tail_part(part):
            if not have_tb:
                return
            ts_ = st2[tb]
            tpp = tb % 2
            if part == 1:
                s_pe.wait(ve, ts_["vst2"])
                ve.tensor_scalar(sm[0:1, 2 * T:3 * T], bank2[3][0:1, 0:T], 1.0 / D, None, ALU.mult)
                ve.tensor_scalar(sm[0:1, 3 * T:4 * T], bank2[3][0:1, T:2 * T], 1.0 / D,
                                 None, ALU.mult)
                ve.tensor_tensor(sm[0:1, 0:T], sm[0:1, 2 * T:3 * T], sm[0:1, 2 * T:3 * T], ALU.mult)
                vvar2 = s_v.inc(ve.tensor_tensor(sm[0:1, 3 * T:4 * T], sm[0:1, 3 * T:4 * T],
                                                 sm[0:1, 0:T], ALU.subtract))
                s_v.wait(sc, vvar2)
                sc.activation(sm[0:1, 0:T], sm[0:1, 3 * T:4 * T], AF.Ln,
                              bias=cf[0:1, K_E5:K_E5 + 1])
                vr2 = s_sc.inc(sc.activation(sm[0:1, T:2 * T], sm[0:1, 0:T], AF.Exp, scale=-0.5))
                s_sc.wait(ve, vr2)
                ve.tensor_copy(smb[0:1, T:2 * T], sm[0:1, 2 * T:3 * T])
                ts_["vmb2"] = s_v.inc(ve.tensor_copy(smb[0:1, 2 * T:3 * T], sm[0:1, T:2 * T]))
            elif part == 2:
                s_v.wait(te, ts_["vmb2"])
                te.matmul(bank2[3][:, 0:T], cbf[0:1, CC_ONESR:CC_ONESR + P], smb[0:1, T:2 * T],
                          start=True, stop=True)
                ts_["vbc2"] = s_pe.inc(te.matmul(bank2[3][:, T:2 * T],
                                                 cbf[0:1, CC_ONESR:CC_ONESR + P],
                                                 smb[0:1, 2 * T:3 * T], start=True, stop=True))
            else:
                s_pe.wait(ve, ts_["vbc2"])
                for jb in range(NB):
                    ve.tensor_tensor(o_st[tpp][:, jb * T:(jb + 1) * T],
                                     o_st[tpp][:, jb * T:(jb + 1) * T], bank2[3][:, 0:T],
                                     ALU.subtract)
                for jb in range(NB):
                    ins = ve.tensor_tensor(o_st[tpp][:, jb * T:(jb + 1) * T],
                                           o_st[tpp][:, jb * T:(jb + 1) * T],
                                           bank2[3][:, T:2 * T], ALU.mult)
                vo = s_v.inc(ins)
                ts_["vo"] = vo
                s_v.wait(sy, vo)
                sy.dma_start(out=out_r[:, :, tb * T:(tb + 1) * T],
                             in_=o_st[tpp][:].rearrange("p (jb t) -> p jb t", jb=NB)
                             ).then_inc(s_out.h, 16)
                s_out.v += 16
                out_busy[tpp] = s_out.v

        if have_f:
            s_gp.wait(te, vynb)
            if have_lb:
                s_v.wait(te, vyc)
            fc_evac = {}
            for g_ in range(16):
                bk = bank2[g_ % 3]
                if g_ >= 3:
                    s_sc.wait(te, fc_evac[g_ - 3])
                s_w2.wait(te, fc_chunk_v[(2 * g_) // 8])
                for fi in (0, 1):
                    Fb = 2 * g_ + fi
                    pt = bk[:, fi * T:fi * T + T]
                    for kb in range(NB):
                        ins = te.matmul(pt, w_fc_bf[:, kb * FF + Fb * 128:kb * FF + (Fb + 1) * 128],
                                        yn_bf[:, kb * T:(kb + 1) * T],
                                        start=(kb == 0 and fi == 0), stop=(kb == NB - 1))
                vfc = s_pe.inc(ins)
                s_pe.wait(sc, vfc)
                for fi in (0, 1):
                    pt = bk[:, fi * T:fi * T + T]
                    slot = (g_ % 3) * 2 + fi
                    sc.activation(relu_f[:], pt, AF.Relu)
                    ins = sc.activation(h2_bf[:, slot * T:(slot + 1) * T], relu_f[:], AF.Square)
                fc_evac[g_] = s_sc.inc(ins)
                if g_ == 1:
                    tail_part(1)
                elif g_ == 4:
                    ladder_step()
                elif g_ == 5:
                    tail_part(2)
                elif g_ == 7:
                    ladder_step()
                elif g_ == 9:
                    tail_part(3)
                elif g_ == 10:
                    ladder_step()
                elif g_ == 13:
                    ladder_step()
                for gp_ in ([g_ - 1] if g_ >= 1 else []) + ([15] if g_ == 15 else []):
                    s_sc.wait(te, fc_evac[gp_])
                    s_w2.wait(te, pj_chunk_v[(2 * gp_ + 1) // 8])
                    for fi in (0, 1):
                        Fb = 2 * gp_ + fi
                        slot = (gp_ % 3) * 2 + fi
                        for mb in range(NB):
                            pt = bank2[4 + mb // 2][:, (mb % 2) * T:(mb % 2) * T + T]
                            ins = te.matmul(pt,
                                            w_pj_bf[:, Fb * 1024 + mb * 128:Fb * 1024 + (mb + 1) * 128],
                                            h2_bf[:, slot * T:(slot + 1) * T],
                                            start=(Fb == 0 and mb in (0, 2, 4, 6)),
                                            stop=(Fb == NFB - 1))
            vpj = s_pe.inc(ins)
            st2[f] = {"vpj": vpj, "vfc_last": vfc}
        else:
            while lstate is not None and lstate["step"] <= 5:
                ladder_step()
            for part_ in (1, 2, 3):
                tail_part(part_)

        if have_lb:
            while lstate["step"] <= 5:
                ladder_step()
            vacc = lstate["vacc"]
            if out_busy[lpp] is not None:
                s_out.wait(ve, out_busy[lpp])
            vy2 = s_v.inc(ve.scalar_tensor_tensor(o_st[lpp][:], acc_bf[:], 0.1, yn_sb[lpp][:],
                                                  ALU.mult, ALU.add))
            yn_last_read[lpp] = vy2
            if lb + 2 < n_tiles:
                s_v.wait(sy, vy2)
                sy.dma_start(out=yn_sb[lpp][:].rearrange("p (jb t) -> p jb t", jb=NB),
                             in_=sp_r[:, :, (lb + 2) * T:(lb + 3) * T]).then_inc(s_spi.h, 16)
                s_spi.v += 16
                yn_loaded[lpp] = s_spi.v
            if f + 1 < n_tiles and f < n_tiles:
                s_spi.wait(gp, yn_loaded[(f + 1) % 2])
                s_pe.wait(gp, st2[f]["vfc_last"])
                st2[f]["vynb_next"] = s_gp.inc(gp.tensor_copy(yn_bf[:], yn_sb[(f + 1) % 2][:]))
            s_v.wait(sc, vy2)
            vsq2 = s_sc.inc(sc.activation(sig_bf[:], o_st[lpp][:], AF.Square))
            vy2b = s_v.inc(ve.tensor_copy(acc_bf[:], o_st[lpp][:]))
            s_sc.wait(te, vsq2)
            s_v.wait(te, vy2b)
            if have_tb and "vo" in st2[tb]:
                s_v.wait(te, st2[tb]["vo"])
            for kb in range(NB):
                te.matmul(bank2[3][0:1, 0:T], cbf[:, CC_ONES1:CC_ONES1 + 1],
                          acc_bf[:, kb * T:(kb + 1) * T], start=(kb == 0), stop=False)
            for kb in range(NB):
                ins = te.matmul(bank2[3][0:1, T:2 * T], cbf[:, CC_ONES1:CC_ONES1 + 1],
                                sig_bf[:, kb * T:(kb + 1) * T], start=False, stop=(kb == NB - 1))
            st2[lb]["vst2"] = s_pe.inc(ins)

    s_out.wait(sy)
    s_sp.wait(sy)
    ph2.close()
    return nc


def make_consts(rms_w, gate_bias):
    cbf = np.zeros((P, CCORE_W), np.float32)
    cbf[:, CC_ID:CC_ID + P] = np.eye(P)
    cbf[:, CC_ONES1] = 1.0
    cbf[0, CC_ONESR:CC_ONESR + P] = 1.0
    cp1 = np.zeros((P, CP1_W), np.float32)
    for j in range(NB):
        cp1[0:64, P1_LSE + j * 16 + 2 * j] = -1.0
        cp1[64:128, P1_LSE + j * 16 + 2 * j + 1] = -1.0
        cp1[0:64, P1_WD + j * 16 + 2 * j] = 1.0
        cp1[64:128, P1_WD + j * 16 + 2 * j + 1] = 1.0
    cc = np.arange(64)
    for i in range(NT):
        for jj in range(CPT):
            m = (cc >= i * CPT + jj).astype(np.float32) * NEGBIG
            cp1[jj, P1_MASK + i * 128:P1_MASK + i * 128 + 64] = m
            cp1[jj, P1_MASK + i * 128 + 64:P1_MASK + (i + 1) * 128] = m
    tt = np.arange(T)
    for jj in range(CPT):
        cp1[jj, P1_BLK:P1_BLK + T] = (tt // CH == jj).astype(np.float32)
    for j in range(NB):
        cp1[2 * j, P1_BCH + j * 128:P1_BCH + j * 128 + 64] = 1.0
        cp1[2 * j + 1, P1_BCH + j * 128 + 64:P1_BCH + (j + 1) * 128] = 1.0

    cf = np.zeros((P, CF_W), np.float32)
    cf[:, K_RMS:K_RMS + 8] = rms_w.reshape(8, P).T
    cf[:, K_GB:K_GB + 8] = gate_bias.reshape(8, P).T
    cf[:, K_E6] = 1e-6
    cf[:, K_E5] = 1e-5
    cf[:, K_N1] = -1.0
    cf[:, K_NH] = -0.5
    return cbf.astype(ml_dtypes.bfloat16), cp1.astype(ml_dtypes.bfloat16), cf


def kernel(L0_attn_hidden_states, L0_attn_rms_weight, L0_attn_W_in, L0_attn_gate_bias,
           L0_attn_W_out, L0_ffn_W_fc, L0_ffn_W_proj):
    hs = np.asarray(L0_attn_hidden_states, np.float32)
    xTs = np.ascontiguousarray(np.transpose(hs, (0, 2, 1)))
    w_in = np.ascontiguousarray(np.asarray(L0_attn_W_in, np.float32))
    w_out = np.ascontiguousarray(np.asarray(L0_attn_W_out, np.float32))
    w_fcT = np.ascontiguousarray(np.asarray(L0_ffn_W_fc, np.float32).T)
    w_projT = np.ascontiguousarray(np.asarray(L0_ffn_W_proj, np.float32).T)
    cbf, cp1, cf = make_consts(np.asarray(L0_attn_rms_weight, np.float32),
                               np.asarray(L0_attn_gate_bias, np.float32))
    nc = build()
    in_maps = [{"xT": xTs[b], "w_in": w_in, "w_out": w_out, "w_fcT": w_fcT,
                "w_projT": w_projT, "cbf": cbf, "cp1": cp1, "cf32": cf} for b in range(8)]
    res = run_bass_kernel_spmd(nc, in_maps, core_ids=list(range(8)))
    out = np.stack([np.asarray(r["outT"], np.float32).T for r in res.results])
    return np.ascontiguousarray(out)


# version 08
# speedup vs baseline: 1.0966x; 1.0966x over previous
import numpy as np
import ml_dtypes
from contextlib import ExitStack

import concourse.bass as bass
import concourse.mybir as mybir
from concourse.bass_utils import run_bass_kernel_spmd

F32 = mybir.dt.float32
BF16 = mybir.dt.bfloat16
AF = mybir.ActivationFunctionType
ALU = mybir.AluOpType

P = 128
T = 256
NT = 16
SEQ = 4096
D = 1024
NB = 8
CH = 64
CPT = T // CH
FF = 4096
NFB = 32
PROJ_M = 26
NEGBIG = -1.0e30

CC_ID = 0
CC_ONES1 = 128
CC_ONESR = 129
CCORE_W = 260
P1_LSE = 0
P1_WD = 128
P1_MASK = 256
P1_BLK = 2304
P1_BCH = 2560
CP1_W = 3584

K_RMS = 0
K_GB = 8
K_E6 = 16
K_E5 = 17
K_N1 = 18
K_NH = 19
CF_W = 24


_ACT_SET_CACHE = {}


def _act_set_id(nc, set_name):
    from concourse.hw_specs import get_activation_tables
    key = (nc.m.arch, set_name)
    if key not in _ACT_SET_CACHE:
        names = list(get_activation_tables(nc.m.arch).keys())
        _ACT_SET_CACHE[key] = names.index(set_name)
    return _ACT_SET_CACHE[key]


def preload_set(nc, sc, set_name):
    import concourse.mybir as mb
    return sc.add_instruction(mb.InstLoadActFuncSet(
        name=nc.get_next_instruction_name(),
        act_func_set_id=_act_set_id(nc, set_name), ins=[], outs=[]))


class Sem:
    def __init__(self, nc, name):
        self.h = nc.alloc_semaphore(name)
        self.v = 0

    def inc(self, instr, n=1):
        instr.then_inc(self.h, n)
        self.v += n
        return self.v

    def wait(self, eng, v=None):
        eng.wait_ge(self.h, self.v if v is None else v)


def build(debug=False, n_tiles=NT, phase2=True):
    nc = bass.Bass("TRN2", target_bir_lowering=False)
    te, ve, sc, gp, sy = nc.tensor, nc.vector, nc.scalar, nc.gpsimd, nc.sync

    xT = nc.dram_tensor("xT", [D, SEQ], F32, kind="ExternalInput").ap()
    w_in = nc.dram_tensor("w_in", [D, 3328], F32, kind="ExternalInput").ap()
    w_out = nc.dram_tensor("w_out", [D, D], F32, kind="ExternalInput").ap()
    w_fcT = nc.dram_tensor("w_fcT", [D, FF], F32, kind="ExternalInput").ap()
    w_projT = nc.dram_tensor("w_projT", [FF, D], F32, kind="ExternalInput").ap()
    cbf_d = nc.dram_tensor("cbf", [P, CCORE_W], BF16, kind="ExternalInput").ap()
    cp1_d = nc.dram_tensor("cp1", [P, CP1_W], BF16, kind="ExternalInput").ap()
    cf32_d = nc.dram_tensor("cf32", [P, CF_W], F32, kind="ExternalInput").ap()
    outT = nc.dram_tensor("outT", [D, SEQ], F32, kind="ExternalOutput").ap()
    spill_kind = "ExternalOutput" if debug else "Internal"
    y1n_sp = nc.dram_tensor("y1n_sp", [D, SEQ], F32, kind=spill_kind).ap()
    dbg = {}
    if debug:
        for name in ["d_cx", "d_ck", "d_y1", "d_den", "d_y1n0", "d_y10"]:
            dbg[name] = nc.dram_tensor(name, [P, 2048], F32, kind="ExternalOutput").ap()
        for name in ["d_w", "d_af", "d_ltx", "d_ltk"]:
            dbg[name] = nc.dram_tensor(name, [P, 2048], BF16, kind="ExternalOutput").ap()

    xT_r = xT.rearrange("(jb p) t -> p jb t", p=P)
    sp_r = y1n_sp.rearrange("(jb p) t -> p jb t", p=P)
    out_r = outT.rearrange("(jb p) t -> p jb t", p=P)

    s_x = Sem(nc, "s_x")
    s_w = Sem(nc, "s_w")
    s_w2 = Sem(nc, "s_w2")
    s_c = Sem(nc, "s_c")
    s_sp = Sem(nc, "s_sp")
    s_spi = Sem(nc, "s_spi")
    s_out = Sem(nc, "s_out")
    s_ext = Sem(nc, "s_ext")
    s_pe = Sem(nc, "s_pe")
    s_v = Sem(nc, "s_v")
    s_sc = Sem(nc, "s_sc")
    s_gp = Sem(nc, "s_gp")

    cbf = nc.alloc_sbuf_tensor("cbf_sb", [P, CCORE_W], BF16)
    cf = nc.alloc_sbuf_tensor("cf32_sb", [P, CF_W], F32)
    sm = nc.alloc_sbuf_tensor("sm", [16, 6 * T], F32)
    smb = nc.alloc_sbuf_tensor("smb", [16, 3 * T], BF16)


    ph1_psum = ExitStack()
    bank = [ph1_psum.enter_context(nc.psum_tensor(f"pb{i}", [P, 512], F32)) for i in range(7)]
    ps7 = ph1_psum.enter_context(nc.psum_tensor("pb7", [P, 1024], BF16))

    ph1 = ExitStack()

    def sb1(name, shape, dt):
        return ph1.enter_context(nc.sbuf_tensor(name, shape, dt))

    w_in_bf = sb1("w_in_bf", [P, 8 * 3328], BF16)
    w_out_bf = sb1("w_out_bf", [P, 8 * 1024], BF16)
    x_sb = [sb1(f"x{i}", [P, NB * T], F32) for i in range(3)]
    h_bf = sb1("h_bf", [P, NB * T], BF16)
    g_sb = [sb1(f"g_sb{i}", [P, NB * T], F32) for i in range(2)]
    cx = [sb1(f"cx{i}", [P, NB * T], F32) for i in range(2)]
    ck = [sb1(f"ck{i}", [P, NB * T], F32) for i in range(2)]
    k_sb = [sb1(f"k_sb{i}", [P, T], F32) for i in range(2)]
    q_bf = [sb1(f"q_bf{i}", [P, T], BF16) for i in range(2)]
    z_bf = [sb1(f"z_bf{i}", [P, NB * T], BF16) for i in range(2)]
    sc_bf = sb1("sc_bf", [P, NB * T], BF16)
    w_sb = sb1("w_sb", [P, NB * T], BF16)
    af_bf = sb1("af_bf", [P, NB * T], BF16)
    t_f32 = sb1("t_f32", [P, 2 * T], F32)
    y1_sb = sb1("y1_sb", [P, NB * T], F32)
    y1b_bf = h_bf
    y1n_st = sb1("y1n_st", [P, NB * T], F32)
    lhsT_k = sb1("lhsT_k", [P, 8 * 128], BF16)
    lhsT_x = sb1("lhsT_x", [P, 8 * 128], BF16)
    stg4 = sb1("stg4", [P, 32], BF16)
    stgx = sb1("stgx", [4, 8 * 128], BF16)
    cp1 = sb1("cp1_sb", [P, CP1_W], BF16)
    iwb = sb1("iwb", [16, T], BF16)
    rdb_ = sb1("rdb", [16, T], BF16)

    print('PH1 sbuf remaining:', nc.sbuf_bytes_remaining)

    sy.dma_start(out=cbf[:], in_=cbf_d).then_inc(s_c.h, 16)
    sy.dma_start(out=cp1[:], in_=cp1_d).then_inc(s_c.h, 16)
    sy.dma_start(out=cf[:], in_=cf32_d).then_inc(s_c.h, 16)
    s_c.v += 48
    win_r = w_in.rearrange("(kb p) m -> p kb m", p=P)
    gp.dma_start(out=w_in_bf[:].rearrange("p (kb m) -> p kb m", kb=8), in_=win_r).then_inc(s_w.h, 16)
    wout_r = w_out.rearrange("(kb p) m -> p kb m", p=P)
    gp.dma_start(out=w_out_bf[:].rearrange("p (kb m) -> p kb m", kb=8), in_=wout_r).then_inc(s_w.h, 16)
    s_w.v += 32
    s_w.wait(ve)
    s_c.wait(ve)
    for kb in range(NB):
        ins = ve.tensor_scalar(w_in_bf[:, kb * 3328:(kb + 1) * 3328],
                               w_in_bf[:, kb * 3328:(kb + 1) * 3328],
                               cf[:, K_RMS + kb:K_RMS + kb + 1], None, ALU.mult)
    s_v.inc(ins)
    s_v.inc(ve.memset(lhsT_k[:], 0.0))
    s_v.inc(ve.memset(lhsT_x[:], 0.0))
    x_loaded = [None, None, None]
    for i in range(min(3, n_tiles)):
        sy.dma_start(out=x_sb[i][:].rearrange("p (jb t) -> p jb t", jb=NB),
                     in_=xT_r[:, :, i * T:(i + 1) * T]).then_inc(s_x.h, 16)
        s_x.v += 16
        x_loaded[i] = s_x.v

    spill_busy = [None]
    wv_after_rms = s_v.v

    st1 = {}
    iter_marks = {}
    for it in range(n_tiles + 2):
        f = it
        b = it - 1
        tb1 = it - 2
        have_f = f < n_tiles
        have_b = 0 <= b < n_tiles
        have_tb1 = 0 <= tb1 < n_tiles
        if it - 2 in iter_marks:
            mpe, mv, msc, mgp = iter_marks[it - 2]
            for eng in (te, ve, sc, gp, sy):
                s_pe.wait(eng, mpe)
                s_v.wait(eng, mv)
                s_sc.wait(eng, msc)
                s_gp.wait(eng, mgp)

        if have_f:
            fpp = f % 2
            fxp = f % 3
            xa = x_sb[fxp]
            s_x.wait(sc, x_loaded[fxp])
            s_c.wait(sc)
            if b >= 1:
                s_pe.wait(sc, st1[b - 1]["vst1"])
            vsq = s_sc.inc(sc.activation(sc_bf[:], xa[:], AF.Square))
            s_sc.wait(te, vsq)
            s_c.wait(te)
            if b >= 1:
                s_v.wait(te, st1[b - 1]["y1v"])
            for kb in range(NB):
                ins = te.matmul(bank[2][0:1, 0:T], cbf[:, CC_ONES1:CC_ONES1 + 1],
                                sc_bf[:, kb * T:(kb + 1) * T], start=(kb == 0), stop=(kb == NB - 1))
            vsumsq = s_pe.inc(ins)
            s_pe.wait(sc, vsumsq)
            sc.activation(sm[0:1, 0:T], bank[2][0:1, 0:T], AF.Ln,
                          bias=cf[0:1, K_E6:K_E6 + 1], scale=1.0 / D)
            vr = s_sc.inc(sc.activation(smb[0:1, 0:T], sm[0:1, 0:T], AF.Exp, scale=-0.5))
            s_sc.wait(te, vr)
            vrbc = s_pe.inc(te.matmul(bank[3][:, 0:T], cbf[0:1, CC_ONESR:CC_ONESR + P],
                                      smb[0:1, 0:T], start=True, stop=True))
            s_pe.wait(ve, vrbc)
            s_x.wait(ve, x_loaded[fxp])
            if b >= 1:
                s_pe.wait(ve, st1[b - 1]["vst1"])
            if it == 0:
                s_v.wait(ve, wv_after_rms)
            for kb in range(NB):
                ins = ve.tensor_tensor(h_bf[:, kb * T:(kb + 1) * T], xa[:, kb * T:(kb + 1) * T],
                                       bank[3][:, 0:T], ALU.mult)
            vh = s_v.inc(ins)
            st1[f] = {"vh": vh, "vsumsq": vsumsq}

        if have_b:
            bpp = b % 2
            bxp = b % 3
            sb_prev = st1[b]

        def back_vstep(stepname):
            if not have_b:
                return
            bs = st1[b]
            if stepname == "vx":
                for sem, val in bs["g_marks"]:
                    sem.wait(ve, val)
                for jb in range(NB):
                    ins = ve.scalar_tensor_tensor(cx[bpp][:, jb * T:(jb + 1) * T],
                                                  g_sb[bpp][:, jb * T:(jb + 1) * T], 1.0,
                                                  cx[bpp][:, jb * T:(jb + 1) * T],
                                                  ALU.subtract, ALU.mult)
                bs["vvx"] = s_v.inc(ins)
            elif stepname == "vk":
                for jb in range(NB):
                    ins = ve.scalar_tensor_tensor(ck[bpp][:, jb * T:(jb + 1) * T],
                                                  g_sb[bpp][:, jb * T:(jb + 1) * T], 1.0,
                                                  k_sb[bpp][:], ALU.subtract, ALU.mult)
                bs["vvk"] = s_v.inc(ins)
            elif stepname == "ext":
                if b >= 1 and "vtr" in st1[b - 1]:
                    s_pe.wait(ve, st1[b - 1]["vtr"])
                C0b = b * CPT
                for jb in range(NB):
                    ve.tensor_scalar(lhsT_k[0:64, jb * 128 + C0b:jb * 128 + C0b + CPT],
                                     ck[bpp][0:64, jb * T:(jb + 1) * T:CH], -0.125, None, ALU.mult)
                    ve.tensor_scalar(lhsT_k[64:128, jb * 128 + 64 + C0b:jb * 128 + 64 + C0b + CPT],
                                     ck[bpp][64:128, jb * T:(jb + 1) * T:CH], -0.125, None, ALU.mult)
                    ins = ve.tensor_scalar(stg4[:, jb * CPT:(jb + 1) * CPT],
                                           cx[bpp][:, jb * T:(jb + 1) * T:CH], -1.0, None, ALU.mult)
                bs["vex"] = s_v.inc(ins)
                s_v.wait(gp, bs["vex"])
                bs["vgm"] = s_gp.inc(gp.memset(g_sb[bpp][:, 0:NB * T:CH], 0.0))
            elif stepname == "scan":
                s_gp.wait(ve, bs["vgm"])
                ve.tensor_tensor_scan(cx[bpp][:], g_sb[bpp][:], cx[bpp][:], 0.0, ALU.mult, ALU.add)
                ins = ve.tensor_tensor_scan(ck[bpp][:], g_sb[bpp][:], ck[bpp][:], 0.0,
                                            ALU.mult, ALU.add)
                bs["vscan"] = s_v.inc(ins)
            elif stepname == "pfull":
                if have_f:
                    s_pe.wait(ve, st1[f]["vsumsq"])
                elif b >= 1:
                    s_pe.wait(ve, st1[b - 1]["vst1"])
                for sem, val in bs["qk_marks"]:
                    sem.wait(ve, val)
                for jb in range(NB):
                    ins = ve.tensor_tensor(sc_bf[:, jb * T:(jb + 1) * T], q_bf[bpp][:],
                                           ck[bpp][:, jb * T:(jb + 1) * T], ALU.mult)
                bs["vP"] = s_v.inc(ins)

        def tail_part(part):
            if not have_tb1:
                return
            ts_ = st1[tb1]
            if part == 0:
                s_pe.wait(ve, ts_["vst1"])
                ve.tensor_scalar(sm[0:1, 2 * T:3 * T], bank[6][0:1, 0:T], 1.0 / D, None, ALU.mult)
                ve.tensor_scalar(sm[0:1, 3 * T:4 * T], bank[6][0:1, T:2 * T], 1.0 / D,
                                 None, ALU.mult)
                ve.tensor_tensor(sm[0:1, 5 * T:6 * T], sm[0:1, 2 * T:3 * T],
                                 sm[0:1, 2 * T:3 * T], ALU.mult)
                vvar = s_v.inc(ve.tensor_tensor(sm[0:1, 3 * T:4 * T], sm[0:1, 3 * T:4 * T],
                                                sm[0:1, 5 * T:6 * T], ALU.subtract))
                ts_["vvar"] = vvar
            elif part == 1:
                s_v.wait(sc, ts_["vvar"])
                sc.activation(sm[0:1, 4 * T:5 * T], sm[0:1, 3 * T:4 * T], AF.Ln,
                              bias=cf[0:1, K_E5:K_E5 + 1])
                vr1 = s_sc.inc(sc.activation(sm[0:1, T:2 * T], sm[0:1, 4 * T:5 * T],
                                             AF.Exp, scale=-0.5))
                s_sc.wait(ve, vr1)
                ve.tensor_copy(smb[0:1, T:2 * T], sm[0:1, 2 * T:3 * T])
                ts_["vmb"] = s_v.inc(ve.tensor_copy(smb[0:1, 2 * T:3 * T], sm[0:1, T:2 * T]))
            elif part == 2:
                s_v.wait(te, ts_["vmb"])
                te.matmul(bank[0][:, 0:T], cbf[0:1, CC_ONESR:CC_ONESR + P], smb[0:1, T:2 * T],
                          start=True, stop=True)
                ts_["vbc"] = s_pe.inc(te.matmul(bank[1][:, 0:T], cbf[0:1, CC_ONESR:CC_ONESR + P],
                                                smb[0:1, 2 * T:3 * T], start=True, stop=True))
            else:
                s_pe.wait(ve, ts_["vbc"])
                nonlocal_spill = spill_busy[0]
                if nonlocal_spill is not None:
                    s_sp.wait(ve, nonlocal_spill)
                for jb in range(NB):
                    ve.tensor_tensor(y1n_st[:, jb * T:(jb + 1) * T],
                                     y1_sb[:, jb * T:(jb + 1) * T], bank[0][:, 0:T], ALU.subtract)
                for jb in range(NB):
                    ins2 = ve.tensor_tensor(y1n_st[:, jb * T:(jb + 1) * T],
                                            y1n_st[:, jb * T:(jb + 1) * T], bank[1][:, 0:T],
                                            ALU.mult)
                vln = s_v.inc(ins2)
                ts_["vln"] = vln
                s_v.wait(sy, vln)
                if debug and tb1 == 0:
                    sy.dma_start(out=dbg["d_y1n0"], in_=y1n_st[:]).then_inc(s_out.h, 16)
                    sy.dma_start(out=dbg["d_y10"], in_=y1_sb[:]).then_inc(s_out.h, 16)
                    s_out.v += 32
                sy.dma_start(out=sp_r[:, :, tb1 * T:(tb1 + 1) * T],
                             in_=y1n_st[:].rearrange("p (jb t) -> p jb t", jb=NB)
                             ).then_inc(s_sp.h, 16)
                s_sp.v += 16
                spill_busy[0] = s_sp.v

        if have_f:
            ms_order = (list(range(16, 24)) + list(range(0, 8)) + list(range(8, 16))
                        + [24, 25])
            groups = [ms_order[gg:gg + 4] for gg in range(0, PROJ_M, 4)]
            evac_done = {}
            g_marks = []
            qk_marks = []
            s_v.wait(te, st1[f]["vh"])
            if it == 0:
                s_w.wait(te)
                s_v.wait(te, wv_after_rms)
            back_steps = {}
            back_vstep("vx")
            back_vstep("vk")
            back_vstep("ext")
            back_steps = {1: "scan", 2: "pfull"}

            for gi, grp in enumerate(groups):
                pb = [bank[2], bank[3]] if gi % 2 == 0 else [bank[4], bank[5]]
                if gi >= 2:
                    for sem, val in evac_done[gi - 2]:
                        sem.wait(te, val)
                for mi, m in enumerate(grp):
                    pt = pb[mi // 2][:, (mi % 2) * T:(mi % 2) * T + T]
                    for kb in range(NB):
                        ins = te.matmul(pt, w_in_bf[:, kb * 3328 + m * 128:kb * 3328 + (m + 1) * 128],
                                        h_bf[:, kb * T:(kb + 1) * T],
                                        start=(kb == 0 and mi in (0, 2)), stop=(kb == NB - 1))
                vpg = s_pe.inc(ins)
                s_pe.wait(sc, vpg)
                s_pe.wait(ve, vpg)
                last_sc = None
                last_v = None
                for mi, m in enumerate(grp):
                    pt = pb[mi // 2][:, (mi % 2) * T:(mi % 2) * T + T]
                    if m < 8:
                        last_sc = sc.activation(z_bf[fpp][:, m * T:(m + 1) * T], pt, AF.Sigmoid)
                    elif m < 16:
                        last_v = ve.tensor_copy(cx[fpp][:, (m - 8) * T:(m - 7) * T], pt)
                    elif m < 24:
                        last_sc = sc.activation(g_sb[fpp][:, (m - 16) * T:(m - 15) * T], pt,
                                                AF.Sigmoid,
                                                bias=cf[:, K_GB + (m - 16):K_GB + (m - 16) + 1])
                    elif m == 24:
                        last_sc = sc.activation(q_bf[fpp][:], pt, AF.Copy)
                    else:
                        vq = s_sc.inc(last_sc)
                        last_sc = None
                        s_sc.wait(ve, vq)
                        qk_marks.append((s_sc, vq))
                        last_v = ve.tensor_copy(k_sb[fpp][:], pt)
                marks = []
                if last_sc is not None:
                    marks.append((s_sc, s_sc.inc(last_sc)))
                if last_v is not None:
                    marks.append((s_v, s_v.inc(last_v)))
                evac_done[gi] = marks
                if gi in (0, 1):
                    g_marks.extend(marks)
                if gi == 6:
                    qk_marks.extend(marks)
                if gi in back_steps:
                    back_vstep(back_steps[gi])
                if gi == 3:
                    tail_part(0)
                if gi == 4:
                    tail_part(1)
                elif gi == 5:
                    tail_part(2)
                elif gi == 6:
                    tail_part(3)
            st1[f]["g_marks"] = g_marks
            st1[f]["qk_marks"] = qk_marks
            st1[f]["evac56"] = evac_done[5] + evac_done[6]
            st1[f]["proj_done"] = s_pe.v
        else:
            for stepname in ("vx", "vk", "ext", "scan", "pfull"):
                back_vstep(stepname)
            for part_ in (0, 1, 2, 3):
                tail_part(part_)

        if have_b:
            bs = st1[b]
            C0 = b * CPT
            t0b = b * T
            s_v.wait(te, bs["vex"])
            for jb in range(NB):
                ins = te.transpose(ps7[0:CPT, jb * 128:(jb + 1) * 128],
                                   stg4[:, jb * CPT:(jb + 1) * CPT], cbf[:, CC_ID:CC_ID + P])
            vtr = s_pe.inc(ins)
            bs["vtr"] = vtr
            s_pe.wait(ve, vtr)
            s_ext.wait(ve)
            vstg = s_v.inc(ve.tensor_copy(stgx[0:CPT, :], ps7[0:CPT, 0:8 * 128]))
            s_v.wait(sy, vstg)
            stgx_r = stgx[:].rearrange("c (jb ff) -> c jb ff", jb=NB)
            ltx_r = lhsT_x[:].rearrange("p (jb ff) -> p jb ff", jb=NB)
            sy.dma_start(out=ltx_r[C0:C0 + CPT, :, 0:64],
                         in_=stgx_r[0:CPT, :, 0:64]).then_inc(s_ext.h, 16)
            sy.dma_start(out=ltx_r[64 + C0:64 + C0 + CPT, :, 64:128],
                         in_=stgx_r[0:CPT, :, 64:128]).then_inc(s_ext.h, 16)
            s_ext.v += 32
            s_v.wait(te, bs["vP"])
            if b >= 1:
                s_v.wait(te, st1[b - 1]["vvar"])
            for j in range(NB):
                ins = te.matmul(bank[6][0:16, 0:T], cp1[:, P1_LSE + j * 16:P1_LSE + (j + 1) * 16],
                                sc_bf[:, j * T:(j + 1) * T], start=(j == 0), stop=(j == NB - 1))
            vlse = s_pe.inc(ins)
            s_pe.wait(sc, vlse)
            viw = s_sc.inc(sc.activation(sm[0:16, 4 * T:5 * T], bank[6][0:16, 0:T], AF.Exp,
                                         scale=0.125))
            s_ext.wait(te)
            for sem, val in bs["evac56"]:
                sem.wait(te, val)
            if have_f:
                for sem, val in st1[f]["evac56"]:
                    sem.wait(te, val)
            exp_done = {}

            def emit_wd(j):
                if j == 0:
                    s_sc.wait(te, viw)
                s_sc.wait(te, exp_done[j])
                ins = te.matmul(bank[6][0:16, T:2 * T], cp1[:, P1_WD + j * 16:P1_WD + (j + 1) * 16],
                                w_sb[:, j * T:(j + 1) * T], start=False, stop=(j == NB - 1))
                s_pe.inc(ins)

            for j in range(NB):
                sb_ = bank[[2, 3, 0, 1][j % 4]]
                if j in (2, 3) and b >= 1 and "vln" in st1[b - 1]:
                    s_v.wait(te, st1[b - 1]["vln"])
                if j >= 4:
                    s_sc.wait(te, exp_done[j - 4])
                te.matmul(sb_[:, 0:T], lhsT_k[:, j * 128:(j + 1) * 128], q_bf[bpp][:],
                          start=True, stop=False)
                ins = te.matmul(sb_[:, 0:T], cp1[0:4, P1_MASK + b * 128:P1_MASK + (b + 1) * 128],
                                cp1[0:4, P1_BLK:P1_BLK + T], start=False, stop=True)
                vsc_ = s_pe.inc(ins)
                s_pe.wait(sc, vsc_)
                exp_done[j] = s_sc.inc(sc.activation(w_sb[:, j * T:(j + 1) * T], sb_[:, 0:T], AF.Exp))
                if j >= 3:
                    emit_wd(j - 3)
            for j in range(NB - 3, NB):
                emit_wd(j)
            vwd = s_pe.v
            s_pe.wait(ve, vwd)
            s_sc.wait(ve, viw)
            ve.tensor_tensor(sm[0:16, 5 * T:6 * T], bank[6][0:16, T:2 * T],
                             sm[0:16, 4 * T:5 * T], ALU.add)
            ve.reciprocal(sm[0:16, 3 * T:4 * T], sm[0:16, 5 * T:6 * T])
            ve.tensor_copy(rdb_[0:16, :], sm[0:16, 3 * T:4 * T])
            vden = s_v.inc(ve.tensor_copy(iwb[0:16, :], sm[0:16, 4 * T:5 * T]))
            s_v.wait(te, vden)
            pair_v = {}
            for jj in range(4):
                ob = bank[2 + (jj % 2)]
                ib = bank[4 + (jj % 2)]
                rb = bank[jj % 2]
                if jj >= 2:
                    s_v.wait(te, pair_v[jj - 2])
                for h2_ in (0, 1):
                    j = 2 * jj + h2_
                    sel = cp1[0:16, P1_BCH + j * 128:P1_BCH + (j + 1) * 128]
                    te.matmul(ob[:, h2_ * T:h2_ * T + T], lhsT_x[:, j * 128:(j + 1) * 128],
                              w_sb[:, j * T:(j + 1) * T], start=(h2_ == 0), stop=True)
                    te.matmul(ib[:, h2_ * T:h2_ * T + T], sel, iwb[0:16, 0:T],
                              start=(h2_ == 0), stop=True)
                    ins = te.matmul(rb[:, h2_ * T:h2_ * T + T], sel, rdb_[0:16, 0:T],
                                    start=(h2_ == 0), stop=True)
                vj = s_pe.inc(ins)
                s_pe.wait(ve, vj)
                cols = slice(2 * jj * T, (2 * jj + 2) * T)
                ve.tensor_tensor(t_f32[:], ib[:, 0:2 * T], cx[bpp][:, cols], ALU.mult)
                ve.tensor_tensor(t_f32[:], ob[:, 0:2 * T], t_f32[:], ALU.subtract)
                ve.tensor_tensor(t_f32[:], t_f32[:], rb[:, 0:2 * T], ALU.mult)
                ins = ve.tensor_tensor(af_bf[:, cols], t_f32[:], z_bf[bpp][:, cols], ALU.mult)
                pair_v[jj] = s_v.inc(ins)
            vaf = s_v.v
            s_v.wait(te, vaf)
            if it == 1:
                s_w.wait(te)
            y1v = None
            vsq1 = None
            vyb = None
            for gi2 in range(2):
                pb2 = [bank[2], bank[3]] if gi2 == 0 else [bank[4], bank[5]]
                for mi in range(4):
                    m = gi2 * 4 + mi
                    pt = pb2[mi // 2][:, (mi % 2) * T:(mi % 2) * T + T]
                    for kb in range(NB):
                        ins = te.matmul(pt, w_out_bf[:, kb * 1024 + m * 128:kb * 1024 + (m + 1) * 128],
                                        af_bf[:, kb * T:(kb + 1) * T],
                                        start=(kb == 0 and mi in (0, 2)), stop=(kb == NB - 1))
                vwo = s_pe.inc(ins)
                s_pe.wait(ve, vwo)
                for mi in range(4):
                    m = gi2 * 4 + mi
                    pt = pb2[mi // 2][:, (mi % 2) * T:(mi % 2) * T + T]
                    ins = ve.tensor_tensor(y1_sb[:, m * T:(m + 1) * T], pt,
                                           x_sb[bxp][:, m * T:(m + 1) * T], ALU.add)
                y1v = s_v.inc(ins)
                half = slice(gi2 * 4 * T, (gi2 + 1) * 4 * T)
                s_v.wait(sc, y1v)
                if gi2 == 0:
                    s_pe.wait(sc, vlse)
                vsq1 = s_sc.inc(sc.activation(sc_bf[:, half], y1_sb[:, half], AF.Square))
                if gi2 == 0 and have_f:
                    s_pe.wait(sc, st1[f]["proj_done"])
                vyb = s_sc.inc(sc.activation(y1b_bf[:, half], y1_sb[:, half], AF.Copy))
            bs["y1v"] = y1v
            if b + 3 < n_tiles:
                s_v.wait(sy, y1v)
                sy.dma_start(out=x_sb[b % 3][:].rearrange("p (jb t) -> p jb t", jb=NB),
                             in_=xT_r[:, :, (b + 3) * T:(b + 4) * T]).then_inc(s_x.h, 16)
                s_x.v += 16
                x_loaded[b % 3] = s_x.v
            s_sc.wait(te, vsq1)
            s_sc.wait(te, vyb)
            s_v.wait(te, vden)
            for kb in range(NB):
                te.matmul(bank[6][0:1, 0:T], cbf[:, CC_ONES1:CC_ONES1 + 1],
                          y1b_bf[:, kb * T:(kb + 1) * T], start=(kb == 0), stop=False)
            for kb in range(NB):
                ins = te.matmul(bank[6][0:1, T:2 * T], cbf[:, CC_ONES1:CC_ONES1 + 1],
                                sc_bf[:, kb * T:(kb + 1) * T], start=False, stop=(kb == NB - 1))
            vst1 = s_pe.inc(ins)
            bs["vst1"] = vst1
            bs["pair_last"] = pair_v[3]

        iter_marks[it] = (s_pe.v, s_v.v, s_sc.v, s_gp.v)

    if not phase2:
        s_sp.wait(sy)
        s_out.wait(sy)
        return nc

    pe_fin, v_fin, sc_fin, gp_fin = s_pe.v, s_v.v, s_sc.v, s_gp.v
    for eng in (gp, sy):
        s_pe.wait(eng, pe_fin)
        s_v.wait(eng, v_fin)
        s_sc.wait(eng, sc_fin)
        s_sp.wait(eng)
        s_ext.wait(eng)
        s_x.wait(eng)
    ph1.close()
    ph1_psum.close()

    ph2 = ExitStack()

    def sb2(name, shape, dt):
        return ph2.enter_context(nc.sbuf_tensor(name, shape, dt))

    w_fc_bf = sb2("w_fc_bf", [P, 8 * FF], BF16)
    w_pj_bf = sb2("w_pj_bf", [P, NFB * 1024], BF16)
    yn_sb = [sb2(f"yn{i}", [P, NB * T], F32) for i in range(2)]
    yn_bf = sb2("yn_bf", [P, NB * T], BF16)
    h2_bf = sb2("h2_bf", [P, 6 * T], BF16)
    yc_sb = sb2("yc_sb", [P, NB * T], F32)
    mem_sb = sb2("mem_sb", [P, NB * T], F32)
    acc_bf = sb2("acc_bf", [P, NB * T], BF16)
    sig_bf = sb2("sig_bf", [P, NB * T], BF16)
    relu_f = sb2("relu_f", [P, T], F32)
    o_st = [sb2(f"o_st{i}", [P, NB * T], F32) for i in range(2)]
    bank2 = [ph2.enter_context(nc.psum_tensor(f"qb{i}", [P, 512], F32)) for i in range(8)]
    print('PH2 sbuf remaining:', nc.sbuf_bytes_remaining)

    wfc_r = w_fcT.rearrange("(kb p) f -> p kb f", p=P)
    wfc_sb_r = w_fc_bf[:].rearrange("p (kb f) -> p kb f", kb=8)
    fc_chunk_v = []
    for c in range(4):
        gp.dma_start(out=wfc_sb_r[:, :, c * 1024:(c + 1) * 1024],
                     in_=wfc_r[:, :, c * 1024:(c + 1) * 1024]).then_inc(s_w2.h, 16)
        s_w2.v += 16
        fc_chunk_v.append(s_w2.v)
    wpj_r = w_projT.rearrange("(fb p) d2 -> p fb d2", p=P)
    wpj_sb_r = w_pj_bf[:].rearrange("p (fb d2) -> p fb d2", fb=NFB)
    pj_chunk_v = []
    for c in range(4):
        gp.dma_start(out=wpj_sb_r[:, c * 8:(c + 1) * 8, :],
                     in_=wpj_r[:, c * 8:(c + 1) * 8, :]).then_inc(s_w2.h, 16)
        s_w2.v += 16
        pj_chunk_v.append(s_w2.v)

    s_sp.wait(sy)
    sy.dma_start(out=yn_sb[0][:].rearrange("p (jb t) -> p jb t", jb=NB),
                 in_=sp_r[:, :, 0:T]).then_inc(s_spi.h, 16)
    s_spi.v += 16
    yn_loaded = [s_spi.v, None]
    if n_tiles > 1:
        sy.dma_start(out=yn_sb[1][:].rearrange("p (jb t) -> p jb t", jb=NB),
                     in_=sp_r[:, :, T:2 * T]).then_inc(s_spi.h, 16)
        s_spi.v += 16
        yn_loaded[1] = s_spi.v
    yn_last_read = [None, None]
    out_busy = [None, None]

    st2 = {}
    for J in range(n_tiles + 2):
        f = J
        lb = J - 1
        tb = J - 2
        have_f = f < n_tiles
        have_lb = 0 <= lb < n_tiles
        have_tb = 0 <= tb < n_tiles

        if have_f:
            pp = f % 2
            if f >= 1 and "vynb_next" in st2.get(f - 1, {}):
                vynb = st2[f - 1]["vynb_next"]
            else:
                s_spi.wait(ve, yn_loaded[pp])
                if f >= 1:
                    s_pe.wait(ve, st2[f - 1]["vfc_last"])
                vynb = s_v.inc(ve.tensor_copy(yn_bf[:], yn_sb[pp][:]))

        if have_lb:
            ls = st2[lb]
            lpp = lb % 2
            s_pe.wait(ve, ls["vpj"])
            for mb in range(NB):
                pt = bank2[4 + mb // 2][:, (mb % 2) * T:(mb % 2) * T + T]
                ins = ve.tensor_scalar(yc_sb[:, mb * T:(mb + 1) * T], pt, -0.5, None, ALU.add)
            vyc = s_v.inc(ins)
            s_v.wait(sc, vyc)
            if lb >= 1 and "vst2" in st2.get(lb - 1, {}):
                s_pe.wait(sc, st2[lb - 1]["vst2"])
            vs1 = s_sc.inc(sc.activation(sig_bf[:], yc_sb[:], AF.Sign, bias=cf[:, K_NH:K_NH + 1]))
            s_sc.wait(ve, vs1)
            vacc = s_v.inc(ve.tensor_copy(acc_bf[:], sig_bf[:]))
            ve.tensor_scalar(mem_sb[:], yc_sb[:], 1.9, 0.45, ALU.mult, ALU.add)
            vm = s_v.inc(ve.scalar_tensor_tensor(mem_sb[:], sig_bf[:], -0.5, mem_sb[:],
                                                 ALU.mult, ALU.add))
            lstate = {"vm": vm, "vacc": vacc, "step": 2}
        else:
            lstate = None

        def ladder_step():
            if lstate is None or lstate["step"] > 5:
                return
            k = lstate["step"]
            s_v.wait(sc, lstate["vm"])
            vsg = s_sc.inc(sc.activation(sig_bf[:], mem_sb[:], AF.Sign,
                                         bias=cf[:, K_N1:K_N1 + 1]))
            s_sc.wait(ve, vsg)
            lstate["vacc"] = s_v.inc(ve.tensor_tensor(acc_bf[:], acc_bf[:], sig_bf[:], ALU.add))
            if k < 5:
                s_sc.wait(ve, vsg)
                ve.scalar_tensor_tensor(mem_sb[:], mem_sb[:], 0.9, yc_sb[:], ALU.mult, ALU.add)
                lstate["vm"] = s_v.inc(ve.scalar_tensor_tensor(mem_sb[:], sig_bf[:], -0.5,
                                                               mem_sb[:], ALU.mult, ALU.add))
            lstate["step"] = k + 1

        def tail_part(part):
            if not have_tb:
                return
            ts_ = st2[tb]
            tpp = tb % 2
            if part == 1:
                s_pe.wait(ve, ts_["vst2"])
                ve.tensor_scalar(sm[0:1, 2 * T:3 * T], bank2[3][0:1, 0:T], 1.0 / D, None, ALU.mult)
                ve.tensor_scalar(sm[0:1, 3 * T:4 * T], bank2[3][0:1, T:2 * T], 1.0 / D,
                                 None, ALU.mult)
                ve.tensor_tensor(sm[0:1, 0:T], sm[0:1, 2 * T:3 * T], sm[0:1, 2 * T:3 * T], ALU.mult)
                vvar2 = s_v.inc(ve.tensor_tensor(sm[0:1, 3 * T:4 * T], sm[0:1, 3 * T:4 * T],
                                                 sm[0:1, 0:T], ALU.subtract))
                s_v.wait(sc, vvar2)
                sc.activation(sm[0:1, 0:T], sm[0:1, 3 * T:4 * T], AF.Ln,
                              bias=cf[0:1, K_E5:K_E5 + 1])
                vr2 = s_sc.inc(sc.activation(sm[0:1, T:2 * T], sm[0:1, 0:T], AF.Exp, scale=-0.5))
                s_sc.wait(ve, vr2)
                ve.tensor_copy(smb[0:1, T:2 * T], sm[0:1, 2 * T:3 * T])
                ts_["vmb2"] = s_v.inc(ve.tensor_copy(smb[0:1, 2 * T:3 * T], sm[0:1, T:2 * T]))
            elif part == 2:
                s_v.wait(te, ts_["vmb2"])
                te.matmul(bank2[3][:, 0:T], cbf[0:1, CC_ONESR:CC_ONESR + P], smb[0:1, T:2 * T],
                          start=True, stop=True)
                ts_["vbc2"] = s_pe.inc(te.matmul(bank2[3][:, T:2 * T],
                                                 cbf[0:1, CC_ONESR:CC_ONESR + P],
                                                 smb[0:1, 2 * T:3 * T], start=True, stop=True))
            else:
                s_pe.wait(ve, ts_["vbc2"])
                for jb in range(NB):
                    ve.tensor_tensor(o_st[tpp][:, jb * T:(jb + 1) * T],
                                     o_st[tpp][:, jb * T:(jb + 1) * T], bank2[3][:, 0:T],
                                     ALU.subtract)
                for jb in range(NB):
                    ins = ve.tensor_tensor(o_st[tpp][:, jb * T:(jb + 1) * T],
                                           o_st[tpp][:, jb * T:(jb + 1) * T],
                                           bank2[3][:, T:2 * T], ALU.mult)
                vo = s_v.inc(ins)
                ts_["vo"] = vo
                s_v.wait(sy, vo)
                sy.dma_start(out=out_r[:, :, tb * T:(tb + 1) * T],
                             in_=o_st[tpp][:].rearrange("p (jb t) -> p jb t", jb=NB)
                             ).then_inc(s_out.h, 16)
                s_out.v += 16
                out_busy[tpp] = s_out.v

        if have_f:
            s_v.wait(te, vynb)
            if have_lb:
                s_v.wait(te, vyc)
            fc_evac = {}
            for g_ in range(16):
                bk = bank2[g_ % 3]
                if g_ >= 3:
                    s_sc.wait(te, fc_evac[g_ - 3])
                s_w2.wait(te, fc_chunk_v[(2 * g_) // 8])
                for fi in (0, 1):
                    Fb = 2 * g_ + fi
                    pt = bk[:, fi * T:fi * T + T]
                    for kb in range(NB):
                        ins = te.matmul(pt, w_fc_bf[:, kb * FF + Fb * 128:kb * FF + (Fb + 1) * 128],
                                        yn_bf[:, kb * T:(kb + 1) * T],
                                        start=(kb == 0 and fi == 0), stop=(kb == NB - 1))
                vfc = s_pe.inc(ins)
                s_pe.wait(sc, vfc)
                for fi in (0, 1):
                    pt = bk[:, fi * T:fi * T + T]
                    slot = (g_ % 3) * 2 + fi
                    sc.activation(relu_f[:], pt, AF.Relu)
                    ins = sc.activation(h2_bf[:, slot * T:(slot + 1) * T], relu_f[:], AF.Square)
                fc_evac[g_] = s_sc.inc(ins)
                if g_ == 1:
                    tail_part(1)
                elif g_ == 3:
                    ladder_step()
                elif g_ == 5:
                    ladder_step()
                    tail_part(2)
                elif g_ == 7:
                    ladder_step()
                elif g_ == 9:
                    ladder_step()
                    tail_part(3)
                for gp_ in ([g_ - 1] if g_ >= 1 else []) + ([15] if g_ == 15 else []):
                    s_sc.wait(te, fc_evac[gp_])
                    s_w2.wait(te, pj_chunk_v[(2 * gp_ + 1) // 8])
                    for fi in (0, 1):
                        Fb = 2 * gp_ + fi
                        slot = (gp_ % 3) * 2 + fi
                        for mb in range(NB):
                            pt = bank2[4 + mb // 2][:, (mb % 2) * T:(mb % 2) * T + T]
                            ins = te.matmul(pt,
                                            w_pj_bf[:, Fb * 1024 + mb * 128:Fb * 1024 + (mb + 1) * 128],
                                            h2_bf[:, slot * T:(slot + 1) * T],
                                            start=(Fb == 0 and mb in (0, 2, 4, 6)),
                                            stop=(Fb == NFB - 1))
            vpj = s_pe.inc(ins)
            st2[f] = {"vpj": vpj, "vfc_last": vfc}
        else:
            while lstate is not None and lstate["step"] <= 5:
                ladder_step()
            for part_ in (1, 2, 3):
                tail_part(part_)

        if have_lb:
            while lstate["step"] <= 5:
                ladder_step()
            vacc = lstate["vacc"]
            if out_busy[lpp] is not None:
                s_out.wait(ve, out_busy[lpp])
            vy2 = s_v.inc(ve.scalar_tensor_tensor(o_st[lpp][:], acc_bf[:], 0.1, yn_sb[lpp][:],
                                                  ALU.mult, ALU.add))
            yn_last_read[lpp] = vy2
            if lb + 2 < n_tiles:
                s_v.wait(sy, vy2)
                sy.dma_start(out=yn_sb[lpp][:].rearrange("p (jb t) -> p jb t", jb=NB),
                             in_=sp_r[:, :, (lb + 2) * T:(lb + 3) * T]).then_inc(s_spi.h, 16)
                s_spi.v += 16
                yn_loaded[lpp] = s_spi.v
            if f + 1 < n_tiles and f < n_tiles:
                s_spi.wait(gp, yn_loaded[(f + 1) % 2])
                s_pe.wait(gp, st2[f]["vfc_last"])
                st2[f]["vynb_next"] = s_gp.inc(gp.tensor_copy(yn_bf[:], yn_sb[(f + 1) % 2][:]))
            s_v.wait(sc, vy2)
            vsq2 = s_sc.inc(sc.activation(sig_bf[:], o_st[lpp][:], AF.Square))
            vy2b = s_v.inc(ve.tensor_copy(acc_bf[:], o_st[lpp][:]))
            s_sc.wait(te, vsq2)
            s_v.wait(te, vy2b)
            if have_tb and "vo" in st2[tb]:
                s_v.wait(te, st2[tb]["vo"])
            for kb in range(NB):
                te.matmul(bank2[3][0:1, 0:T], cbf[:, CC_ONES1:CC_ONES1 + 1],
                          acc_bf[:, kb * T:(kb + 1) * T], start=(kb == 0), stop=False)
            for kb in range(NB):
                ins = te.matmul(bank2[3][0:1, T:2 * T], cbf[:, CC_ONES1:CC_ONES1 + 1],
                                sig_bf[:, kb * T:(kb + 1) * T], start=False, stop=(kb == NB - 1))
            st2[lb]["vst2"] = s_pe.inc(ins)

    s_out.wait(sy)
    s_sp.wait(sy)
    ph2.close()
    return nc


def make_consts(rms_w, gate_bias):
    cbf = np.zeros((P, CCORE_W), np.float32)
    cbf[:, CC_ID:CC_ID + P] = np.eye(P)
    cbf[:, CC_ONES1] = 1.0
    cbf[0, CC_ONESR:CC_ONESR + P] = 1.0
    cp1 = np.zeros((P, CP1_W), np.float32)
    for j in range(NB):
        cp1[0:64, P1_LSE + j * 16 + 2 * j] = -1.0
        cp1[64:128, P1_LSE + j * 16 + 2 * j + 1] = -1.0
        cp1[0:64, P1_WD + j * 16 + 2 * j] = 1.0
        cp1[64:128, P1_WD + j * 16 + 2 * j + 1] = 1.0
    cc = np.arange(64)
    for i in range(NT):
        for jj in range(CPT):
            m = (cc >= i * CPT + jj).astype(np.float32) * NEGBIG
            cp1[jj, P1_MASK + i * 128:P1_MASK + i * 128 + 64] = m
            cp1[jj, P1_MASK + i * 128 + 64:P1_MASK + (i + 1) * 128] = m
    tt = np.arange(T)
    for jj in range(CPT):
        cp1[jj, P1_BLK:P1_BLK + T] = (tt // CH == jj).astype(np.float32)
    for j in range(NB):
        cp1[2 * j, P1_BCH + j * 128:P1_BCH + j * 128 + 64] = 1.0
        cp1[2 * j + 1, P1_BCH + j * 128 + 64:P1_BCH + (j + 1) * 128] = 1.0

    cf = np.zeros((P, CF_W), np.float32)
    cf[:, K_RMS:K_RMS + 8] = rms_w.reshape(8, P).T
    cf[:, K_GB:K_GB + 8] = gate_bias.reshape(8, P).T
    cf[:, K_E6] = 1e-6
    cf[:, K_E5] = 1e-5
    cf[:, K_N1] = -1.0
    cf[:, K_NH] = -0.5
    return cbf.astype(ml_dtypes.bfloat16), cp1.astype(ml_dtypes.bfloat16), cf


def kernel(L0_attn_hidden_states, L0_attn_rms_weight, L0_attn_W_in, L0_attn_gate_bias,
           L0_attn_W_out, L0_ffn_W_fc, L0_ffn_W_proj):
    hs = np.asarray(L0_attn_hidden_states, np.float32)
    xTs = np.ascontiguousarray(np.transpose(hs, (0, 2, 1)))
    w_in = np.ascontiguousarray(np.asarray(L0_attn_W_in, np.float32))
    w_out = np.ascontiguousarray(np.asarray(L0_attn_W_out, np.float32))
    w_fcT = np.ascontiguousarray(np.asarray(L0_ffn_W_fc, np.float32).T)
    w_projT = np.ascontiguousarray(np.asarray(L0_ffn_W_proj, np.float32).T)
    cbf, cp1, cf = make_consts(np.asarray(L0_attn_rms_weight, np.float32),
                               np.asarray(L0_attn_gate_bias, np.float32))
    nc = build()
    in_maps = [{"xT": xTs[b], "w_in": w_in, "w_out": w_out, "w_fcT": w_fcT,
                "w_projT": w_projT, "cbf": cbf, "cp1": cp1, "cf32": cf} for b in range(8)]
    res = run_bass_kernel_spmd(nc, in_maps, core_ids=list(range(8)))
    out = np.stack([np.asarray(r["outT"], np.float32).T for r in res.results])
    return np.ascontiguousarray(out)


# version 10
# speedup vs baseline: 1.0969x; 1.0003x over previous
import numpy as np
import ml_dtypes
from contextlib import ExitStack

import concourse.bass as bass
import concourse.mybir as mybir
from concourse.bass_utils import run_bass_kernel_spmd

F32 = mybir.dt.float32
BF16 = mybir.dt.bfloat16
AF = mybir.ActivationFunctionType
ALU = mybir.AluOpType

P = 128
T = 256
NT = 16
SEQ = 4096
D = 1024
NB = 8
CH = 64
CPT = T // CH
FF = 4096
NFB = 32
PROJ_M = 26
NEGBIG = -1.0e30

CC_ID = 0
CC_ONES1 = 128
CC_ONESR = 129
CCORE_W = 260
P1_LSE = 0
P1_WD = 128
P1_MASK = 256
P1_BLK = 2304
P1_BCH = 2560
CP1_W = 3584

K_RMS = 0
K_GB = 8
K_E6 = 16
K_E5 = 17
K_N1 = 18
K_NH = 19
CF_W = 24


_ACT_SET_CACHE = {}


def _act_set_id(nc, set_name):
    from concourse.hw_specs import get_activation_tables
    key = (nc.m.arch, set_name)
    if key not in _ACT_SET_CACHE:
        names = list(get_activation_tables(nc.m.arch).keys())
        _ACT_SET_CACHE[key] = names.index(set_name)
    return _ACT_SET_CACHE[key]


def preload_set(nc, sc, set_name):
    import concourse.mybir as mb
    return sc.add_instruction(mb.InstLoadActFuncSet(
        name=nc.get_next_instruction_name(),
        act_func_set_id=_act_set_id(nc, set_name), ins=[], outs=[]))


class Sem:
    def __init__(self, nc, name):
        self.h = nc.alloc_semaphore(name)
        self.v = 0

    def inc(self, instr, n=1):
        instr.then_inc(self.h, n)
        self.v += n
        return self.v

    def wait(self, eng, v=None):
        eng.wait_ge(self.h, self.v if v is None else v)


def build(debug=False, n_tiles=NT, phase2=True):
    nc = bass.Bass("TRN2", target_bir_lowering=False)
    te, ve, sc, gp, sy = nc.tensor, nc.vector, nc.scalar, nc.gpsimd, nc.sync

    xT = nc.dram_tensor("xT", [D, SEQ], F32, kind="ExternalInput").ap()
    w_in = nc.dram_tensor("w_in", [D, 3328], F32, kind="ExternalInput").ap()
    w_out = nc.dram_tensor("w_out", [D, D], F32, kind="ExternalInput").ap()
    w_fcT = nc.dram_tensor("w_fcT", [D, FF], F32, kind="ExternalInput").ap()
    w_projT = nc.dram_tensor("w_projT", [FF, D], F32, kind="ExternalInput").ap()
    cbf_d = nc.dram_tensor("cbf", [P, CCORE_W], BF16, kind="ExternalInput").ap()
    cp1_d = nc.dram_tensor("cp1", [P, CP1_W], BF16, kind="ExternalInput").ap()
    cf32_d = nc.dram_tensor("cf32", [P, CF_W], F32, kind="ExternalInput").ap()
    outT = nc.dram_tensor("outT", [D, SEQ], F32, kind="ExternalOutput").ap()
    spill_kind = "ExternalOutput" if debug else "Internal"
    y1n_sp = nc.dram_tensor("y1n_sp", [D, SEQ], F32, kind=spill_kind).ap()
    dbg = {}
    if debug:
        for name in ["d_cx", "d_ck", "d_y1", "d_den", "d_y1n0", "d_y10"]:
            dbg[name] = nc.dram_tensor(name, [P, 2048], F32, kind="ExternalOutput").ap()
        for name in ["d_w", "d_af", "d_ltx", "d_ltk"]:
            dbg[name] = nc.dram_tensor(name, [P, 2048], BF16, kind="ExternalOutput").ap()

    xT_r = xT.rearrange("(jb p) t -> p jb t", p=P)
    sp_r = y1n_sp.rearrange("(jb p) t -> p jb t", p=P)
    out_r = outT.rearrange("(jb p) t -> p jb t", p=P)

    s_x = Sem(nc, "s_x")
    s_w = Sem(nc, "s_w")
    s_w2 = Sem(nc, "s_w2")
    s_c = Sem(nc, "s_c")
    s_sp = Sem(nc, "s_sp")
    s_spi = Sem(nc, "s_spi")
    s_out = Sem(nc, "s_out")
    s_ext = Sem(nc, "s_ext")
    s_pe = Sem(nc, "s_pe")
    s_v = Sem(nc, "s_v")
    s_sc = Sem(nc, "s_sc")
    s_gp = Sem(nc, "s_gp")

    cbf = nc.alloc_sbuf_tensor("cbf_sb", [P, CCORE_W], BF16)
    cf = nc.alloc_sbuf_tensor("cf32_sb", [P, CF_W], F32)
    sm = nc.alloc_sbuf_tensor("sm", [16, 6 * T], F32)
    smb = nc.alloc_sbuf_tensor("smb", [16, 3 * T], BF16)


    ph1_psum = ExitStack()
    bank = [ph1_psum.enter_context(nc.psum_tensor(f"pb{i}", [P, 512], F32)) for i in range(7)]
    ps7 = ph1_psum.enter_context(nc.psum_tensor("pb7", [P, 1024], BF16))

    ph1 = ExitStack()

    def sb1(name, shape, dt):
        return ph1.enter_context(nc.sbuf_tensor(name, shape, dt))

    w_in_bf = sb1("w_in_bf", [P, 8 * 3328], BF16)
    w_out_bf = sb1("w_out_bf", [P, 8 * 1024], BF16)
    x_sb = [sb1(f"x{i}", [P, NB * T], F32) for i in range(3)]
    h_bf = sb1("h_bf", [P, NB * T], BF16)
    g_sb = [sb1(f"g_sb{i}", [P, NB * T], F32) for i in range(2)]
    cx = [sb1(f"cx{i}", [P, NB * T], F32) for i in range(2)]
    ck = [sb1(f"ck{i}", [P, NB * T], F32) for i in range(2)]
    k_sb = [sb1(f"k_sb{i}", [P, T], F32) for i in range(2)]
    q_bf = [sb1(f"q_bf{i}", [P, T], BF16) for i in range(2)]
    z_bf = [sb1(f"z_bf{i}", [P, NB * T], BF16) for i in range(2)]
    sc_bf = sb1("sc_bf", [P, NB * T], BF16)
    w_sb = sb1("w_sb", [P, NB * T], BF16)
    af_bf = sb1("af_bf", [P, NB * T], BF16)
    t_f32 = sb1("t_f32", [P, 2 * T], F32)
    y1_sb = sb1("y1_sb", [P, NB * T], F32)
    y1b_bf = h_bf
    y1n_st = sb1("y1n_st", [P, NB * T], F32)
    lhsT_k = sb1("lhsT_k", [P, 8 * 128], BF16)
    lhsT_x = sb1("lhsT_x", [P, 8 * 128], BF16)
    stg4 = sb1("stg4", [P, 32], BF16)
    stgx = sb1("stgx", [4, 8 * 128], BF16)
    cp1 = sb1("cp1_sb", [P, CP1_W], BF16)
    iwb = sb1("iwb", [16, T], BF16)
    rdb_ = sb1("rdb", [16, T], BF16)

    print('PH1 sbuf remaining:', nc.sbuf_bytes_remaining)

    sy.dma_start(out=cbf[:], in_=cbf_d).then_inc(s_c.h, 16)
    sy.dma_start(out=cp1[:], in_=cp1_d).then_inc(s_c.h, 16)
    sy.dma_start(out=cf[:], in_=cf32_d).then_inc(s_c.h, 16)
    s_c.v += 48
    win_r = w_in.rearrange("(kb p) m -> p kb m", p=P)
    gp.dma_start(out=w_in_bf[:].rearrange("p (kb m) -> p kb m", kb=8), in_=win_r).then_inc(s_w.h, 16)
    wout_r = w_out.rearrange("(kb p) m -> p kb m", p=P)
    gp.dma_start(out=w_out_bf[:].rearrange("p (kb m) -> p kb m", kb=8), in_=wout_r).then_inc(s_w.h, 16)
    s_w.v += 32
    s_w.wait(ve)
    s_c.wait(ve)
    for kb in range(NB):
        ins = ve.tensor_scalar(w_in_bf[:, kb * 3328:(kb + 1) * 3328],
                               w_in_bf[:, kb * 3328:(kb + 1) * 3328],
                               cf[:, K_RMS + kb:K_RMS + kb + 1], None, ALU.mult)
    s_v.inc(ins)
    s_v.inc(ve.memset(lhsT_k[:], 0.0))
    s_v.inc(ve.memset(lhsT_x[:], 0.0))
    x_loaded = [None, None, None]
    for i in range(min(3, n_tiles)):
        sy.dma_start(out=x_sb[i][:].rearrange("p (jb t) -> p jb t", jb=NB),
                     in_=xT_r[:, :, i * T:(i + 1) * T]).then_inc(s_x.h, 16)
        s_x.v += 16
        x_loaded[i] = s_x.v

    spill_busy = [None]
    wv_after_rms = s_v.v

    st1 = {}
    iter_marks = {}
    for it in range(n_tiles + 2):
        f = it
        b = it - 1
        tb1 = it - 2
        have_f = f < n_tiles
        have_b = 0 <= b < n_tiles
        have_tb1 = 0 <= tb1 < n_tiles
        if it - 2 in iter_marks:
            mpe, mv, msc, mgp = iter_marks[it - 2]
            for eng in (te, ve, sc, gp, sy):
                s_pe.wait(eng, mpe)
                s_v.wait(eng, mv)
                s_sc.wait(eng, msc)
                s_gp.wait(eng, mgp)

        if have_f:
            fpp = f % 2
            fxp = f % 3
            xa = x_sb[fxp]
            s_x.wait(sc, x_loaded[fxp])
            s_c.wait(sc)
            if b >= 1:
                s_pe.wait(sc, st1[b - 1]["vst1"])
            vsq = s_sc.inc(sc.activation(sc_bf[:], xa[:], AF.Square))
            s_sc.wait(te, vsq)
            s_c.wait(te)
            if b >= 1:
                s_v.wait(te, st1[b - 1]["y1v"])
            for kb in range(NB):
                ins = te.matmul(bank[2][0:1, 0:T], cbf[:, CC_ONES1:CC_ONES1 + 1],
                                sc_bf[:, kb * T:(kb + 1) * T], start=(kb == 0), stop=(kb == NB - 1))
            vsumsq = s_pe.inc(ins)
            s_pe.wait(sc, vsumsq)
            sc.activation(sm[0:1, 0:T], bank[2][0:1, 0:T], AF.Ln,
                          bias=cf[0:1, K_E6:K_E6 + 1], scale=1.0 / D)
            vr = s_sc.inc(sc.activation(smb[0:1, 0:T], sm[0:1, 0:T], AF.Exp, scale=-0.5))
            s_sc.wait(te, vr)
            vrbc = s_pe.inc(te.matmul(bank[3][:, 0:T], cbf[0:1, CC_ONESR:CC_ONESR + P],
                                      smb[0:1, 0:T], start=True, stop=True))
            s_pe.wait(ve, vrbc)
            s_x.wait(ve, x_loaded[fxp])
            if b >= 1:
                s_pe.wait(ve, st1[b - 1]["vst1"])
            if it == 0:
                s_v.wait(ve, wv_after_rms)
            for kb in range(NB):
                ins = ve.tensor_tensor(h_bf[:, kb * T:(kb + 1) * T], xa[:, kb * T:(kb + 1) * T],
                                       bank[3][:, 0:T], ALU.mult)
            vh = s_v.inc(ins)
            st1[f] = {"vh": vh, "vsumsq": vsumsq}

        if have_b:
            bpp = b % 2
            bxp = b % 3
            sb_prev = st1[b]

        def back_vstep(stepname):
            if not have_b:
                return
            bs = st1[b]
            if stepname == "vx":
                for sem, val in bs["g_marks"]:
                    sem.wait(ve, val)
                for jb in range(NB):
                    ins = ve.scalar_tensor_tensor(cx[bpp][:, jb * T:(jb + 1) * T],
                                                  g_sb[bpp][:, jb * T:(jb + 1) * T], 1.0,
                                                  cx[bpp][:, jb * T:(jb + 1) * T],
                                                  ALU.subtract, ALU.mult)
                bs["vvx"] = s_v.inc(ins)
            elif stepname == "vk":
                for jb in range(NB):
                    ins = ve.scalar_tensor_tensor(ck[bpp][:, jb * T:(jb + 1) * T],
                                                  g_sb[bpp][:, jb * T:(jb + 1) * T], 1.0,
                                                  k_sb[bpp][:], ALU.subtract, ALU.mult)
                bs["vvk"] = s_v.inc(ins)
            elif stepname == "ext":
                if b >= 1 and "vtr" in st1[b - 1]:
                    s_pe.wait(ve, st1[b - 1]["vtr"])
                C0b = b * CPT
                for jb in range(NB):
                    ve.tensor_scalar(lhsT_k[0:64, jb * 128 + C0b:jb * 128 + C0b + CPT],
                                     ck[bpp][0:64, jb * T:(jb + 1) * T:CH], -0.125, None, ALU.mult)
                    ve.tensor_scalar(lhsT_k[64:128, jb * 128 + 64 + C0b:jb * 128 + 64 + C0b + CPT],
                                     ck[bpp][64:128, jb * T:(jb + 1) * T:CH], -0.125, None, ALU.mult)
                    ins = ve.tensor_scalar(stg4[:, jb * CPT:(jb + 1) * CPT],
                                           cx[bpp][:, jb * T:(jb + 1) * T:CH], -1.0, None, ALU.mult)
                bs["vex"] = s_v.inc(ins)
                s_v.wait(gp, bs["vex"])
                bs["vgm"] = s_gp.inc(gp.memset(g_sb[bpp][:, 0:NB * T:CH], 0.0))
            elif stepname == "scan":
                s_gp.wait(ve, bs["vgm"])
                ve.tensor_tensor_scan(cx[bpp][:], g_sb[bpp][:], cx[bpp][:], 0.0, ALU.mult, ALU.add)
                ins = ve.tensor_tensor_scan(ck[bpp][:], g_sb[bpp][:], ck[bpp][:], 0.0,
                                            ALU.mult, ALU.add)
                bs["vscan"] = s_v.inc(ins)
            elif stepname == "pfull":
                if have_f:
                    s_pe.wait(ve, st1[f]["vsumsq"])
                elif b >= 1:
                    s_pe.wait(ve, st1[b - 1]["vst1"])
                for sem, val in bs["qk_marks"]:
                    sem.wait(ve, val)
                for jb in range(NB):
                    ins = ve.tensor_tensor(sc_bf[:, jb * T:(jb + 1) * T], q_bf[bpp][:],
                                           ck[bpp][:, jb * T:(jb + 1) * T], ALU.mult)
                bs["vP"] = s_v.inc(ins)

        def tail_part(part):
            if not have_tb1:
                return
            ts_ = st1[tb1]
            if part == 0:
                s_pe.wait(ve, ts_["vst1"])
                ve.tensor_scalar(sm[0:1, 2 * T:3 * T], bank[6][0:1, 0:T], 1.0 / D, None, ALU.mult)
                ve.tensor_scalar(sm[0:1, 3 * T:4 * T], bank[6][0:1, T:2 * T], 1.0 / D,
                                 None, ALU.mult)
                ve.tensor_tensor(sm[0:1, 5 * T:6 * T], sm[0:1, 2 * T:3 * T],
                                 sm[0:1, 2 * T:3 * T], ALU.mult)
                vvar = s_v.inc(ve.tensor_tensor(sm[0:1, 3 * T:4 * T], sm[0:1, 3 * T:4 * T],
                                                sm[0:1, 5 * T:6 * T], ALU.subtract))
                ts_["vvar"] = vvar
            elif part == 1:
                s_v.wait(sc, ts_["vvar"])
                sc.activation(sm[0:1, 4 * T:5 * T], sm[0:1, 3 * T:4 * T], AF.Ln,
                              bias=cf[0:1, K_E5:K_E5 + 1])
                vr1 = s_sc.inc(sc.activation(sm[0:1, T:2 * T], sm[0:1, 4 * T:5 * T],
                                             AF.Exp, scale=-0.5))
                s_sc.wait(ve, vr1)
                ve.tensor_copy(smb[0:1, T:2 * T], sm[0:1, 2 * T:3 * T])
                ts_["vmb"] = s_v.inc(ve.tensor_copy(smb[0:1, 2 * T:3 * T], sm[0:1, T:2 * T]))
            elif part == 2:
                s_v.wait(te, ts_["vmb"])
                te.matmul(bank[0][:, 0:T], cbf[0:1, CC_ONESR:CC_ONESR + P], smb[0:1, T:2 * T],
                          start=True, stop=True)
                ts_["vbc"] = s_pe.inc(te.matmul(bank[1][:, 0:T], cbf[0:1, CC_ONESR:CC_ONESR + P],
                                                smb[0:1, 2 * T:3 * T], start=True, stop=True))
            else:
                s_pe.wait(ve, ts_["vbc"])
                nonlocal_spill = spill_busy[0]
                if nonlocal_spill is not None:
                    s_sp.wait(ve, nonlocal_spill)
                for jb in range(NB):
                    ve.tensor_tensor(y1n_st[:, jb * T:(jb + 1) * T],
                                     y1_sb[:, jb * T:(jb + 1) * T], bank[0][:, 0:T], ALU.subtract)
                for jb in range(NB):
                    ins2 = ve.tensor_tensor(y1n_st[:, jb * T:(jb + 1) * T],
                                            y1n_st[:, jb * T:(jb + 1) * T], bank[1][:, 0:T],
                                            ALU.mult)
                vln = s_v.inc(ins2)
                ts_["vln"] = vln
                s_v.wait(sy, vln)
                if debug and tb1 == 0:
                    sy.dma_start(out=dbg["d_y1n0"], in_=y1n_st[:]).then_inc(s_out.h, 16)
                    sy.dma_start(out=dbg["d_y10"], in_=y1_sb[:]).then_inc(s_out.h, 16)
                    s_out.v += 32
                sy.dma_start(out=sp_r[:, :, tb1 * T:(tb1 + 1) * T],
                             in_=y1n_st[:].rearrange("p (jb t) -> p jb t", jb=NB)
                             ).then_inc(s_sp.h, 16)
                s_sp.v += 16
                spill_busy[0] = s_sp.v

        def emit_transposes():
            bs = st1[b]
            C0 = b * CPT
            s_v.wait(te, bs["vex"])
            for jb in range(NB):
                ins = te.transpose(ps7[0:CPT, jb * 128:(jb + 1) * 128],
                                   stg4[:, jb * CPT:(jb + 1) * CPT], cbf[:, CC_ID:CC_ID + P])
            vtr = s_pe.inc(ins)
            bs["vtr"] = vtr
            s_pe.wait(ve, vtr)
            s_ext.wait(ve)
            vstg = s_v.inc(ve.tensor_copy(stgx[0:CPT, :], ps7[0:CPT, 0:8 * 128]))
            s_v.wait(sy, vstg)
            stgx_r = stgx[:].rearrange("c (jb ff) -> c jb ff", jb=NB)
            ltx_r = lhsT_x[:].rearrange("p (jb ff) -> p jb ff", jb=NB)
            sy.dma_start(out=ltx_r[C0:C0 + CPT, :, 0:64],
                         in_=stgx_r[0:CPT, :, 0:64]).then_inc(s_ext.h, 16)
            sy.dma_start(out=ltx_r[64 + C0:64 + C0 + CPT, :, 64:128],
                         in_=stgx_r[0:CPT, :, 64:128]).then_inc(s_ext.h, 16)
            s_ext.v += 32
            bs["tr_done"] = True

        if have_f:
            ms_order = (list(range(16, 24)) + list(range(0, 8)) + list(range(8, 16))
                        + [24, 25])
            groups = [ms_order[gg:gg + 4] for gg in range(0, PROJ_M, 4)]
            evac_done = {}
            g_marks = []
            qk_marks = []
            s_v.wait(te, st1[f]["vh"])
            if it == 0:
                s_w.wait(te)
                s_v.wait(te, wv_after_rms)
            back_steps = {}
            back_vstep("vx")
            back_vstep("vk")
            back_vstep("ext")
            back_steps = {1: "scan", 2: "pfull"}

            for gi, grp in enumerate(groups):
                pb = [bank[2], bank[3]] if gi % 2 == 0 else [bank[4], bank[5]]
                if gi >= 2:
                    for sem, val in evac_done[gi - 2]:
                        sem.wait(te, val)
                for mi, m in enumerate(grp):
                    pt = pb[mi // 2][:, (mi % 2) * T:(mi % 2) * T + T]
                    for kb in range(NB):
                        ins = te.matmul(pt, w_in_bf[:, kb * 3328 + m * 128:kb * 3328 + (m + 1) * 128],
                                        h_bf[:, kb * T:(kb + 1) * T],
                                        start=(kb == 0 and mi in (0, 2)), stop=(kb == NB - 1))
                vpg = s_pe.inc(ins)
                s_pe.wait(sc, vpg)
                s_pe.wait(ve, vpg)
                last_sc = None
                last_v = None
                for mi, m in enumerate(grp):
                    pt = pb[mi // 2][:, (mi % 2) * T:(mi % 2) * T + T]
                    if m < 8:
                        last_sc = sc.activation(z_bf[fpp][:, m * T:(m + 1) * T], pt, AF.Sigmoid)
                    elif m < 16:
                        last_v = ve.tensor_copy(cx[fpp][:, (m - 8) * T:(m - 7) * T], pt)
                    elif m < 24:
                        last_sc = sc.activation(g_sb[fpp][:, (m - 16) * T:(m - 15) * T], pt,
                                                AF.Sigmoid,
                                                bias=cf[:, K_GB + (m - 16):K_GB + (m - 16) + 1])
                    elif m == 24:
                        last_sc = sc.activation(q_bf[fpp][:], pt, AF.Copy)
                    else:
                        vq = s_sc.inc(last_sc)
                        last_sc = None
                        s_sc.wait(ve, vq)
                        qk_marks.append((s_sc, vq))
                        last_v = ve.tensor_copy(k_sb[fpp][:], pt)
                marks = []
                if last_sc is not None:
                    marks.append((s_sc, s_sc.inc(last_sc)))
                if last_v is not None:
                    marks.append((s_v, s_v.inc(last_v)))
                evac_done[gi] = marks
                if gi in (0, 1):
                    g_marks.extend(marks)
                if gi == 6:
                    qk_marks.extend(marks)
                if gi in back_steps:
                    back_vstep(back_steps[gi])
                if gi == 2:
                    if have_b:
                        emit_transposes()
                    tail_part(0)
                elif gi == 3:
                    tail_part(1)
                elif gi == 4:
                    tail_part(2)
                elif gi == 5:
                    tail_part(3)
            st1[f]["g_marks"] = g_marks
            st1[f]["qk_marks"] = qk_marks
            st1[f]["evac56"] = evac_done[5] + evac_done[6]
            st1[f]["proj_done"] = s_pe.v
        else:
            for stepname in ("vx", "vk", "ext", "scan", "pfull"):
                back_vstep(stepname)
            for part_ in (0, 1, 2, 3):
                tail_part(part_)

        if have_b:
            bs = st1[b]
            C0 = b * CPT
            t0b = b * T
            if not bs.get("tr_done"):
                emit_transposes()
            s_v.wait(te, bs["vP"])
            if b >= 1:
                s_v.wait(te, st1[b - 1]["vvar"])
            for j in range(NB):
                ins = te.matmul(bank[6][0:16, 0:T], cp1[:, P1_LSE + j * 16:P1_LSE + (j + 1) * 16],
                                sc_bf[:, j * T:(j + 1) * T], start=(j == 0), stop=(j == NB - 1))
            vlse = s_pe.inc(ins)
            s_pe.wait(sc, vlse)
            viw = s_sc.inc(sc.activation(sm[0:16, 4 * T:5 * T], bank[6][0:16, 0:T], AF.Exp,
                                         scale=0.125))
            s_ext.wait(te)
            for sem, val in bs["evac56"]:
                sem.wait(te, val)
            if have_f:
                for sem, val in st1[f]["evac56"]:
                    sem.wait(te, val)
            exp_done = {}

            def emit_wd(j):
                if j == 0:
                    s_sc.wait(te, viw)
                s_sc.wait(te, exp_done[j])
                ins = te.matmul(bank[6][0:16, T:2 * T], cp1[:, P1_WD + j * 16:P1_WD + (j + 1) * 16],
                                w_sb[:, j * T:(j + 1) * T], start=False, stop=(j == NB - 1))
                s_pe.inc(ins)

            for j in range(NB):
                sb_ = bank[[2, 3, 0, 1][j % 4]]
                if j in (2, 3) and b >= 1 and "vln" in st1[b - 1]:
                    s_v.wait(te, st1[b - 1]["vln"])
                if j >= 4:
                    s_sc.wait(te, exp_done[j - 4])
                te.matmul(sb_[:, 0:T], lhsT_k[:, j * 128:(j + 1) * 128], q_bf[bpp][:],
                          start=True, stop=False)
                ins = te.matmul(sb_[:, 0:T], cp1[0:4, P1_MASK + b * 128:P1_MASK + (b + 1) * 128],
                                cp1[0:4, P1_BLK:P1_BLK + T], start=False, stop=True)
                vsc_ = s_pe.inc(ins)
                s_pe.wait(sc, vsc_)
                exp_done[j] = s_sc.inc(sc.activation(w_sb[:, j * T:(j + 1) * T], sb_[:, 0:T], AF.Exp))
                if j >= 3:
                    emit_wd(j - 3)
            for j in range(NB - 3, NB):
                emit_wd(j)
            vwd = s_pe.v
            s_sc.wait(ve, viw)
            viwb = s_v.inc(ve.tensor_copy(iwb[0:16, :], sm[0:16, 4 * T:5 * T]))
            s_pe.wait(ve, vwd)
            ve.tensor_tensor(sm[0:16, 5 * T:6 * T], bank[6][0:16, T:2 * T],
                             sm[0:16, 4 * T:5 * T], ALU.add)
            ve.reciprocal(sm[0:16, 3 * T:4 * T], sm[0:16, 5 * T:6 * T])
            vden = s_v.inc(ve.tensor_copy(rdb_[0:16, :], sm[0:16, 3 * T:4 * T]))
            s_sc.wait(te, exp_done[NB - 1])
            pair_v = {}
            for jj in range(4):
                ob = bank[2 + (jj % 2)]
                ib = bank[4 + (jj % 2)]
                rb = bank[jj % 2]
                if jj >= 2:
                    s_v.wait(te, pair_v[jj - 2])
                if jj == 0:
                    s_v.wait(te, viwb)
                for h2_ in (0, 1):
                    j = 2 * jj + h2_
                    sel = cp1[0:16, P1_BCH + j * 128:P1_BCH + (j + 1) * 128]
                    te.matmul(ob[:, h2_ * T:h2_ * T + T], lhsT_x[:, j * 128:(j + 1) * 128],
                              w_sb[:, j * T:(j + 1) * T], start=(h2_ == 0), stop=True)
                    ins = te.matmul(ib[:, h2_ * T:h2_ * T + T], sel, iwb[0:16, 0:T],
                                    start=(h2_ == 0), stop=True)
                vja = s_pe.inc(ins)
                if jj == 0:
                    s_v.wait(te, vden)
                for h2_ in (0, 1):
                    j = 2 * jj + h2_
                    sel = cp1[0:16, P1_BCH + j * 128:P1_BCH + (j + 1) * 128]
                    ins = te.matmul(rb[:, h2_ * T:h2_ * T + T], sel, rdb_[0:16, 0:T],
                                    start=(h2_ == 0), stop=True)
                vjb = s_pe.inc(ins)
                cols = slice(2 * jj * T, (2 * jj + 2) * T)
                s_pe.wait(ve, vja)
                ve.tensor_tensor(t_f32[:], ib[:, 0:2 * T], cx[bpp][:, cols], ALU.mult)
                ve.tensor_tensor(t_f32[:], ob[:, 0:2 * T], t_f32[:], ALU.subtract)
                s_pe.wait(ve, vjb)
                ve.tensor_tensor(t_f32[:], t_f32[:], rb[:, 0:2 * T], ALU.mult)
                ins = ve.tensor_tensor(af_bf[:, cols], t_f32[:], z_bf[bpp][:, cols], ALU.mult)
                pair_v[jj] = s_v.inc(ins)
            vaf = s_v.v
            s_v.wait(te, vaf)
            if it == 1:
                s_w.wait(te)
            y1v = None
            vsq1 = None
            vyb = None
            for gi2 in range(2):
                pb2 = [bank[2], bank[3]] if gi2 == 0 else [bank[4], bank[5]]
                for mi in range(4):
                    m = gi2 * 4 + mi
                    pt = pb2[mi // 2][:, (mi % 2) * T:(mi % 2) * T + T]
                    for kb in range(NB):
                        ins = te.matmul(pt, w_out_bf[:, kb * 1024 + m * 128:kb * 1024 + (m + 1) * 128],
                                        af_bf[:, kb * T:(kb + 1) * T],
                                        start=(kb == 0 and mi in (0, 2)), stop=(kb == NB - 1))
                vwo = s_pe.inc(ins)
                s_pe.wait(ve, vwo)
                for mi in range(4):
                    m = gi2 * 4 + mi
                    pt = pb2[mi // 2][:, (mi % 2) * T:(mi % 2) * T + T]
                    ins = ve.tensor_tensor(y1_sb[:, m * T:(m + 1) * T], pt,
                                           x_sb[bxp][:, m * T:(m + 1) * T], ALU.add)
                y1v = s_v.inc(ins)
                half = slice(gi2 * 4 * T, (gi2 + 1) * 4 * T)
                s_v.wait(sc, y1v)
                if gi2 == 0:
                    s_pe.wait(sc, vlse)
                vsq1 = s_sc.inc(sc.activation(sc_bf[:, half], y1_sb[:, half], AF.Square))
                if gi2 == 0 and have_f:
                    s_pe.wait(sc, st1[f]["proj_done"])
                vyb = s_sc.inc(sc.activation(y1b_bf[:, half], y1_sb[:, half], AF.Copy))
            bs["y1v"] = y1v
            if b + 3 < n_tiles:
                s_v.wait(sy, y1v)
                sy.dma_start(out=x_sb[b % 3][:].rearrange("p (jb t) -> p jb t", jb=NB),
                             in_=xT_r[:, :, (b + 3) * T:(b + 4) * T]).then_inc(s_x.h, 16)
                s_x.v += 16
                x_loaded[b % 3] = s_x.v
            s_sc.wait(te, vsq1)
            s_sc.wait(te, vyb)
            s_v.wait(te, vden)
            for kb in range(NB):
                te.matmul(bank[6][0:1, 0:T], cbf[:, CC_ONES1:CC_ONES1 + 1],
                          y1b_bf[:, kb * T:(kb + 1) * T], start=(kb == 0), stop=False)
            for kb in range(NB):
                ins = te.matmul(bank[6][0:1, T:2 * T], cbf[:, CC_ONES1:CC_ONES1 + 1],
                                sc_bf[:, kb * T:(kb + 1) * T], start=False, stop=(kb == NB - 1))
            vst1 = s_pe.inc(ins)
            bs["vst1"] = vst1
            bs["pair_last"] = pair_v[3]

        iter_marks[it] = (s_pe.v, s_v.v, s_sc.v, s_gp.v)

    if not phase2:
        s_sp.wait(sy)
        s_out.wait(sy)
        return nc

    pe_fin, v_fin, sc_fin, gp_fin = s_pe.v, s_v.v, s_sc.v, s_gp.v
    for eng in (gp, sy):
        s_pe.wait(eng, pe_fin)
        s_v.wait(eng, v_fin)
        s_sc.wait(eng, sc_fin)
        s_sp.wait(eng)
        s_ext.wait(eng)
        s_x.wait(eng)
    ph1.close()
    ph1_psum.close()

    ph2 = ExitStack()

    def sb2(name, shape, dt):
        return ph2.enter_context(nc.sbuf_tensor(name, shape, dt))

    w_fc_bf = sb2("w_fc_bf", [P, 8 * FF], BF16)
    w_pj_bf = sb2("w_pj_bf", [P, NFB * 1024], BF16)
    yn_sb = [sb2(f"yn{i}", [P, NB * T], F32) for i in range(2)]
    yn_bf = sb2("yn_bf", [P, NB * T], BF16)
    h2_bf = sb2("h2_bf", [P, 6 * T], BF16)
    yc_sb = sb2("yc_sb", [P, NB * T], F32)
    mem_sb = sb2("mem_sb", [P, NB * T], F32)
    acc_bf = sb2("acc_bf", [P, NB * T], BF16)
    sig_bf = sb2("sig_bf", [P, NB * T], BF16)
    relu_f = sb2("relu_f", [P, T], F32)
    o_st = [sb2(f"o_st{i}", [P, NB * T], F32) for i in range(2)]
    bank2 = [ph2.enter_context(nc.psum_tensor(f"qb{i}", [P, 512], F32)) for i in range(8)]
    print('PH2 sbuf remaining:', nc.sbuf_bytes_remaining)

    wfc_r = w_fcT.rearrange("(kb p) f -> p kb f", p=P)
    wfc_sb_r = w_fc_bf[:].rearrange("p (kb f) -> p kb f", kb=8)
    fc_chunk_v = []
    for c in range(4):
        gp.dma_start(out=wfc_sb_r[:, :, c * 1024:(c + 1) * 1024],
                     in_=wfc_r[:, :, c * 1024:(c + 1) * 1024]).then_inc(s_w2.h, 16)
        s_w2.v += 16
        fc_chunk_v.append(s_w2.v)
    wpj_r = w_projT.rearrange("(fb p) d2 -> p fb d2", p=P)
    wpj_sb_r = w_pj_bf[:].rearrange("p (fb d2) -> p fb d2", fb=NFB)
    pj_chunk_v = []
    for c in range(4):
        gp.dma_start(out=wpj_sb_r[:, c * 8:(c + 1) * 8, :],
                     in_=wpj_r[:, c * 8:(c + 1) * 8, :]).then_inc(s_w2.h, 16)
        s_w2.v += 16
        pj_chunk_v.append(s_w2.v)

    s_sp.wait(sy)
    sy.dma_start(out=yn_sb[0][:].rearrange("p (jb t) -> p jb t", jb=NB),
                 in_=sp_r[:, :, 0:T]).then_inc(s_spi.h, 16)
    s_spi.v += 16
    yn_loaded = [s_spi.v, None]
    if n_tiles > 1:
        sy.dma_start(out=yn_sb[1][:].rearrange("p (jb t) -> p jb t", jb=NB),
                     in_=sp_r[:, :, T:2 * T]).then_inc(s_spi.h, 16)
        s_spi.v += 16
        yn_loaded[1] = s_spi.v
    yn_last_read = [None, None]
    out_busy = [None, None]

    st2 = {}
    for J in range(n_tiles + 2):
        f = J
        lb = J - 1
        tb = J - 2
        have_f = f < n_tiles
        have_lb = 0 <= lb < n_tiles
        have_tb = 0 <= tb < n_tiles

        if have_f:
            pp = f % 2
            if f >= 1 and "vynb_next" in st2.get(f - 1, {}):
                vynb = st2[f - 1]["vynb_next"]
            else:
                s_spi.wait(ve, yn_loaded[pp])
                if f >= 1:
                    s_pe.wait(ve, st2[f - 1]["vfc_last"])
                vynb = s_v.inc(ve.tensor_copy(yn_bf[:], yn_sb[pp][:]))

        if have_lb:
            ls = st2[lb]
            lpp = lb % 2
            s_pe.wait(ve, ls["vpj"])
            for mb in range(NB):
                pt = bank2[4 + mb // 2][:, (mb % 2) * T:(mb % 2) * T + T]
                ins = ve.tensor_scalar(yc_sb[:, mb * T:(mb + 1) * T], pt, -0.5, None, ALU.add)
            vyc = s_v.inc(ins)
            s_v.wait(sc, vyc)
            if lb >= 1 and "vst2" in st2.get(lb - 1, {}):
                s_pe.wait(sc, st2[lb - 1]["vst2"])
            vs1 = s_sc.inc(sc.activation(sig_bf[:], yc_sb[:], AF.Sign, bias=cf[:, K_NH:K_NH + 1]))
            s_sc.wait(ve, vs1)
            vacc = s_v.inc(ve.tensor_copy(acc_bf[:], sig_bf[:]))
            ve.tensor_scalar(mem_sb[:], yc_sb[:], 1.9, 0.45, ALU.mult, ALU.add)
            vm = s_v.inc(ve.scalar_tensor_tensor(mem_sb[:], sig_bf[:], -0.5, mem_sb[:],
                                                 ALU.mult, ALU.add))
            lstate = {"vm": vm, "vacc": vacc, "step": 2}
        else:
            lstate = None

        def ladder_step():
            if lstate is None or lstate["step"] > 5:
                return
            k = lstate["step"]
            s_v.wait(sc, lstate["vm"])
            vsg = s_sc.inc(sc.activation(sig_bf[:], mem_sb[:], AF.Sign,
                                         bias=cf[:, K_N1:K_N1 + 1]))
            s_sc.wait(ve, vsg)
            lstate["vacc"] = s_v.inc(ve.tensor_tensor(acc_bf[:], acc_bf[:], sig_bf[:], ALU.add))
            if k < 5:
                s_sc.wait(ve, vsg)
                ve.scalar_tensor_tensor(mem_sb[:], mem_sb[:], 0.9, yc_sb[:], ALU.mult, ALU.add)
                lstate["vm"] = s_v.inc(ve.scalar_tensor_tensor(mem_sb[:], sig_bf[:], -0.5,
                                                               mem_sb[:], ALU.mult, ALU.add))
            lstate["step"] = k + 1

        def tail_part(part):
            if not have_tb:
                return
            ts_ = st2[tb]
            tpp = tb % 2
            if part == 1:
                s_pe.wait(ve, ts_["vst2"])
                ve.tensor_scalar(sm[0:1, 2 * T:3 * T], bank2[3][0:1, 0:T], 1.0 / D, None, ALU.mult)
                ve.tensor_scalar(sm[0:1, 3 * T:4 * T], bank2[3][0:1, T:2 * T], 1.0 / D,
                                 None, ALU.mult)
                ve.tensor_tensor(sm[0:1, 0:T], sm[0:1, 2 * T:3 * T], sm[0:1, 2 * T:3 * T], ALU.mult)
                vvar2 = s_v.inc(ve.tensor_tensor(sm[0:1, 3 * T:4 * T], sm[0:1, 3 * T:4 * T],
                                                 sm[0:1, 0:T], ALU.subtract))
                s_v.wait(sc, vvar2)
                sc.activation(sm[0:1, 0:T], sm[0:1, 3 * T:4 * T], AF.Ln,
                              bias=cf[0:1, K_E5:K_E5 + 1])
                vr2 = s_sc.inc(sc.activation(sm[0:1, T:2 * T], sm[0:1, 0:T], AF.Exp, scale=-0.5))
                s_sc.wait(ve, vr2)
                ve.tensor_copy(smb[0:1, T:2 * T], sm[0:1, 2 * T:3 * T])
                ts_["vmb2"] = s_v.inc(ve.tensor_copy(smb[0:1, 2 * T:3 * T], sm[0:1, T:2 * T]))
            elif part == 2:
                s_v.wait(te, ts_["vmb2"])
                te.matmul(bank2[3][:, 0:T], cbf[0:1, CC_ONESR:CC_ONESR + P], smb[0:1, T:2 * T],
                          start=True, stop=True)
                ts_["vbc2"] = s_pe.inc(te.matmul(bank2[3][:, T:2 * T],
                                                 cbf[0:1, CC_ONESR:CC_ONESR + P],
                                                 smb[0:1, 2 * T:3 * T], start=True, stop=True))
            else:
                s_pe.wait(ve, ts_["vbc2"])
                for jb in range(NB):
                    ve.tensor_tensor(o_st[tpp][:, jb * T:(jb + 1) * T],
                                     o_st[tpp][:, jb * T:(jb + 1) * T], bank2[3][:, 0:T],
                                     ALU.subtract)
                for jb in range(NB):
                    ins = ve.tensor_tensor(o_st[tpp][:, jb * T:(jb + 1) * T],
                                           o_st[tpp][:, jb * T:(jb + 1) * T],
                                           bank2[3][:, T:2 * T], ALU.mult)
                vo = s_v.inc(ins)
                ts_["vo"] = vo
                s_v.wait(sy, vo)
                sy.dma_start(out=out_r[:, :, tb * T:(tb + 1) * T],
                             in_=o_st[tpp][:].rearrange("p (jb t) -> p jb t", jb=NB)
                             ).then_inc(s_out.h, 16)
                s_out.v += 16
                out_busy[tpp] = s_out.v

        if have_f:
            s_v.wait(te, vynb)
            if have_lb:
                s_v.wait(te, vyc)
            fc_evac = {}
            for g_ in range(16):
                bk = bank2[g_ % 3]
                if g_ >= 3:
                    s_sc.wait(te, fc_evac[g_ - 3])
                s_w2.wait(te, fc_chunk_v[(2 * g_) // 8])
                for fi in (0, 1):
                    Fb = 2 * g_ + fi
                    pt = bk[:, fi * T:fi * T + T]
                    for kb in range(NB):
                        ins = te.matmul(pt, w_fc_bf[:, kb * FF + Fb * 128:kb * FF + (Fb + 1) * 128],
                                        yn_bf[:, kb * T:(kb + 1) * T],
                                        start=(kb == 0 and fi == 0), stop=(kb == NB - 1))
                vfc = s_pe.inc(ins)
                s_pe.wait(sc, vfc)
                for fi in (0, 1):
                    pt = bk[:, fi * T:fi * T + T]
                    slot = (g_ % 3) * 2 + fi
                    sc.activation(relu_f[:], pt, AF.Relu)
                    ins = sc.activation(h2_bf[:, slot * T:(slot + 1) * T], relu_f[:], AF.Square)
                fc_evac[g_] = s_sc.inc(ins)
                if g_ == 1:
                    tail_part(1)
                elif g_ == 3:
                    ladder_step()
                elif g_ == 5:
                    ladder_step()
                    tail_part(2)
                elif g_ == 7:
                    ladder_step()
                elif g_ == 9:
                    ladder_step()
                    tail_part(3)
                for gp_ in ([g_ - 1] if g_ >= 1 else []) + ([15] if g_ == 15 else []):
                    s_sc.wait(te, fc_evac[gp_])
                    s_w2.wait(te, pj_chunk_v[(2 * gp_ + 1) // 8])
                    for fi in (0, 1):
                        Fb = 2 * gp_ + fi
                        slot = (gp_ % 3) * 2 + fi
                        for mb in range(NB):
                            pt = bank2[4 + mb // 2][:, (mb % 2) * T:(mb % 2) * T + T]
                            ins = te.matmul(pt,
                                            w_pj_bf[:, Fb * 1024 + mb * 128:Fb * 1024 + (mb + 1) * 128],
                                            h2_bf[:, slot * T:(slot + 1) * T],
                                            start=(Fb == 0 and mb in (0, 2, 4, 6)),
                                            stop=(Fb == NFB - 1))
            vpj = s_pe.inc(ins)
            st2[f] = {"vpj": vpj, "vfc_last": vfc}
        else:
            while lstate is not None and lstate["step"] <= 5:
                ladder_step()
            for part_ in (1, 2, 3):
                tail_part(part_)

        if have_lb:
            while lstate["step"] <= 5:
                ladder_step()
            vacc = lstate["vacc"]
            if out_busy[lpp] is not None:
                s_out.wait(ve, out_busy[lpp])
            vy2 = s_v.inc(ve.scalar_tensor_tensor(o_st[lpp][:], acc_bf[:], 0.1, yn_sb[lpp][:],
                                                  ALU.mult, ALU.add))
            yn_last_read[lpp] = vy2
            if lb + 2 < n_tiles:
                s_v.wait(sy, vy2)
                sy.dma_start(out=yn_sb[lpp][:].rearrange("p (jb t) -> p jb t", jb=NB),
                             in_=sp_r[:, :, (lb + 2) * T:(lb + 3) * T]).then_inc(s_spi.h, 16)
                s_spi.v += 16
                yn_loaded[lpp] = s_spi.v
            if f + 1 < n_tiles and f < n_tiles:
                s_spi.wait(gp, yn_loaded[(f + 1) % 2])
                s_pe.wait(gp, st2[f]["vfc_last"])
                st2[f]["vynb_next"] = s_gp.inc(gp.tensor_copy(yn_bf[:], yn_sb[(f + 1) % 2][:]))
            s_v.wait(sc, vy2)
            vsq2 = s_sc.inc(sc.activation(sig_bf[:], o_st[lpp][:], AF.Square))
            vy2b = s_v.inc(ve.tensor_copy(acc_bf[:], o_st[lpp][:]))
            s_sc.wait(te, vsq2)
            s_v.wait(te, vy2b)
            if have_tb and "vo" in st2[tb]:
                s_v.wait(te, st2[tb]["vo"])
            for kb in range(NB):
                te.matmul(bank2[3][0:1, 0:T], cbf[:, CC_ONES1:CC_ONES1 + 1],
                          acc_bf[:, kb * T:(kb + 1) * T], start=(kb == 0), stop=False)
            for kb in range(NB):
                ins = te.matmul(bank2[3][0:1, T:2 * T], cbf[:, CC_ONES1:CC_ONES1 + 1],
                                sig_bf[:, kb * T:(kb + 1) * T], start=False, stop=(kb == NB - 1))
            st2[lb]["vst2"] = s_pe.inc(ins)

    s_out.wait(sy)
    s_sp.wait(sy)
    ph2.close()
    return nc


def make_consts(rms_w, gate_bias):
    cbf = np.zeros((P, CCORE_W), np.float32)
    cbf[:, CC_ID:CC_ID + P] = np.eye(P)
    cbf[:, CC_ONES1] = 1.0
    cbf[0, CC_ONESR:CC_ONESR + P] = 1.0
    cp1 = np.zeros((P, CP1_W), np.float32)
    for j in range(NB):
        cp1[0:64, P1_LSE + j * 16 + 2 * j] = -1.0
        cp1[64:128, P1_LSE + j * 16 + 2 * j + 1] = -1.0
        cp1[0:64, P1_WD + j * 16 + 2 * j] = 1.0
        cp1[64:128, P1_WD + j * 16 + 2 * j + 1] = 1.0
    cc = np.arange(64)
    for i in range(NT):
        for jj in range(CPT):
            m = (cc >= i * CPT + jj).astype(np.float32) * NEGBIG
            cp1[jj, P1_MASK + i * 128:P1_MASK + i * 128 + 64] = m
            cp1[jj, P1_MASK + i * 128 + 64:P1_MASK + (i + 1) * 128] = m
    tt = np.arange(T)
    for jj in range(CPT):
        cp1[jj, P1_BLK:P1_BLK + T] = (tt // CH == jj).astype(np.float32)
    for j in range(NB):
        cp1[2 * j, P1_BCH + j * 128:P1_BCH + j * 128 + 64] = 1.0
        cp1[2 * j + 1, P1_BCH + j * 128 + 64:P1_BCH + (j + 1) * 128] = 1.0

    cf = np.zeros((P, CF_W), np.float32)
    cf[:, K_RMS:K_RMS + 8] = rms_w.reshape(8, P).T
    cf[:, K_GB:K_GB + 8] = gate_bias.reshape(8, P).T
    cf[:, K_E6] = 1e-6
    cf[:, K_E5] = 1e-5
    cf[:, K_N1] = -1.0
    cf[:, K_NH] = -0.5
    return cbf.astype(ml_dtypes.bfloat16), cp1.astype(ml_dtypes.bfloat16), cf


def kernel(L0_attn_hidden_states, L0_attn_rms_weight, L0_attn_W_in, L0_attn_gate_bias,
           L0_attn_W_out, L0_ffn_W_fc, L0_ffn_W_proj):
    hs = np.asarray(L0_attn_hidden_states, np.float32)
    xTs = np.ascontiguousarray(np.transpose(hs, (0, 2, 1)))
    w_in = np.ascontiguousarray(np.asarray(L0_attn_W_in, np.float32))
    w_out = np.ascontiguousarray(np.asarray(L0_attn_W_out, np.float32))
    w_fcT = np.ascontiguousarray(np.asarray(L0_ffn_W_fc, np.float32).T)
    w_projT = np.ascontiguousarray(np.asarray(L0_ffn_W_proj, np.float32).T)
    cbf, cp1, cf = make_consts(np.asarray(L0_attn_rms_weight, np.float32),
                               np.asarray(L0_attn_gate_bias, np.float32))
    nc = build()
    in_maps = [{"xT": xTs[b], "w_in": w_in, "w_out": w_out, "w_fcT": w_fcT,
                "w_projT": w_projT, "cbf": cbf, "cp1": cp1, "cf32": cf} for b in range(8)]
    res = run_bass_kernel_spmd(nc, in_maps, core_ids=list(range(8)))
    out = np.stack([np.asarray(r["outT"], np.float32).T for r in res.results])
    return np.ascontiguousarray(out)


# version 11
# speedup vs baseline: 1.1287x; 1.0290x over previous
import numpy as np
import ml_dtypes
from contextlib import ExitStack

import concourse.bass as bass
import concourse.mybir as mybir
from concourse.bass_utils import run_bass_kernel_spmd

F32 = mybir.dt.float32
BF16 = mybir.dt.bfloat16
AF = mybir.ActivationFunctionType
ALU = mybir.AluOpType

P = 128
T = 256
NT = 16
SEQ = 4096
D = 1024
NB = 8
CH = 64
CPT = T // CH
FF = 4096
NFB = 32
PROJ_M = 26
NEGBIG = -1.0e30

CC_ID = 0
CC_ONES1 = 128
CC_ONESR = 129
CCORE_W = 260
P1_LSE = 0
P1_WD = 128
P1_MASK = 256
P1_BLK = 2304
P1_BCH = 2560
CP1_W = 3584

K_RMS = 0
K_GB = 8
K_E6 = 16
K_E5 = 17
K_N1 = 18
K_NH = 19
CF_W = 24


_ACT_SET_CACHE = {}


def _act_set_id(nc, set_name):
    from concourse.hw_specs import get_activation_tables
    key = (nc.m.arch, set_name)
    if key not in _ACT_SET_CACHE:
        names = list(get_activation_tables(nc.m.arch).keys())
        _ACT_SET_CACHE[key] = names.index(set_name)
    return _ACT_SET_CACHE[key]


def preload_set(nc, sc, set_name):
    import concourse.mybir as mb
    return sc.add_instruction(mb.InstLoadActFuncSet(
        name=nc.get_next_instruction_name(),
        act_func_set_id=_act_set_id(nc, set_name), ins=[], outs=[]))


class Sem:
    def __init__(self, nc, name):
        self.h = nc.alloc_semaphore(name)
        self.v = 0

    def inc(self, instr, n=1):
        instr.then_inc(self.h, n)
        self.v += n
        return self.v

    def wait(self, eng, v=None):
        eng.wait_ge(self.h, self.v if v is None else v)


def build(debug=False, n_tiles=NT, phase2=True):
    nc = bass.Bass("TRN2", target_bir_lowering=False)
    te, ve, sc, gp, sy = nc.tensor, nc.vector, nc.scalar, nc.gpsimd, nc.sync

    xT = nc.dram_tensor("xT", [D, SEQ], F32, kind="ExternalInput").ap()
    w_in = nc.dram_tensor("w_in", [D, 3328], F32, kind="ExternalInput").ap()
    w_out = nc.dram_tensor("w_out", [D, D], F32, kind="ExternalInput").ap()
    w_fcT = nc.dram_tensor("w_fcT", [D, FF], F32, kind="ExternalInput").ap()
    w_projT = nc.dram_tensor("w_projT", [FF, D], F32, kind="ExternalInput").ap()
    cbf_d = nc.dram_tensor("cbf", [P, CCORE_W], BF16, kind="ExternalInput").ap()
    cp1_d = nc.dram_tensor("cp1", [P, CP1_W], BF16, kind="ExternalInput").ap()
    cf32_d = nc.dram_tensor("cf32", [P, CF_W], F32, kind="ExternalInput").ap()
    outT = nc.dram_tensor("outT", [D, SEQ], F32, kind="ExternalOutput").ap()
    spill_kind = "ExternalOutput" if debug else "Internal"
    y1n_sp = nc.dram_tensor("y1n_sp", [D, SEQ], F32, kind=spill_kind).ap()
    dbg = {}
    if debug:
        for name in ["d_cx", "d_ck", "d_y1", "d_den", "d_y1n0", "d_y10"]:
            dbg[name] = nc.dram_tensor(name, [P, 2048], F32, kind="ExternalOutput").ap()
        for name in ["d_w", "d_af", "d_ltx", "d_ltk"]:
            dbg[name] = nc.dram_tensor(name, [P, 2048], BF16, kind="ExternalOutput").ap()

    xT_r = xT.rearrange("(jb p) t -> p jb t", p=P)
    sp_r = y1n_sp.rearrange("(jb p) t -> p jb t", p=P)
    out_r = outT.rearrange("(jb p) t -> p jb t", p=P)

    s_x = Sem(nc, "s_x")
    s_w = Sem(nc, "s_w")
    s_w2 = Sem(nc, "s_w2")
    s_c = Sem(nc, "s_c")
    s_sp = Sem(nc, "s_sp")
    s_spi = Sem(nc, "s_spi")
    s_out = Sem(nc, "s_out")
    s_ext = Sem(nc, "s_ext")
    s_pe = Sem(nc, "s_pe")
    s_v = Sem(nc, "s_v")
    s_sc = Sem(nc, "s_sc")
    s_gp = Sem(nc, "s_gp")

    cbf = nc.alloc_sbuf_tensor("cbf_sb", [P, CCORE_W], BF16)
    cf = nc.alloc_sbuf_tensor("cf32_sb", [P, CF_W], F32)
    sm = nc.alloc_sbuf_tensor("sm", [16, 6 * T], F32)
    smb = nc.alloc_sbuf_tensor("smb", [16, 3 * T], BF16)


    ph1_psum = ExitStack()
    bank = [ph1_psum.enter_context(nc.psum_tensor(f"pb{i}", [P, 512], F32)) for i in range(7)]
    ps7 = ph1_psum.enter_context(nc.psum_tensor("pb7", [P, 1024], BF16))

    ph1 = ExitStack()

    def sb1(name, shape, dt):
        return ph1.enter_context(nc.sbuf_tensor(name, shape, dt))

    w_in_bf = sb1("w_in_bf", [P, 8 * 3328], BF16)
    w_out_bf = sb1("w_out_bf", [P, 8 * 1024], BF16)
    x_sb = [sb1(f"x{i}", [P, NB * T], F32) for i in range(3)]
    h_bf = sb1("h_bf", [P, NB * T], BF16)
    g_sb = [sb1(f"g_sb{i}", [P, NB * T], F32) for i in range(2)]
    cx = [sb1(f"cx{i}", [P, NB * T], F32) for i in range(2)]
    ck = [sb1(f"ck{i}", [P, NB * T], F32) for i in range(2)]
    k_sb = [sb1(f"k_sb{i}", [P, T], F32) for i in range(2)]
    q_bf = [sb1(f"q_bf{i}", [P, T], BF16) for i in range(2)]
    z_bf = [sb1(f"z_bf{i}", [P, NB * T], BF16) for i in range(2)]
    sc_bf = sb1("sc_bf", [P, NB * T], BF16)
    w_sb = sb1("w_sb", [P, NB * T], BF16)
    af_bf = sb1("af_bf", [P, NB * T], BF16)
    t_f32 = sb1("t_f32", [P, 2 * T], F32)
    y1_sb = sb1("y1_sb", [P, NB * T], F32)
    y1b_bf = h_bf
    y1n_st = sb1("y1n_st", [P, NB * T], F32)
    lhsT_k = sb1("lhsT_k", [P, 8 * 128], BF16)
    lhsT_x = sb1("lhsT_x", [P, 8 * 128], BF16)
    stg4 = sb1("stg4", [P, 32], BF16)
    stgx = sb1("stgx", [4, 8 * 128], BF16)
    cp1 = sb1("cp1_sb", [P, CP1_W], BF16)
    iwb = sb1("iwb", [16, T], BF16)
    rdb_ = sb1("rdb", [16, T], BF16)

    print('PH1 sbuf remaining:', nc.sbuf_bytes_remaining)

    sy.dma_start(out=cbf[:], in_=cbf_d).then_inc(s_c.h, 16)
    sy.dma_start(out=cp1[:], in_=cp1_d).then_inc(s_c.h, 16)
    sy.dma_start(out=cf[:], in_=cf32_d).then_inc(s_c.h, 16)
    s_c.v += 48
    win_r = w_in.rearrange("(kb p) m -> p kb m", p=P)
    gp.dma_start(out=w_in_bf[:].rearrange("p (kb m) -> p kb m", kb=8), in_=win_r).then_inc(s_w.h, 16)
    wout_r = w_out.rearrange("(kb p) m -> p kb m", p=P)
    gp.dma_start(out=w_out_bf[:].rearrange("p (kb m) -> p kb m", kb=8), in_=wout_r).then_inc(s_w.h, 16)
    s_w.v += 32
    s_w.wait(ve)
    s_c.wait(ve)
    for kb in range(NB):
        ins = ve.tensor_scalar(w_in_bf[:, kb * 3328:(kb + 1) * 3328],
                               w_in_bf[:, kb * 3328:(kb + 1) * 3328],
                               cf[:, K_RMS + kb:K_RMS + kb + 1], None, ALU.mult)
    s_v.inc(ins)
    s_v.inc(ve.memset(lhsT_k[:], 0.0))
    s_v.inc(ve.memset(lhsT_x[:], 0.0))
    x_loaded = [None, None, None]
    for i in range(min(3, n_tiles)):
        sy.dma_start(out=x_sb[i][:].rearrange("p (jb t) -> p jb t", jb=NB),
                     in_=xT_r[:, :, i * T:(i + 1) * T]).then_inc(s_x.h, 16)
        s_x.v += 16
        x_loaded[i] = s_x.v

    spill_busy = [None]
    wv_after_rms = s_v.v

    st1 = {}
    iter_marks = {}
    for it in range(n_tiles + 2):
        f = it
        b = it - 1
        tb1 = it - 2
        have_f = f < n_tiles
        have_b = 0 <= b < n_tiles
        have_tb1 = 0 <= tb1 < n_tiles
        if it - 2 in iter_marks:
            mpe, mv, msc, mgp = iter_marks[it - 2]
            for eng in (te, ve, sc, gp, sy):
                s_pe.wait(eng, mpe)
                s_v.wait(eng, mv)
                s_sc.wait(eng, msc)
                s_gp.wait(eng, mgp)

        def emit_stats(tt_):
            ts_ = st1[tt_]
            s_sc.wait(te, ts_["vsq1"])
            s_sc.wait(te, ts_["vyb"])
            s_v.wait(te, ts_["vden_v"])
            for kb in range(NB):
                te.matmul(bank[6][0:1, 0:T], cbf[:, CC_ONES1:CC_ONES1 + 1],
                          y1b_bf[:, kb * T:(kb + 1) * T], start=(kb == 0), stop=False)
            for kb in range(NB):
                ins_ = te.matmul(bank[6][0:1, T:2 * T], cbf[:, CC_ONES1:CC_ONES1 + 1],
                                 sc_bf[:, kb * T:(kb + 1) * T], start=False,
                                 stop=(kb == NB - 1))
            ts_["vst1"] = s_pe.inc(ins_)

        if have_f:
            fpp = f % 2
            fxp = f % 3
            xa = x_sb[fxp]
            if b >= 1 and "vsq1" in st1.get(b - 1, {}):
                emit_stats(b - 1)
            s_x.wait(sc, x_loaded[fxp])
            s_c.wait(sc)
            if b >= 1:
                s_pe.wait(sc, st1[b - 1]["vst1"])
            vsq = s_sc.inc(sc.activation(sc_bf[:], xa[:], AF.Square))
            s_sc.wait(te, vsq)
            s_c.wait(te)
            if b >= 1:
                s_v.wait(te, st1[b - 1]["y1v"])
            for kb in range(NB):
                ins = te.matmul(bank[2][0:1, 0:T], cbf[:, CC_ONES1:CC_ONES1 + 1],
                                sc_bf[:, kb * T:(kb + 1) * T], start=(kb == 0), stop=(kb == NB - 1))
            vsumsq = s_pe.inc(ins)
            s_pe.wait(sc, vsumsq)
            sc.activation(sm[0:1, 0:T], bank[2][0:1, 0:T], AF.Ln,
                          bias=cf[0:1, K_E6:K_E6 + 1], scale=1.0 / D)
            vr = s_sc.inc(sc.activation(smb[0:1, 0:T], sm[0:1, 0:T], AF.Exp, scale=-0.5))
            s_sc.wait(te, vr)
            vrbc = s_pe.inc(te.matmul(bank[3][:, 0:T], cbf[0:1, CC_ONESR:CC_ONESR + P],
                                      smb[0:1, 0:T], start=True, stop=True))
            s_pe.wait(ve, vrbc)
            s_x.wait(ve, x_loaded[fxp])
            if b >= 1:
                s_pe.wait(ve, st1[b - 1]["vst1"])
            if it == 0:
                s_v.wait(ve, wv_after_rms)
            for kb in range(NB):
                ins = ve.tensor_tensor(h_bf[:, kb * T:(kb + 1) * T], xa[:, kb * T:(kb + 1) * T],
                                       bank[3][:, 0:T], ALU.mult)
            vh = s_v.inc(ins)
            st1[f] = {"vh": vh, "vsumsq": vsumsq}

        if have_b:
            bpp = b % 2
            bxp = b % 3
            sb_prev = st1[b]

        def back_vstep(stepname):
            if not have_b:
                return
            bs = st1[b]
            if stepname == "vx":
                for sem, val in bs["g_marks"]:
                    sem.wait(ve, val)
                for jb in range(NB):
                    ins = ve.scalar_tensor_tensor(cx[bpp][:, jb * T:(jb + 1) * T],
                                                  g_sb[bpp][:, jb * T:(jb + 1) * T], 1.0,
                                                  cx[bpp][:, jb * T:(jb + 1) * T],
                                                  ALU.subtract, ALU.mult)
                bs["vvx"] = s_v.inc(ins)
            elif stepname == "vk":
                for jb in range(NB):
                    ins = ve.scalar_tensor_tensor(ck[bpp][:, jb * T:(jb + 1) * T],
                                                  g_sb[bpp][:, jb * T:(jb + 1) * T], 1.0,
                                                  k_sb[bpp][:], ALU.subtract, ALU.mult)
                bs["vvk"] = s_v.inc(ins)
            elif stepname == "ext":
                if b >= 1 and "vtr" in st1[b - 1]:
                    s_pe.wait(ve, st1[b - 1]["vtr"])
                C0b = b * CPT
                for jb in range(NB):
                    ve.tensor_scalar(lhsT_k[0:64, jb * 128 + C0b:jb * 128 + C0b + CPT],
                                     ck[bpp][0:64, jb * T:(jb + 1) * T:CH], -0.125, None, ALU.mult)
                    ve.tensor_scalar(lhsT_k[64:128, jb * 128 + 64 + C0b:jb * 128 + 64 + C0b + CPT],
                                     ck[bpp][64:128, jb * T:(jb + 1) * T:CH], -0.125, None, ALU.mult)
                    ins = ve.tensor_scalar(stg4[:, jb * CPT:(jb + 1) * CPT],
                                           cx[bpp][:, jb * T:(jb + 1) * T:CH], -1.0, None, ALU.mult)
                bs["vex"] = s_v.inc(ins)
                s_v.wait(gp, bs["vex"])
                bs["vgm"] = s_gp.inc(gp.memset(g_sb[bpp][:, 0:NB * T:CH], 0.0))
            elif stepname == "scan":
                s_gp.wait(ve, bs["vgm"])
                ve.tensor_tensor_scan(cx[bpp][:], g_sb[bpp][:], cx[bpp][:], 0.0, ALU.mult, ALU.add)
                ins = ve.tensor_tensor_scan(ck[bpp][:], g_sb[bpp][:], ck[bpp][:], 0.0,
                                            ALU.mult, ALU.add)
                bs["vscan"] = s_v.inc(ins)
            elif stepname == "pfull":
                if have_f:
                    s_pe.wait(ve, st1[f]["vsumsq"])
                elif b >= 1:
                    s_pe.wait(ve, st1[b - 1]["vst1"])
                for sem, val in bs["qk_marks"]:
                    sem.wait(ve, val)
                for jb in range(NB):
                    ins = ve.tensor_tensor(sc_bf[:, jb * T:(jb + 1) * T], q_bf[bpp][:],
                                           ck[bpp][:, jb * T:(jb + 1) * T], ALU.mult)
                bs["vP"] = s_v.inc(ins)

        def tail_part(part):
            if not have_tb1:
                return
            ts_ = st1[tb1]
            if part == 0:
                s_pe.wait(ve, ts_["vst1"])
                ve.tensor_scalar(sm[0:1, 2 * T:3 * T], bank[6][0:1, 0:T], 1.0 / D, None, ALU.mult)
                ve.tensor_scalar(sm[0:1, 3 * T:4 * T], bank[6][0:1, T:2 * T], 1.0 / D,
                                 None, ALU.mult)
                ve.tensor_tensor(sm[0:1, 5 * T:6 * T], sm[0:1, 2 * T:3 * T],
                                 sm[0:1, 2 * T:3 * T], ALU.mult)
                vvar = s_v.inc(ve.tensor_tensor(sm[0:1, 3 * T:4 * T], sm[0:1, 3 * T:4 * T],
                                                sm[0:1, 5 * T:6 * T], ALU.subtract))
                ts_["vvar"] = vvar
            elif part == 1:
                s_v.wait(sc, ts_["vvar"])
                sc.activation(sm[0:1, 4 * T:5 * T], sm[0:1, 3 * T:4 * T], AF.Ln,
                              bias=cf[0:1, K_E5:K_E5 + 1])
                vr1 = s_sc.inc(sc.activation(sm[0:1, T:2 * T], sm[0:1, 4 * T:5 * T],
                                             AF.Exp, scale=-0.5))
                s_sc.wait(ve, vr1)
                ve.tensor_copy(smb[0:1, T:2 * T], sm[0:1, 2 * T:3 * T])
                ts_["vmb"] = s_v.inc(ve.tensor_copy(smb[0:1, 2 * T:3 * T], sm[0:1, T:2 * T]))
            elif part == 2:
                s_v.wait(te, ts_["vmb"])
                te.matmul(bank[0][:, 0:T], cbf[0:1, CC_ONESR:CC_ONESR + P], smb[0:1, T:2 * T],
                          start=True, stop=True)
                ts_["vbc"] = s_pe.inc(te.matmul(bank[1][:, 0:T], cbf[0:1, CC_ONESR:CC_ONESR + P],
                                                smb[0:1, 2 * T:3 * T], start=True, stop=True))
            else:
                s_pe.wait(ve, ts_["vbc"])
                nonlocal_spill = spill_busy[0]
                if nonlocal_spill is not None:
                    s_sp.wait(ve, nonlocal_spill)
                for jb in range(NB):
                    ve.tensor_tensor(y1n_st[:, jb * T:(jb + 1) * T],
                                     y1_sb[:, jb * T:(jb + 1) * T], bank[0][:, 0:T], ALU.subtract)
                for jb in range(NB):
                    ins2 = ve.tensor_tensor(y1n_st[:, jb * T:(jb + 1) * T],
                                            y1n_st[:, jb * T:(jb + 1) * T], bank[1][:, 0:T],
                                            ALU.mult)
                vln = s_v.inc(ins2)
                ts_["vln"] = vln
                s_v.wait(sy, vln)
                if debug and tb1 == 0:
                    sy.dma_start(out=dbg["d_y1n0"], in_=y1n_st[:]).then_inc(s_out.h, 16)
                    sy.dma_start(out=dbg["d_y10"], in_=y1_sb[:]).then_inc(s_out.h, 16)
                    s_out.v += 32
                sy.dma_start(out=sp_r[:, :, tb1 * T:(tb1 + 1) * T],
                             in_=y1n_st[:].rearrange("p (jb t) -> p jb t", jb=NB)
                             ).then_inc(s_sp.h, 16)
                s_sp.v += 16
                spill_busy[0] = s_sp.v

        def emit_transposes():
            bs = st1[b]
            C0 = b * CPT
            s_v.wait(te, bs["vex"])
            for jb in range(NB):
                ins = te.transpose(ps7[0:CPT, jb * 128:(jb + 1) * 128],
                                   stg4[:, jb * CPT:(jb + 1) * CPT], cbf[:, CC_ID:CC_ID + P])
            vtr = s_pe.inc(ins)
            bs["vtr"] = vtr
            s_pe.wait(ve, vtr)
            s_ext.wait(ve)
            vstg = s_v.inc(ve.tensor_copy(stgx[0:CPT, :], ps7[0:CPT, 0:8 * 128]))
            s_v.wait(sy, vstg)
            stgx_r = stgx[:].rearrange("c (jb ff) -> c jb ff", jb=NB)
            ltx_r = lhsT_x[:].rearrange("p (jb ff) -> p jb ff", jb=NB)
            sy.dma_start(out=ltx_r[C0:C0 + CPT, :, 0:64],
                         in_=stgx_r[0:CPT, :, 0:64]).then_inc(s_ext.h, 16)
            sy.dma_start(out=ltx_r[64 + C0:64 + C0 + CPT, :, 64:128],
                         in_=stgx_r[0:CPT, :, 64:128]).then_inc(s_ext.h, 16)
            s_ext.v += 32
            bs["tr_done"] = True

        if have_f:
            ms_order = (list(range(16, 24)) + list(range(0, 8)) + list(range(8, 16))
                        + [24, 25])
            groups = [ms_order[gg:gg + 4] for gg in range(0, PROJ_M, 4)]
            evac_done = {}
            g_marks = []
            qk_marks = []
            s_v.wait(te, st1[f]["vh"])
            if it == 0:
                s_w.wait(te)
                s_v.wait(te, wv_after_rms)
            back_steps = {}
            back_vstep("vx")
            back_vstep("vk")
            back_vstep("ext")
            back_steps = {1: "scan", 2: "pfull"}

            for gi, grp in enumerate(groups):
                pb = [bank[2], bank[3]] if gi % 2 == 0 else [bank[4], bank[5]]
                if gi >= 2:
                    for sem, val in evac_done[gi - 2]:
                        sem.wait(te, val)
                for mi, m in enumerate(grp):
                    pt = pb[mi // 2][:, (mi % 2) * T:(mi % 2) * T + T]
                    for kb in range(NB):
                        ins = te.matmul(pt, w_in_bf[:, kb * 3328 + m * 128:kb * 3328 + (m + 1) * 128],
                                        h_bf[:, kb * T:(kb + 1) * T],
                                        start=(kb == 0 and mi in (0, 2)), stop=(kb == NB - 1))
                vpg = s_pe.inc(ins)
                s_pe.wait(sc, vpg)
                s_pe.wait(ve, vpg)
                last_sc = None
                last_v = None
                for mi, m in enumerate(grp):
                    pt = pb[mi // 2][:, (mi % 2) * T:(mi % 2) * T + T]
                    if m < 8:
                        last_sc = sc.activation(z_bf[fpp][:, m * T:(m + 1) * T], pt, AF.Sigmoid)
                    elif m < 16:
                        last_v = ve.tensor_copy(cx[fpp][:, (m - 8) * T:(m - 7) * T], pt)
                    elif m < 24:
                        last_sc = sc.activation(g_sb[fpp][:, (m - 16) * T:(m - 15) * T], pt,
                                                AF.Sigmoid,
                                                bias=cf[:, K_GB + (m - 16):K_GB + (m - 16) + 1])
                    elif m == 24:
                        last_sc = sc.activation(q_bf[fpp][:], pt, AF.Copy)
                    else:
                        vq = s_sc.inc(last_sc)
                        last_sc = None
                        s_sc.wait(ve, vq)
                        qk_marks.append((s_sc, vq))
                        last_v = ve.tensor_copy(k_sb[fpp][:], pt)
                marks = []
                if last_sc is not None:
                    marks.append((s_sc, s_sc.inc(last_sc)))
                if last_v is not None:
                    marks.append((s_v, s_v.inc(last_v)))
                evac_done[gi] = marks
                if gi in (0, 1):
                    g_marks.extend(marks)
                if gi == 6:
                    qk_marks.extend(marks)
                if gi in back_steps:
                    back_vstep(back_steps[gi])
                if gi == 2:
                    if have_b:
                        emit_transposes()
                    tail_part(0)
                elif gi == 3:
                    tail_part(1)
                elif gi == 5:
                    tail_part(2)
                elif gi == 6:
                    tail_part(3)
            st1[f]["g_marks"] = g_marks
            st1[f]["qk_marks"] = qk_marks
            st1[f]["evac56"] = evac_done[5] + evac_done[6]
            st1[f]["proj_done"] = s_pe.v
        else:
            if b >= 1 and "vsq1" in st1.get(b - 1, {}) and "vst1" not in st1[b - 1]:
                emit_stats(b - 1)
            for stepname in ("vx", "vk", "ext", "scan", "pfull"):
                back_vstep(stepname)
            for part_ in (0, 1, 2, 3):
                tail_part(part_)

        if have_b:
            bs = st1[b]
            C0 = b * CPT
            t0b = b * T
            if not bs.get("tr_done"):
                emit_transposes()
            s_v.wait(te, bs["vP"])
            if b >= 1:
                s_v.wait(te, st1[b - 1]["vvar"])
            for j in range(NB):
                ins = te.matmul(bank[6][0:16, 0:T], cp1[:, P1_LSE + j * 16:P1_LSE + (j + 1) * 16],
                                sc_bf[:, j * T:(j + 1) * T], start=(j == 0), stop=(j == NB - 1))
            vlse = s_pe.inc(ins)
            s_pe.wait(sc, vlse)
            viw = s_sc.inc(sc.activation(sm[0:16, 4 * T:5 * T], bank[6][0:16, 0:T], AF.Exp,
                                         scale=0.125))
            s_ext.wait(te)
            for sem, val in bs["evac56"]:
                sem.wait(te, val)
            if have_f:
                for sem, val in st1[f]["evac56"]:
                    sem.wait(te, val)
            exp_done = {}

            def emit_wd(j):
                if j == 0:
                    s_sc.wait(te, viw)
                s_sc.wait(te, exp_done[j])
                ins = te.matmul(bank[6][0:16, T:2 * T], cp1[:, P1_WD + j * 16:P1_WD + (j + 1) * 16],
                                w_sb[:, j * T:(j + 1) * T], start=False, stop=(j == NB - 1))
                s_pe.inc(ins)

            for j in range(NB):
                sb_ = bank[[2, 3, 0, 1][j % 4]]
                if j in (2, 3) and b >= 1 and "vln" in st1[b - 1]:
                    s_v.wait(te, st1[b - 1]["vln"])
                if j >= 4:
                    s_sc.wait(te, exp_done[j - 4])
                te.matmul(sb_[:, 0:T], lhsT_k[:, j * 128:(j + 1) * 128], q_bf[bpp][:],
                          start=True, stop=False)
                ins = te.matmul(sb_[:, 0:T], cp1[0:4, P1_MASK + b * 128:P1_MASK + (b + 1) * 128],
                                cp1[0:4, P1_BLK:P1_BLK + T], start=False, stop=True)
                vsc_ = s_pe.inc(ins)
                s_pe.wait(sc, vsc_)
                exp_done[j] = s_sc.inc(sc.activation(w_sb[:, j * T:(j + 1) * T], sb_[:, 0:T], AF.Exp))
                if j >= 3:
                    emit_wd(j - 3)
            for j in range(NB - 3, NB):
                emit_wd(j)
            vwd = s_pe.v
            s_sc.wait(ve, viw)
            viwb = s_v.inc(ve.tensor_copy(iwb[0:16, :], sm[0:16, 4 * T:5 * T]))
            s_pe.wait(ve, vwd)
            ve.tensor_tensor(sm[0:16, 5 * T:6 * T], bank[6][0:16, T:2 * T],
                             sm[0:16, 4 * T:5 * T], ALU.add)
            ve.reciprocal(sm[0:16, 3 * T:4 * T], sm[0:16, 5 * T:6 * T])
            vden = s_v.inc(ve.tensor_copy(rdb_[0:16, :], sm[0:16, 3 * T:4 * T]))
            s_sc.wait(te, exp_done[NB - 1])
            pair_v = {}
            for jj in range(4):
                ob = bank[2 + (jj % 2)]
                ib = bank[4 + (jj % 2)]
                rb = bank[jj % 2]
                if jj >= 2:
                    s_v.wait(te, pair_v[jj - 2])
                if jj == 0:
                    s_v.wait(te, viwb)
                for h2_ in (0, 1):
                    j = 2 * jj + h2_
                    sel = cp1[0:16, P1_BCH + j * 128:P1_BCH + (j + 1) * 128]
                    te.matmul(ob[:, h2_ * T:h2_ * T + T], lhsT_x[:, j * 128:(j + 1) * 128],
                              w_sb[:, j * T:(j + 1) * T], start=(h2_ == 0), stop=True)
                    ins = te.matmul(ib[:, h2_ * T:h2_ * T + T], sel, iwb[0:16, 0:T],
                                    start=(h2_ == 0), stop=True)
                vja = s_pe.inc(ins)
                if jj == 0:
                    s_v.wait(te, vden)
                for h2_ in (0, 1):
                    j = 2 * jj + h2_
                    sel = cp1[0:16, P1_BCH + j * 128:P1_BCH + (j + 1) * 128]
                    ins = te.matmul(rb[:, h2_ * T:h2_ * T + T], sel, rdb_[0:16, 0:T],
                                    start=(h2_ == 0), stop=True)
                vjb = s_pe.inc(ins)
                cols = slice(2 * jj * T, (2 * jj + 2) * T)
                s_pe.wait(ve, vja)
                ve.tensor_tensor(t_f32[:], ib[:, 0:2 * T], cx[bpp][:, cols], ALU.mult)
                ve.tensor_tensor(t_f32[:], ob[:, 0:2 * T], t_f32[:], ALU.subtract)
                s_pe.wait(ve, vjb)
                ve.tensor_tensor(t_f32[:], t_f32[:], rb[:, 0:2 * T], ALU.mult)
                ins = ve.tensor_tensor(af_bf[:, cols], t_f32[:], z_bf[bpp][:, cols], ALU.mult)
                pair_v[jj] = s_v.inc(ins)
            vaf = s_v.v
            s_v.wait(te, vaf)
            if it == 1:
                s_w.wait(te)
            y1v = None
            vsq1 = None
            vyb = None
            for gi2 in range(2):
                pb2 = [bank[2], bank[3]] if gi2 == 0 else [bank[4], bank[5]]
                for mi in range(4):
                    m = gi2 * 4 + mi
                    pt = pb2[mi // 2][:, (mi % 2) * T:(mi % 2) * T + T]
                    for kb in range(NB):
                        ins = te.matmul(pt, w_out_bf[:, kb * 1024 + m * 128:kb * 1024 + (m + 1) * 128],
                                        af_bf[:, kb * T:(kb + 1) * T],
                                        start=(kb == 0 and mi in (0, 2)), stop=(kb == NB - 1))
                vwo = s_pe.inc(ins)
                s_pe.wait(ve, vwo)
                for mi in range(4):
                    m = gi2 * 4 + mi
                    pt = pb2[mi // 2][:, (mi % 2) * T:(mi % 2) * T + T]
                    ins = ve.tensor_tensor(y1_sb[:, m * T:(m + 1) * T], pt,
                                           x_sb[bxp][:, m * T:(m + 1) * T], ALU.add)
                y1v = s_v.inc(ins)
                half = slice(gi2 * 4 * T, (gi2 + 1) * 4 * T)
                s_v.wait(sc, y1v)
                if gi2 == 0:
                    s_pe.wait(sc, vlse)
                vsq1 = s_sc.inc(sc.activation(sc_bf[:, half], y1_sb[:, half], AF.Square))
                if gi2 == 0 and have_f:
                    s_pe.wait(sc, st1[f]["proj_done"])
                vyb = s_sc.inc(sc.activation(y1b_bf[:, half], y1_sb[:, half], AF.Copy))
            bs["y1v"] = y1v
            if b + 3 < n_tiles:
                s_v.wait(sy, y1v)
                sy.dma_start(out=x_sb[b % 3][:].rearrange("p (jb t) -> p jb t", jb=NB),
                             in_=xT_r[:, :, (b + 3) * T:(b + 4) * T]).then_inc(s_x.h, 16)
                s_x.v += 16
                x_loaded[b % 3] = s_x.v
            s_sc.wait(te, vsq1)
            s_sc.wait(te, vyb)
            s_v.wait(te, vden)
            for kb in range(NB):
                te.matmul(bank[6][0:1, 0:T], cbf[:, CC_ONES1:CC_ONES1 + 1],
                          y1b_bf[:, kb * T:(kb + 1) * T], start=(kb == 0), stop=False)
            for kb in range(NB):
                ins = te.matmul(bank[6][0:1, T:2 * T], cbf[:, CC_ONES1:CC_ONES1 + 1],
                                sc_bf[:, kb * T:(kb + 1) * T], start=False, stop=(kb == NB - 1))
            vst1 = s_pe.inc(ins)
            bs["vst1"] = vst1
            bs["pair_last"] = pair_v[3]

        iter_marks[it] = (s_pe.v, s_v.v, s_sc.v, s_gp.v)

    if not phase2:
        s_sp.wait(sy)
        s_out.wait(sy)
        return nc

    pe_fin, v_fin, sc_fin, gp_fin = s_pe.v, s_v.v, s_sc.v, s_gp.v
    for eng in (gp, sy):
        s_pe.wait(eng, pe_fin)
        s_v.wait(eng, v_fin)
        s_sc.wait(eng, sc_fin)
        s_sp.wait(eng)
        s_ext.wait(eng)
        s_x.wait(eng)
    ph1.close()
    ph1_psum.close()

    ph2 = ExitStack()

    def sb2(name, shape, dt):
        return ph2.enter_context(nc.sbuf_tensor(name, shape, dt))

    w_fc_bf = sb2("w_fc_bf", [P, 8 * FF], BF16)
    w_pj_bf = sb2("w_pj_bf", [P, NFB * 1024], BF16)
    yn_sb = [sb2(f"yn{i}", [P, NB * T], F32) for i in range(2)]
    yn_bf = sb2("yn_bf", [P, NB * T], BF16)
    h2_bf = sb2("h2_bf", [P, 6 * T], BF16)
    yc_sb = sb2("yc_sb", [P, NB * T], F32)
    mem_sb = sb2("mem_sb", [P, NB * T], F32)
    acc_bf = sb2("acc_bf", [P, NB * T], BF16)
    sig_bf = sb2("sig_bf", [P, NB * T], BF16)
    relu_f = sb2("relu_f", [P, T], F32)
    o_st = [sb2(f"o_st{i}", [P, NB * T], F32) for i in range(2)]
    bank2 = [ph2.enter_context(nc.psum_tensor(f"qb{i}", [P, 512], F32)) for i in range(8)]
    print('PH2 sbuf remaining:', nc.sbuf_bytes_remaining)

    wfc_r = w_fcT.rearrange("(kb p) f -> p kb f", p=P)
    wfc_sb_r = w_fc_bf[:].rearrange("p (kb f) -> p kb f", kb=8)
    fc_chunk_v = []
    for c in range(4):
        gp.dma_start(out=wfc_sb_r[:, :, c * 1024:(c + 1) * 1024],
                     in_=wfc_r[:, :, c * 1024:(c + 1) * 1024]).then_inc(s_w2.h, 16)
        s_w2.v += 16
        fc_chunk_v.append(s_w2.v)
    wpj_r = w_projT.rearrange("(fb p) d2 -> p fb d2", p=P)
    wpj_sb_r = w_pj_bf[:].rearrange("p (fb d2) -> p fb d2", fb=NFB)
    pj_chunk_v = []
    for c in range(4):
        gp.dma_start(out=wpj_sb_r[:, c * 8:(c + 1) * 8, :],
                     in_=wpj_r[:, c * 8:(c + 1) * 8, :]).then_inc(s_w2.h, 16)
        s_w2.v += 16
        pj_chunk_v.append(s_w2.v)

    s_sp.wait(sy)
    sy.dma_start(out=yn_sb[0][:].rearrange("p (jb t) -> p jb t", jb=NB),
                 in_=sp_r[:, :, 0:T]).then_inc(s_spi.h, 16)
    s_spi.v += 16
    yn_loaded = [s_spi.v, None]
    if n_tiles > 1:
        sy.dma_start(out=yn_sb[1][:].rearrange("p (jb t) -> p jb t", jb=NB),
                     in_=sp_r[:, :, T:2 * T]).then_inc(s_spi.h, 16)
        s_spi.v += 16
        yn_loaded[1] = s_spi.v
    yn_last_read = [None, None]
    out_busy = [None, None]

    st2 = {}
    for J in range(n_tiles + 2):
        f = J
        lb = J - 1
        tb = J - 2
        have_f = f < n_tiles
        have_lb = 0 <= lb < n_tiles
        have_tb = 0 <= tb < n_tiles

        if have_f:
            pp = f % 2
            if f >= 1 and "vynb_next" in st2.get(f - 1, {}):
                vynb = st2[f - 1]["vynb_next"]
            else:
                s_spi.wait(ve, yn_loaded[pp])
                if f >= 1:
                    s_pe.wait(ve, st2[f - 1]["vfc_last"])
                vynb = s_v.inc(ve.tensor_copy(yn_bf[:], yn_sb[pp][:]))

        if have_lb:
            ls = st2[lb]
            lpp = lb % 2
            s_pe.wait(ve, ls["vpj"])
            for mb in range(NB):
                pt = bank2[4 + mb // 2][:, (mb % 2) * T:(mb % 2) * T + T]
                ins = ve.tensor_scalar(yc_sb[:, mb * T:(mb + 1) * T], pt, -0.5, None, ALU.add)
            vyc = s_v.inc(ins)
            s_v.wait(sc, vyc)
            if lb >= 1 and "vst2" in st2.get(lb - 1, {}):
                s_pe.wait(sc, st2[lb - 1]["vst2"])
            vs1 = s_sc.inc(sc.activation(sig_bf[:], yc_sb[:], AF.Sign, bias=cf[:, K_NH:K_NH + 1]))
            s_sc.wait(ve, vs1)
            vacc = s_v.inc(ve.tensor_copy(acc_bf[:], sig_bf[:]))
            ve.tensor_scalar(mem_sb[:], yc_sb[:], 1.9, 0.45, ALU.mult, ALU.add)
            vm = s_v.inc(ve.scalar_tensor_tensor(mem_sb[:], sig_bf[:], -0.5, mem_sb[:],
                                                 ALU.mult, ALU.add))
            lstate = {"vm": vm, "vacc": vacc, "step": 2}
        else:
            lstate = None

        def ladder_step():
            if lstate is None or lstate["step"] > 5:
                return
            k = lstate["step"]
            s_v.wait(sc, lstate["vm"])
            vsg = s_sc.inc(sc.activation(sig_bf[:], mem_sb[:], AF.Sign,
                                         bias=cf[:, K_N1:K_N1 + 1]))
            s_sc.wait(ve, vsg)
            lstate["vacc"] = s_v.inc(ve.tensor_tensor(acc_bf[:], acc_bf[:], sig_bf[:], ALU.add))
            if k < 5:
                s_sc.wait(ve, vsg)
                ve.scalar_tensor_tensor(mem_sb[:], mem_sb[:], 0.9, yc_sb[:], ALU.mult, ALU.add)
                lstate["vm"] = s_v.inc(ve.scalar_tensor_tensor(mem_sb[:], sig_bf[:], -0.5,
                                                               mem_sb[:], ALU.mult, ALU.add))
            lstate["step"] = k + 1

        def tail_part(part):
            if not have_tb:
                return
            ts_ = st2[tb]
            tpp = tb % 2
            if part == 1:
                s_pe.wait(ve, ts_["vst2"])
                ve.tensor_scalar(sm[0:1, 2 * T:3 * T], bank2[3][0:1, 0:T], 1.0 / D, None, ALU.mult)
                ve.tensor_scalar(sm[0:1, 3 * T:4 * T], bank2[3][0:1, T:2 * T], 1.0 / D,
                                 None, ALU.mult)
                ve.tensor_tensor(sm[0:1, 0:T], sm[0:1, 2 * T:3 * T], sm[0:1, 2 * T:3 * T], ALU.mult)
                vvar2 = s_v.inc(ve.tensor_tensor(sm[0:1, 3 * T:4 * T], sm[0:1, 3 * T:4 * T],
                                                 sm[0:1, 0:T], ALU.subtract))
                s_v.wait(sc, vvar2)
                sc.activation(sm[0:1, 0:T], sm[0:1, 3 * T:4 * T], AF.Ln,
                              bias=cf[0:1, K_E5:K_E5 + 1])
                vr2 = s_sc.inc(sc.activation(sm[0:1, T:2 * T], sm[0:1, 0:T], AF.Exp, scale=-0.5))
                s_sc.wait(ve, vr2)
                ve.tensor_copy(smb[0:1, T:2 * T], sm[0:1, 2 * T:3 * T])
                ts_["vmb2"] = s_v.inc(ve.tensor_copy(smb[0:1, 2 * T:3 * T], sm[0:1, T:2 * T]))
            elif part == 2:
                s_v.wait(te, ts_["vmb2"])
                te.matmul(bank2[3][:, 0:T], cbf[0:1, CC_ONESR:CC_ONESR + P], smb[0:1, T:2 * T],
                          start=True, stop=True)
                ts_["vbc2"] = s_pe.inc(te.matmul(bank2[3][:, T:2 * T],
                                                 cbf[0:1, CC_ONESR:CC_ONESR + P],
                                                 smb[0:1, 2 * T:3 * T], start=True, stop=True))
            else:
                s_pe.wait(ve, ts_["vbc2"])
                for jb in range(NB):
                    ve.tensor_tensor(o_st[tpp][:, jb * T:(jb + 1) * T],
                                     o_st[tpp][:, jb * T:(jb + 1) * T], bank2[3][:, 0:T],
                                     ALU.subtract)
                for jb in range(NB):
                    ins = ve.tensor_tensor(o_st[tpp][:, jb * T:(jb + 1) * T],
                                           o_st[tpp][:, jb * T:(jb + 1) * T],
                                           bank2[3][:, T:2 * T], ALU.mult)
                vo = s_v.inc(ins)
                ts_["vo"] = vo
                s_v.wait(sy, vo)
                sy.dma_start(out=out_r[:, :, tb * T:(tb + 1) * T],
                             in_=o_st[tpp][:].rearrange("p (jb t) -> p jb t", jb=NB)
                             ).then_inc(s_out.h, 16)
                s_out.v += 16
                out_busy[tpp] = s_out.v

        if have_f:
            s_v.wait(te, vynb)
            if have_lb:
                s_v.wait(te, vyc)
            fc_evac = {}
            for g_ in range(16):
                bk = bank2[g_ % 3]
                if g_ >= 3:
                    s_sc.wait(te, fc_evac[g_ - 3])
                s_w2.wait(te, fc_chunk_v[(2 * g_) // 8])
                for fi in (0, 1):
                    Fb = 2 * g_ + fi
                    pt = bk[:, fi * T:fi * T + T]
                    for kb in range(NB):
                        ins = te.matmul(pt, w_fc_bf[:, kb * FF + Fb * 128:kb * FF + (Fb + 1) * 128],
                                        yn_bf[:, kb * T:(kb + 1) * T],
                                        start=(kb == 0 and fi == 0), stop=(kb == NB - 1))
                vfc = s_pe.inc(ins)
                s_pe.wait(sc, vfc)
                for fi in (0, 1):
                    pt = bk[:, fi * T:fi * T + T]
                    slot = (g_ % 3) * 2 + fi
                    sc.activation(relu_f[:], pt, AF.Relu)
                    ins = sc.activation(h2_bf[:, slot * T:(slot + 1) * T], relu_f[:], AF.Square)
                fc_evac[g_] = s_sc.inc(ins)
                if g_ == 1:
                    tail_part(1)
                elif g_ == 3:
                    ladder_step()
                elif g_ == 5:
                    ladder_step()
                    tail_part(2)
                elif g_ == 7:
                    ladder_step()
                elif g_ == 9:
                    ladder_step()
                    tail_part(3)
                for gp_ in ([g_ - 1] if g_ >= 1 else []) + ([15] if g_ == 15 else []):
                    s_sc.wait(te, fc_evac[gp_])
                    s_w2.wait(te, pj_chunk_v[(2 * gp_ + 1) // 8])
                    for fi in (0, 1):
                        Fb = 2 * gp_ + fi
                        slot = (gp_ % 3) * 2 + fi
                        for mb in range(NB):
                            pt = bank2[4 + mb // 2][:, (mb % 2) * T:(mb % 2) * T + T]
                            ins = te.matmul(pt,
                                            w_pj_bf[:, Fb * 1024 + mb * 128:Fb * 1024 + (mb + 1) * 128],
                                            h2_bf[:, slot * T:(slot + 1) * T],
                                            start=(Fb == 0 and mb in (0, 2, 4, 6)),
                                            stop=(Fb == NFB - 1))
            vpj = s_pe.inc(ins)
            st2[f] = {"vpj": vpj, "vfc_last": vfc}
        else:
            while lstate is not None and lstate["step"] <= 5:
                ladder_step()
            for part_ in (1, 2, 3):
                tail_part(part_)

        if have_lb:
            while lstate["step"] <= 5:
                ladder_step()
            vacc = lstate["vacc"]
            if out_busy[lpp] is not None:
                s_out.wait(ve, out_busy[lpp])
            vy2 = s_v.inc(ve.scalar_tensor_tensor(o_st[lpp][:], acc_bf[:], 0.1, yn_sb[lpp][:],
                                                  ALU.mult, ALU.add))
            yn_last_read[lpp] = vy2
            if lb + 2 < n_tiles:
                s_v.wait(sy, vy2)
                sy.dma_start(out=yn_sb[lpp][:].rearrange("p (jb t) -> p jb t", jb=NB),
                             in_=sp_r[:, :, (lb + 2) * T:(lb + 3) * T]).then_inc(s_spi.h, 16)
                s_spi.v += 16
                yn_loaded[lpp] = s_spi.v
            if f + 1 < n_tiles and f < n_tiles:
                s_spi.wait(gp, yn_loaded[(f + 1) % 2])
                s_pe.wait(gp, st2[f]["vfc_last"])
                st2[f]["vynb_next"] = s_gp.inc(gp.tensor_copy(yn_bf[:], yn_sb[(f + 1) % 2][:]))
            s_v.wait(sc, vy2)
            vsq2 = s_sc.inc(sc.activation(sig_bf[:], o_st[lpp][:], AF.Square))
            vy2b = s_v.inc(ve.tensor_copy(acc_bf[:], o_st[lpp][:]))
            s_sc.wait(te, vsq2)
            s_v.wait(te, vy2b)
            if have_tb and "vo" in st2[tb]:
                s_v.wait(te, st2[tb]["vo"])
            for kb in range(NB):
                te.matmul(bank2[3][0:1, 0:T], cbf[:, CC_ONES1:CC_ONES1 + 1],
                          acc_bf[:, kb * T:(kb + 1) * T], start=(kb == 0), stop=False)
            for kb in range(NB):
                ins = te.matmul(bank2[3][0:1, T:2 * T], cbf[:, CC_ONES1:CC_ONES1 + 1],
                                sig_bf[:, kb * T:(kb + 1) * T], start=False, stop=(kb == NB - 1))
            st2[lb]["vst2"] = s_pe.inc(ins)

    s_out.wait(sy)
    s_sp.wait(sy)
    ph2.close()
    return nc


def make_consts(rms_w, gate_bias):
    cbf = np.zeros((P, CCORE_W), np.float32)
    cbf[:, CC_ID:CC_ID + P] = np.eye(P)
    cbf[:, CC_ONES1] = 1.0
    cbf[0, CC_ONESR:CC_ONESR + P] = 1.0
    cp1 = np.zeros((P, CP1_W), np.float32)
    for j in range(NB):
        cp1[0:64, P1_LSE + j * 16 + 2 * j] = -1.0
        cp1[64:128, P1_LSE + j * 16 + 2 * j + 1] = -1.0
        cp1[0:64, P1_WD + j * 16 + 2 * j] = 1.0
        cp1[64:128, P1_WD + j * 16 + 2 * j + 1] = 1.0
    cc = np.arange(64)
    for i in range(NT):
        for jj in range(CPT):
            m = (cc >= i * CPT + jj).astype(np.float32) * NEGBIG
            cp1[jj, P1_MASK + i * 128:P1_MASK + i * 128 + 64] = m
            cp1[jj, P1_MASK + i * 128 + 64:P1_MASK + (i + 1) * 128] = m
    tt = np.arange(T)
    for jj in range(CPT):
        cp1[jj, P1_BLK:P1_BLK + T] = (tt // CH == jj).astype(np.float32)
    for j in range(NB):
        cp1[2 * j, P1_BCH + j * 128:P1_BCH + j * 128 + 64] = 1.0
        cp1[2 * j + 1, P1_BCH + j * 128 + 64:P1_BCH + (j + 1) * 128] = 1.0

    cf = np.zeros((P, CF_W), np.float32)
    cf[:, K_RMS:K_RMS + 8] = rms_w.reshape(8, P).T
    cf[:, K_GB:K_GB + 8] = gate_bias.reshape(8, P).T
    cf[:, K_E6] = 1e-6
    cf[:, K_E5] = 1e-5
    cf[:, K_N1] = -1.0
    cf[:, K_NH] = -0.5
    return cbf.astype(ml_dtypes.bfloat16), cp1.astype(ml_dtypes.bfloat16), cf


def kernel(L0_attn_hidden_states, L0_attn_rms_weight, L0_attn_W_in, L0_attn_gate_bias,
           L0_attn_W_out, L0_ffn_W_fc, L0_ffn_W_proj):
    hs = np.asarray(L0_attn_hidden_states, np.float32)
    xTs = np.ascontiguousarray(np.transpose(hs, (0, 2, 1)))
    w_in = np.ascontiguousarray(np.asarray(L0_attn_W_in, np.float32))
    w_out = np.ascontiguousarray(np.asarray(L0_attn_W_out, np.float32))
    w_fcT = np.ascontiguousarray(np.asarray(L0_ffn_W_fc, np.float32).T)
    w_projT = np.ascontiguousarray(np.asarray(L0_ffn_W_proj, np.float32).T)
    cbf, cp1, cf = make_consts(np.asarray(L0_attn_rms_weight, np.float32),
                               np.asarray(L0_attn_gate_bias, np.float32))
    nc = build()
    in_maps = [{"xT": xTs[b], "w_in": w_in, "w_out": w_out, "w_fcT": w_fcT,
                "w_projT": w_projT, "cbf": cbf, "cp1": cp1, "cf32": cf} for b in range(8)]
    res = run_bass_kernel_spmd(nc, in_maps, core_ids=list(range(8)))
    out = np.stack([np.asarray(r["outT"], np.float32).T for r in res.results])
    return np.ascontiguousarray(out)


# version 13
# speedup vs baseline: 1.1344x; 1.0051x over previous
import numpy as np
import ml_dtypes
from contextlib import ExitStack

import concourse.bass as bass
import concourse.mybir as mybir
from concourse.bass_utils import run_bass_kernel_spmd

F32 = mybir.dt.float32
BF16 = mybir.dt.bfloat16
AF = mybir.ActivationFunctionType
ALU = mybir.AluOpType

P = 128
T = 256
NT = 16
SEQ = 4096
D = 1024
NB = 8
CH = 64
CPT = T // CH
FF = 4096
NFB = 32
PROJ_M = 26
NEGBIG = -1.0e30

CC_ID = 0
CC_ONES1 = 128
CC_ONESR = 129
CCORE_W = 260
P1_LSE = 0
P1_WD = 128
P1_MASK = 256
P1_BLK = 2304
P1_BCH = 2560
CP1_W = 3584

K_RMS = 0
K_GB = 8
K_E6 = 16
K_E5 = 17
K_N1 = 18
K_NH = 19
CF_W = 24


_ACT_SET_CACHE = {}


def _act_set_id(nc, set_name):
    from concourse.hw_specs import get_activation_tables
    key = (nc.m.arch, set_name)
    if key not in _ACT_SET_CACHE:
        names = list(get_activation_tables(nc.m.arch).keys())
        _ACT_SET_CACHE[key] = names.index(set_name)
    return _ACT_SET_CACHE[key]


def preload_set(nc, sc, set_name):
    import concourse.mybir as mb
    return sc.add_instruction(mb.InstLoadActFuncSet(
        name=nc.get_next_instruction_name(),
        act_func_set_id=_act_set_id(nc, set_name), ins=[], outs=[]))


class Sem:
    def __init__(self, nc, name):
        self.h = nc.alloc_semaphore(name)
        self.v = 0

    def inc(self, instr, n=1):
        instr.then_inc(self.h, n)
        self.v += n
        return self.v

    def wait(self, eng, v=None):
        eng.wait_ge(self.h, self.v if v is None else v)


def build(debug=False, n_tiles=NT, phase2=True):
    nc = bass.Bass("TRN2", target_bir_lowering=False)
    te, ve, sc, gp, sy = nc.tensor, nc.vector, nc.scalar, nc.gpsimd, nc.sync

    xT = nc.dram_tensor("xT", [D, SEQ], F32, kind="ExternalInput").ap()
    w_in = nc.dram_tensor("w_in", [D, 3328], F32, kind="ExternalInput").ap()
    w_out = nc.dram_tensor("w_out", [D, D], F32, kind="ExternalInput").ap()
    w_fcT = nc.dram_tensor("w_fcT", [D, FF], F32, kind="ExternalInput").ap()
    w_projT = nc.dram_tensor("w_projT", [FF, D], F32, kind="ExternalInput").ap()
    cbf_d = nc.dram_tensor("cbf", [P, CCORE_W], BF16, kind="ExternalInput").ap()
    cp1_d = nc.dram_tensor("cp1", [P, CP1_W], BF16, kind="ExternalInput").ap()
    cf32_d = nc.dram_tensor("cf32", [P, CF_W], F32, kind="ExternalInput").ap()
    outT = nc.dram_tensor("outT", [D, SEQ], F32, kind="ExternalOutput").ap()
    spill_kind = "ExternalOutput" if debug else "Internal"
    y1n_sp = nc.dram_tensor("y1n_sp", [D, SEQ], F32, kind=spill_kind).ap()
    dbg = {}
    if debug:
        for name in ["d_cx", "d_ck", "d_y1", "d_den", "d_y1n0", "d_y10"]:
            dbg[name] = nc.dram_tensor(name, [P, 2048], F32, kind="ExternalOutput").ap()
        for name in ["d_w", "d_af", "d_ltx", "d_ltk"]:
            dbg[name] = nc.dram_tensor(name, [P, 2048], BF16, kind="ExternalOutput").ap()

    xT_r = xT.rearrange("(jb p) t -> p jb t", p=P)
    sp_r = y1n_sp.rearrange("(jb p) t -> p jb t", p=P)
    out_r = outT.rearrange("(jb p) t -> p jb t", p=P)

    s_xs = [Sem(nc, f"s_x{i}") for i in range(3)]
    s_spis = [Sem(nc, f"s_spi{i}") for i in range(2)]
    s_outs = [Sem(nc, f"s_out{i}") for i in range(2)]
    s_win = [Sem(nc, f"s_win{i}") for i in range(NB)]
    s_wfc = [Sem(nc, f"s_wfc{i}") for i in range(4)]
    s_wpj = [Sem(nc, f"s_wpj{i}") for i in range(4)]
    s_x = Sem(nc, "s_x")
    s_w = Sem(nc, "s_w")
    s_w2 = Sem(nc, "s_w2")
    s_c = Sem(nc, "s_c")
    s_sp = Sem(nc, "s_sp")
    s_spi = Sem(nc, "s_spi")
    s_out = Sem(nc, "s_out")
    s_ext = Sem(nc, "s_ext")
    s_pe = Sem(nc, "s_pe")
    s_v = Sem(nc, "s_v")
    s_sc = Sem(nc, "s_sc")
    s_gp = Sem(nc, "s_gp")

    cbf = nc.alloc_sbuf_tensor("cbf_sb", [P, CCORE_W], BF16)
    cf = nc.alloc_sbuf_tensor("cf32_sb", [P, CF_W], F32)
    sm = nc.alloc_sbuf_tensor("sm", [16, 6 * T], F32)
    smb = nc.alloc_sbuf_tensor("smb", [16, 3 * T], BF16)


    ph1_psum = ExitStack()
    bank = [ph1_psum.enter_context(nc.psum_tensor(f"pb{i}", [P, 512], F32)) for i in range(7)]
    ps7 = ph1_psum.enter_context(nc.psum_tensor("pb7", [P, 1024], BF16))

    ph1 = ExitStack()

    def sb1(name, shape, dt):
        return ph1.enter_context(nc.sbuf_tensor(name, shape, dt))

    w_in_bf = sb1("w_in_bf", [P, 8 * 3328], BF16)
    w_out_bf = sb1("w_out_bf", [P, 8 * 1024], BF16)
    x_sb = [sb1(f"x{i}", [P, NB * T], F32) for i in range(3)]
    h_bf = sb1("h_bf", [P, NB * T], BF16)
    g_sb = [sb1(f"g_sb{i}", [P, NB * T], F32) for i in range(2)]
    cx = [sb1(f"cx{i}", [P, NB * T], F32) for i in range(2)]
    ck = [sb1(f"ck{i}", [P, NB * T], F32) for i in range(2)]
    k_sb = [sb1(f"k_sb{i}", [P, T], F32) for i in range(2)]
    q_bf = [sb1(f"q_bf{i}", [P, T], BF16) for i in range(2)]
    z_bf = [sb1(f"z_bf{i}", [P, NB * T], BF16) for i in range(2)]
    sc_bf = sb1("sc_bf", [P, NB * T], BF16)
    w_sb = sb1("w_sb", [P, NB * T], BF16)
    af_bf = sb1("af_bf", [P, NB * T], BF16)
    t_f32 = sb1("t_f32", [P, 2 * T], F32)
    y1_sb = sb1("y1_sb", [P, NB * T], F32)
    y1b_bf = h_bf
    y1n_st = sb1("y1n_st", [P, NB * T], F32)
    lhsT_k = sb1("lhsT_k", [P, 8 * 128], BF16)
    lhsT_x = sb1("lhsT_x", [P, 8 * 128], BF16)
    stg4 = sb1("stg4", [P, 32], BF16)
    stgx = sb1("stgx", [4, 8 * 128], BF16)
    cp1 = sb1("cp1_sb", [P, CP1_W], BF16)
    iwb = sb1("iwb", [16, T], BF16)
    rdb_ = sb1("rdb", [16, T], BF16)

    print('PH1 sbuf remaining:', nc.sbuf_bytes_remaining)

    sy.dma_start(out=cbf[:], in_=cbf_d).then_inc(s_c.h, 16)
    sy.dma_start(out=cp1[:], in_=cp1_d).then_inc(s_c.h, 16)
    sy.dma_start(out=cf[:], in_=cf32_d).then_inc(s_c.h, 16)
    s_c.v += 48
    win_r = w_in.rearrange("(kb p) m -> p kb m", p=P)
    win_sb_r = w_in_bf[:].rearrange("p (kb m) -> p kb m", kb=8)
    for kb in range(NB):
        gp.dma_start(out=win_sb_r[:, kb:kb + 1, :], in_=win_r[:, kb:kb + 1, :]).then_inc(s_win[kb].h, 16)
        s_win[kb].v += 16
    wout_r = w_out.rearrange("(kb p) m -> p kb m", p=P)
    gp.dma_start(out=w_out_bf[:].rearrange("p (kb m) -> p kb m", kb=8), in_=wout_r).then_inc(s_w.h, 16)
    s_w.v += 16
    s_c.wait(ve)
    wv_fold = []
    for kb in range(NB):
        s_win[kb].wait(ve)
        ins = ve.tensor_scalar(w_in_bf[:, kb * 3328:(kb + 1) * 3328],
                               w_in_bf[:, kb * 3328:(kb + 1) * 3328],
                               cf[:, K_RMS + kb:K_RMS + kb + 1], None, ALU.mult)
        wv_fold.append(s_v.inc(ins))
    s_v.inc(ve.memset(lhsT_k[:], 0.0))
    s_v.inc(ve.memset(lhsT_x[:], 0.0))
    for i in range(min(3, n_tiles)):
        sy.dma_start(out=x_sb[i][:].rearrange("p (jb t) -> p jb t", jb=NB),
                     in_=xT_r[:, :, i * T:(i + 1) * T]).then_inc(s_xs[i].h, 16)
        s_xs[i].v += 16

    spill_busy = [None]
    wv_after_rms = s_v.v

    st1 = {}
    iter_marks = {}
    for it in range(n_tiles + 2):
        f = it
        b = it - 1
        tb1 = it - 2
        have_f = f < n_tiles
        have_b = 0 <= b < n_tiles
        have_tb1 = 0 <= tb1 < n_tiles
        if it - 2 in iter_marks:
            mpe, mv, msc, mgp = iter_marks[it - 2]
            for eng in (te, ve, sc, gp, sy):
                s_pe.wait(eng, mpe)
                s_v.wait(eng, mv)
                s_sc.wait(eng, msc)
                s_gp.wait(eng, mgp)

        def emit_stats(tt_):
            ts_ = st1[tt_]
            s_sc.wait(te, ts_["vsq1"])
            s_sc.wait(te, ts_["vyb"])
            s_v.wait(te, ts_["vden_v"])
            for kb in range(NB):
                te.matmul(bank[6][0:1, 0:T], cbf[:, CC_ONES1:CC_ONES1 + 1],
                          y1b_bf[:, kb * T:(kb + 1) * T], start=(kb == 0), stop=False)
            for kb in range(NB):
                ins_ = te.matmul(bank[6][0:1, T:2 * T], cbf[:, CC_ONES1:CC_ONES1 + 1],
                                 sc_bf[:, kb * T:(kb + 1) * T], start=False,
                                 stop=(kb == NB - 1))
            ts_["vst1"] = s_pe.inc(ins_)

        if have_f:
            fpp = f % 2
            fxp = f % 3
            xa = x_sb[fxp]
            if b >= 1 and "vsq1" in st1.get(b - 1, {}):
                emit_stats(b - 1)
            s_xs[fxp].wait(sc)
            s_c.wait(sc)
            if b >= 1:
                s_pe.wait(sc, st1[b - 1]["vst1"])
            vsq = s_sc.inc(sc.activation(sc_bf[:], xa[:], AF.Square))
            s_sc.wait(te, vsq)
            s_c.wait(te)
            if b >= 1:
                s_v.wait(te, st1[b - 1]["y1v"])
            for kb in range(NB):
                ins = te.matmul(bank[2][0:1, 0:T], cbf[:, CC_ONES1:CC_ONES1 + 1],
                                sc_bf[:, kb * T:(kb + 1) * T], start=(kb == 0), stop=(kb == NB - 1))
            vsumsq = s_pe.inc(ins)
            s_pe.wait(sc, vsumsq)
            sc.activation(sm[0:1, 0:T], bank[2][0:1, 0:T], AF.Ln,
                          bias=cf[0:1, K_E6:K_E6 + 1], scale=1.0 / D)
            vr = s_sc.inc(sc.activation(smb[0:1, 0:T], sm[0:1, 0:T], AF.Exp, scale=-0.5))
            s_sc.wait(te, vr)
            vrbc = s_pe.inc(te.matmul(bank[3][:, 0:T], cbf[0:1, CC_ONESR:CC_ONESR + P],
                                      smb[0:1, 0:T], start=True, stop=True))
            s_pe.wait(ve, vrbc)
            s_xs[fxp].wait(ve)
            if b >= 1:
                s_pe.wait(ve, st1[b - 1]["vst1"])
            if it == 0:
                s_v.wait(ve, wv_after_rms)
            for kb in range(NB):
                ins = ve.tensor_tensor(h_bf[:, kb * T:(kb + 1) * T], xa[:, kb * T:(kb + 1) * T],
                                       bank[3][:, 0:T], ALU.mult)
            vh = s_v.inc(ins)
            st1[f] = {"vh": vh, "vsumsq": vsumsq}

        if have_b:
            bpp = b % 2
            bxp = b % 3
            sb_prev = st1[b]

        def back_vstep(stepname):
            if not have_b:
                return
            bs = st1[b]
            if stepname == "vx":
                for sem, val in bs["g_marks"]:
                    sem.wait(ve, val)
                for jb in range(NB):
                    ins = ve.scalar_tensor_tensor(cx[bpp][:, jb * T:(jb + 1) * T],
                                                  g_sb[bpp][:, jb * T:(jb + 1) * T], 1.0,
                                                  cx[bpp][:, jb * T:(jb + 1) * T],
                                                  ALU.subtract, ALU.mult)
                bs["vvx"] = s_v.inc(ins)
            elif stepname == "vk":
                for jb in range(NB):
                    ins = ve.scalar_tensor_tensor(ck[bpp][:, jb * T:(jb + 1) * T],
                                                  g_sb[bpp][:, jb * T:(jb + 1) * T], 1.0,
                                                  k_sb[bpp][:], ALU.subtract, ALU.mult)
                bs["vvk"] = s_v.inc(ins)
            elif stepname == "ext":
                if b >= 1 and "vtr" in st1[b - 1]:
                    s_pe.wait(ve, st1[b - 1]["vtr"])
                C0b = b * CPT
                for jb in range(NB):
                    ve.tensor_scalar(lhsT_k[0:64, jb * 128 + C0b:jb * 128 + C0b + CPT],
                                     ck[bpp][0:64, jb * T:(jb + 1) * T:CH], -0.125, None, ALU.mult)
                    ve.tensor_scalar(lhsT_k[64:128, jb * 128 + 64 + C0b:jb * 128 + 64 + C0b + CPT],
                                     ck[bpp][64:128, jb * T:(jb + 1) * T:CH], -0.125, None, ALU.mult)
                    ins = ve.tensor_scalar(stg4[:, jb * CPT:(jb + 1) * CPT],
                                           cx[bpp][:, jb * T:(jb + 1) * T:CH], -1.0, None, ALU.mult)
                bs["vex"] = s_v.inc(ins)
                s_v.wait(gp, bs["vex"])
                bs["vgm"] = s_gp.inc(gp.memset(g_sb[bpp][:, 0:NB * T:CH], 0.0))
            elif stepname == "scan":
                s_gp.wait(ve, bs["vgm"])
                ve.tensor_tensor_scan(cx[bpp][:], g_sb[bpp][:], cx[bpp][:], 0.0, ALU.mult, ALU.add)
                ins = ve.tensor_tensor_scan(ck[bpp][:], g_sb[bpp][:], ck[bpp][:], 0.0,
                                            ALU.mult, ALU.add)
                bs["vscan"] = s_v.inc(ins)
            elif stepname == "pfull":
                if have_f:
                    s_pe.wait(ve, st1[f]["vsumsq"])
                elif b >= 1:
                    s_pe.wait(ve, st1[b - 1]["vst1"])
                for sem, val in bs["qk_marks"]:
                    sem.wait(ve, val)
                for jb in range(NB):
                    ins = ve.tensor_tensor(sc_bf[:, jb * T:(jb + 1) * T], q_bf[bpp][:],
                                           ck[bpp][:, jb * T:(jb + 1) * T], ALU.mult)
                bs["vP"] = s_v.inc(ins)

        def tail_part(part):
            if not have_tb1:
                return
            ts_ = st1[tb1]
            if part == 0:
                s_pe.wait(ve, ts_["vst1"])
                ve.tensor_scalar(sm[0:1, 2 * T:3 * T], bank[6][0:1, 0:T], 1.0 / D, None, ALU.mult)
                ve.tensor_scalar(sm[0:1, 3 * T:4 * T], bank[6][0:1, T:2 * T], 1.0 / D,
                                 None, ALU.mult)
                ve.tensor_tensor(sm[0:1, 5 * T:6 * T], sm[0:1, 2 * T:3 * T],
                                 sm[0:1, 2 * T:3 * T], ALU.mult)
                vvar = s_v.inc(ve.tensor_tensor(sm[0:1, 3 * T:4 * T], sm[0:1, 3 * T:4 * T],
                                                sm[0:1, 5 * T:6 * T], ALU.subtract))
                ts_["vvar"] = vvar
            elif part == 1:
                s_v.wait(sc, ts_["vvar"])
                sc.activation(sm[0:1, 4 * T:5 * T], sm[0:1, 3 * T:4 * T], AF.Ln,
                              bias=cf[0:1, K_E5:K_E5 + 1])
                vr1 = s_sc.inc(sc.activation(sm[0:1, T:2 * T], sm[0:1, 4 * T:5 * T],
                                             AF.Exp, scale=-0.5))
                s_sc.wait(ve, vr1)
                ve.tensor_copy(smb[0:1, T:2 * T], sm[0:1, 2 * T:3 * T])
                ts_["vmb"] = s_v.inc(ve.tensor_copy(smb[0:1, 2 * T:3 * T], sm[0:1, T:2 * T]))
            elif part == 2:
                s_v.wait(te, ts_["vmb"])
                te.matmul(bank[0][:, 0:T], cbf[0:1, CC_ONESR:CC_ONESR + P], smb[0:1, T:2 * T],
                          start=True, stop=True)
                ts_["vbc"] = s_pe.inc(te.matmul(bank[1][:, 0:T], cbf[0:1, CC_ONESR:CC_ONESR + P],
                                                smb[0:1, 2 * T:3 * T], start=True, stop=True))
            else:
                s_pe.wait(ve, ts_["vbc"])
                nonlocal_spill = spill_busy[0]
                if nonlocal_spill is not None:
                    s_sp.wait(ve, nonlocal_spill)
                for jb in range(NB):
                    ve.tensor_tensor(y1n_st[:, jb * T:(jb + 1) * T],
                                     y1_sb[:, jb * T:(jb + 1) * T], bank[0][:, 0:T], ALU.subtract)
                for jb in range(NB):
                    ins2 = ve.tensor_tensor(y1n_st[:, jb * T:(jb + 1) * T],
                                            y1n_st[:, jb * T:(jb + 1) * T], bank[1][:, 0:T],
                                            ALU.mult)
                vln = s_v.inc(ins2)
                ts_["vln"] = vln
                s_v.wait(sy, vln)
                if debug and tb1 == 0:
                    sy.dma_start(out=dbg["d_y1n0"], in_=y1n_st[:]).then_inc(s_out.h, 16)
                    sy.dma_start(out=dbg["d_y10"], in_=y1_sb[:]).then_inc(s_out.h, 16)
                    s_out.v += 32
                sy.dma_start(out=sp_r[:, :, tb1 * T:(tb1 + 1) * T],
                             in_=y1n_st[:].rearrange("p (jb t) -> p jb t", jb=NB)
                             ).then_inc(s_sp.h, 16)
                s_sp.v += 16
                spill_busy[0] = s_sp.v

        def emit_transposes():
            bs = st1[b]
            C0 = b * CPT
            s_v.wait(te, bs["vex"])
            for jb in range(NB):
                ins = te.transpose(ps7[0:CPT, jb * 128:(jb + 1) * 128],
                                   stg4[:, jb * CPT:(jb + 1) * CPT], cbf[:, CC_ID:CC_ID + P])
            vtr = s_pe.inc(ins)
            bs["vtr"] = vtr
            s_pe.wait(ve, vtr)
            s_ext.wait(ve)
            vstg = s_v.inc(ve.tensor_copy(stgx[0:CPT, :], ps7[0:CPT, 0:8 * 128]))
            s_v.wait(sy, vstg)
            stgx_r = stgx[:].rearrange("c (jb ff) -> c jb ff", jb=NB)
            ltx_r = lhsT_x[:].rearrange("p (jb ff) -> p jb ff", jb=NB)
            sy.dma_start(out=ltx_r[C0:C0 + CPT, :, 0:64],
                         in_=stgx_r[0:CPT, :, 0:64]).then_inc(s_ext.h, 16)
            sy.dma_start(out=ltx_r[64 + C0:64 + C0 + CPT, :, 64:128],
                         in_=stgx_r[0:CPT, :, 64:128]).then_inc(s_ext.h, 16)
            s_ext.v += 32
            bs["tr_done"] = True

        if have_f:
            ms_order = (list(range(16, 24)) + list(range(0, 8)) + list(range(8, 16))
                        + [24, 25])
            groups = [ms_order[gg:gg + 4] for gg in range(0, PROJ_M, 4)]
            evac_done = {}
            g_marks = []
            qk_marks = []
            s_v.wait(te, st1[f]["vh"])
            back_steps = {}
            back_vstep("vx")
            back_vstep("vk")
            back_vstep("ext")
            back_steps = {1: "scan", 2: "pfull"}

            for gi, grp in enumerate(groups):
                pb = [bank[2], bank[3]] if gi % 2 == 0 else [bank[4], bank[5]]
                if gi == 4 and f >= 2 and "pair_last" in st1.get(f - 2, {}):
                    s_v.wait(sc, st1[f - 2]["pair_last"])
                if gi >= 2:
                    for sem, val in evac_done[gi - 2]:
                        sem.wait(te, val)
                for kb in range(NB):
                    if it == 0:
                        s_v.wait(te, wv_fold[kb])
                    for mi, m in enumerate(grp):
                        pt = pb[mi // 2][:, (mi % 2) * T:(mi % 2) * T + T]
                        ins = te.matmul(pt, w_in_bf[:, kb * 3328 + m * 128:kb * 3328 + (m + 1) * 128],
                                        h_bf[:, kb * T:(kb + 1) * T],
                                        start=(kb == 0 and mi in (0, 2)), stop=(kb == NB - 1))
                vpg = s_pe.inc(ins)
                s_pe.wait(sc, vpg)
                s_pe.wait(ve, vpg)
                last_sc = None
                last_v = None
                for mi, m in enumerate(grp):
                    pt = pb[mi // 2][:, (mi % 2) * T:(mi % 2) * T + T]
                    if m < 8:
                        last_sc = sc.activation(z_bf[fpp][:, m * T:(m + 1) * T], pt, AF.Sigmoid)
                    elif m < 16:
                        last_sc = sc.activation(cx[fpp][:, (m - 8) * T:(m - 7) * T], pt, AF.Copy)
                    elif m < 24:
                        last_sc = sc.activation(g_sb[fpp][:, (m - 16) * T:(m - 15) * T], pt,
                                                AF.Sigmoid,
                                                bias=cf[:, K_GB + (m - 16):K_GB + (m - 16) + 1])
                    elif m == 24:
                        last_sc = sc.activation(q_bf[fpp][:], pt, AF.Copy)
                    else:
                        last_sc = sc.activation(k_sb[fpp][:], pt, AF.Copy)
                marks = []
                if last_sc is not None:
                    marks.append((s_sc, s_sc.inc(last_sc)))
                if last_v is not None:
                    marks.append((s_v, s_v.inc(last_v)))
                evac_done[gi] = marks
                if gi in (0, 1):
                    g_marks.extend(marks)
                if gi == 6:
                    qk_marks.extend(marks)
                if gi in back_steps:
                    back_vstep(back_steps[gi])
                if gi == 2:
                    if have_b:
                        emit_transposes()
                    tail_part(0)
                elif gi == 3:
                    tail_part(1)
                elif gi == 5:
                    tail_part(2)
                elif gi == 6:
                    tail_part(3)
            st1[f]["g_marks"] = g_marks
            st1[f]["qk_marks"] = qk_marks
            st1[f]["evac56"] = evac_done[5] + evac_done[6]
            st1[f]["proj_done"] = s_pe.v
        else:
            if b >= 1 and "vsq1" in st1.get(b - 1, {}) and "vst1" not in st1[b - 1]:
                emit_stats(b - 1)
            for stepname in ("vx", "vk", "ext", "scan", "pfull"):
                back_vstep(stepname)
            for part_ in (0, 1, 2, 3):
                tail_part(part_)

        if have_b:
            bs = st1[b]
            C0 = b * CPT
            t0b = b * T
            if not bs.get("tr_done"):
                emit_transposes()
            s_v.wait(te, bs["vP"])
            if b >= 1:
                s_v.wait(te, st1[b - 1]["vvar"])
            for j in range(NB):
                ins = te.matmul(bank[6][0:16, 0:T], cp1[:, P1_LSE + j * 16:P1_LSE + (j + 1) * 16],
                                sc_bf[:, j * T:(j + 1) * T], start=(j == 0), stop=(j == NB - 1))
            vlse = s_pe.inc(ins)
            s_pe.wait(sc, vlse)
            viw = s_sc.inc(sc.activation(sm[0:16, 4 * T:5 * T], bank[6][0:16, 0:T], AF.Exp,
                                         scale=0.125))
            s_ext.wait(te)
            for sem, val in bs["evac56"]:
                sem.wait(te, val)
            if have_f:
                for sem, val in st1[f]["evac56"]:
                    sem.wait(te, val)
            exp_done = {}

            def emit_wd(j):
                if j == 0:
                    s_sc.wait(te, viw)
                s_sc.wait(te, exp_done[j])
                ins = te.matmul(bank[6][0:16, T:2 * T], cp1[:, P1_WD + j * 16:P1_WD + (j + 1) * 16],
                                w_sb[:, j * T:(j + 1) * T], start=False, stop=(j == NB - 1))
                s_pe.inc(ins)

            for j in range(NB):
                sb_ = bank[[2, 3, 0, 1][j % 4]]
                if j in (2, 3) and b >= 1 and "vln" in st1[b - 1]:
                    s_v.wait(te, st1[b - 1]["vln"])
                if j >= 4:
                    s_sc.wait(te, exp_done[j - 4])
                te.matmul(sb_[:, 0:T], lhsT_k[:, j * 128:(j + 1) * 128], q_bf[bpp][:],
                          start=True, stop=False)
                ins = te.matmul(sb_[:, 0:T], cp1[0:4, P1_MASK + b * 128:P1_MASK + (b + 1) * 128],
                                cp1[0:4, P1_BLK:P1_BLK + T], start=False, stop=True)
                vsc_ = s_pe.inc(ins)
                s_pe.wait(sc, vsc_)
                exp_done[j] = s_sc.inc(sc.activation(w_sb[:, j * T:(j + 1) * T], sb_[:, 0:T], AF.Exp))
                if j >= 3:
                    emit_wd(j - 3)
            for j in range(NB - 3, NB):
                emit_wd(j)
            vwd = s_pe.v
            s_sc.wait(ve, viw)
            viwb = s_v.inc(ve.tensor_copy(iwb[0:16, :], sm[0:16, 4 * T:5 * T]))
            s_pe.wait(ve, vwd)
            ve.tensor_tensor(sm[0:16, 5 * T:6 * T], bank[6][0:16, T:2 * T],
                             sm[0:16, 4 * T:5 * T], ALU.add)
            ve.reciprocal(sm[0:16, 3 * T:4 * T], sm[0:16, 5 * T:6 * T])
            vden = s_v.inc(ve.tensor_copy(rdb_[0:16, :], sm[0:16, 3 * T:4 * T]))
            s_sc.wait(te, exp_done[NB - 1])
            pair_v = {}
            for jj in range(4):
                ob = bank[2 + (jj % 2)]
                ib = bank[4 + (jj % 2)]
                rb = bank[jj % 2]
                if jj >= 2:
                    s_v.wait(te, pair_v[jj - 2])
                if jj == 0:
                    s_v.wait(te, viwb)
                for h2_ in (0, 1):
                    j = 2 * jj + h2_
                    sel = cp1[0:16, P1_BCH + j * 128:P1_BCH + (j + 1) * 128]
                    te.matmul(ob[:, h2_ * T:h2_ * T + T], lhsT_x[:, j * 128:(j + 1) * 128],
                              w_sb[:, j * T:(j + 1) * T], start=(h2_ == 0), stop=True)
                    ins = te.matmul(ib[:, h2_ * T:h2_ * T + T], sel, iwb[0:16, 0:T],
                                    start=(h2_ == 0), stop=True)
                vja = s_pe.inc(ins)
                if jj == 0:
                    s_v.wait(te, vden)
                for h2_ in (0, 1):
                    j = 2 * jj + h2_
                    sel = cp1[0:16, P1_BCH + j * 128:P1_BCH + (j + 1) * 128]
                    ins = te.matmul(rb[:, h2_ * T:h2_ * T + T], sel, rdb_[0:16, 0:T],
                                    start=(h2_ == 0), stop=True)
                vjb = s_pe.inc(ins)
                cols = slice(2 * jj * T, (2 * jj + 2) * T)
                s_pe.wait(ve, vja)
                ve.tensor_tensor(t_f32[:], ib[:, 0:2 * T], cx[bpp][:, cols], ALU.mult)
                ve.tensor_tensor(t_f32[:], ob[:, 0:2 * T], t_f32[:], ALU.subtract)
                s_pe.wait(ve, vjb)
                ve.tensor_tensor(t_f32[:], t_f32[:], rb[:, 0:2 * T], ALU.mult)
                ins = ve.tensor_tensor(af_bf[:, cols], t_f32[:], z_bf[bpp][:, cols], ALU.mult)
                pair_v[jj] = s_v.inc(ins)
            vaf = s_v.v
            s_v.wait(te, vaf)
            if it == 1:
                s_w.wait(te)
            y1v = None
            vsq1 = None
            vyb = None
            for gi2 in range(2):
                pb2 = [bank[2], bank[3]] if gi2 == 0 else [bank[4], bank[5]]
                for mi in range(4):
                    m = gi2 * 4 + mi
                    pt = pb2[mi // 2][:, (mi % 2) * T:(mi % 2) * T + T]
                    for kb in range(NB):
                        ins = te.matmul(pt, w_out_bf[:, kb * 1024 + m * 128:kb * 1024 + (m + 1) * 128],
                                        af_bf[:, kb * T:(kb + 1) * T],
                                        start=(kb == 0 and mi in (0, 2)), stop=(kb == NB - 1))
                vwo = s_pe.inc(ins)
                s_pe.wait(ve, vwo)
                for mi in range(4):
                    m = gi2 * 4 + mi
                    pt = pb2[mi // 2][:, (mi % 2) * T:(mi % 2) * T + T]
                    ins = ve.tensor_tensor(y1_sb[:, m * T:(m + 1) * T], pt,
                                           x_sb[bxp][:, m * T:(m + 1) * T], ALU.add)
                y1v = s_v.inc(ins)
                half = slice(gi2 * 4 * T, (gi2 + 1) * 4 * T)
                s_v.wait(sc, y1v)
                if gi2 == 0:
                    s_pe.wait(sc, vlse)
                vsq1 = s_sc.inc(sc.activation(sc_bf[:, half], y1_sb[:, half], AF.Square))
                if gi2 == 0 and have_f:
                    s_pe.wait(sc, st1[f]["proj_done"])
                vyb = s_sc.inc(sc.activation(y1b_bf[:, half], y1_sb[:, half], AF.Copy))
            bs["y1v"] = y1v
            if b + 3 < n_tiles:
                s_v.wait(sy, y1v)
                sy.dma_start(out=x_sb[b % 3][:].rearrange("p (jb t) -> p jb t", jb=NB),
                             in_=xT_r[:, :, (b + 3) * T:(b + 4) * T]).then_inc(s_xs[b % 3].h, 16)
                s_xs[b % 3].v += 16
            s_sc.wait(te, vsq1)
            s_sc.wait(te, vyb)
            s_v.wait(te, vden)
            for kb in range(NB):
                te.matmul(bank[6][0:1, 0:T], cbf[:, CC_ONES1:CC_ONES1 + 1],
                          y1b_bf[:, kb * T:(kb + 1) * T], start=(kb == 0), stop=False)
            for kb in range(NB):
                ins = te.matmul(bank[6][0:1, T:2 * T], cbf[:, CC_ONES1:CC_ONES1 + 1],
                                sc_bf[:, kb * T:(kb + 1) * T], start=False, stop=(kb == NB - 1))
            vst1 = s_pe.inc(ins)
            bs["vst1"] = vst1
            bs["pair_last"] = pair_v[3]

        iter_marks[it] = (s_pe.v, s_v.v, s_sc.v, s_gp.v)

    if not phase2:
        s_sp.wait(sy)
        s_out.wait(sy)
        return nc

    pe_fin, v_fin, sc_fin, gp_fin = s_pe.v, s_v.v, s_sc.v, s_gp.v
    for eng in (gp, sy):
        s_pe.wait(eng, pe_fin)
        s_v.wait(eng, v_fin)
        s_sc.wait(eng, sc_fin)
        s_sp.wait(eng)
        s_ext.wait(eng)
        for sx_ in s_xs:
            sx_.wait(eng)
    ph1.close()
    ph1_psum.close()

    ph2 = ExitStack()

    def sb2(name, shape, dt):
        return ph2.enter_context(nc.sbuf_tensor(name, shape, dt))

    w_fc_bf = sb2("w_fc_bf", [P, 8 * FF], BF16)
    w_pj_bf = sb2("w_pj_bf", [P, NFB * 1024], BF16)
    yn_sb = [sb2(f"yn{i}", [P, NB * T], F32) for i in range(2)]
    yn_bf = sb2("yn_bf", [P, NB * T], BF16)
    h2_bf = sb2("h2_bf", [P, 6 * T], BF16)
    yc_sb = sb2("yc_sb", [P, NB * T], F32)
    mem_sb = sb2("mem_sb", [P, NB * T], F32)
    acc_bf = sb2("acc_bf", [P, NB * T], BF16)
    sig_bf = sb2("sig_bf", [P, NB * T], BF16)
    relu_f = sb2("relu_f", [P, T], F32)
    relu_v = sb2("relu_v", [P, T], F32)
    o_st = [sb2(f"o_st{i}", [P, NB * T], F32) for i in range(2)]
    bank2 = [ph2.enter_context(nc.psum_tensor(f"qb{i}", [P, 512], F32)) for i in range(8)]
    print('PH2 sbuf remaining:', nc.sbuf_bytes_remaining)

    wfc_r = w_fcT.rearrange("(kb p) f -> p kb f", p=P)
    wfc_sb_r = w_fc_bf[:].rearrange("p (kb f) -> p kb f", kb=8)
    for c in range(4):
        gp.dma_start(out=wfc_sb_r[:, :, c * 1024:(c + 1) * 1024],
                     in_=wfc_r[:, :, c * 1024:(c + 1) * 1024]).then_inc(s_wfc[c].h, 16)
        s_wfc[c].v += 16
    wpj_r = w_projT.rearrange("(fb p) d2 -> p fb d2", p=P)
    wpj_sb_r = w_pj_bf[:].rearrange("p (fb d2) -> p fb d2", fb=NFB)
    for c in range(4):
        gp.dma_start(out=wpj_sb_r[:, c * 8:(c + 1) * 8, :],
                     in_=wpj_r[:, c * 8:(c + 1) * 8, :]).then_inc(s_wpj[c].h, 16)
        s_wpj[c].v += 16

    s_sp.wait(sy)
    sy.dma_start(out=yn_sb[0][:].rearrange("p (jb t) -> p jb t", jb=NB),
                 in_=sp_r[:, :, 0:T]).then_inc(s_spis[0].h, 16)
    s_spis[0].v += 16
    if n_tiles > 1:
        sy.dma_start(out=yn_sb[1][:].rearrange("p (jb t) -> p jb t", jb=NB),
                     in_=sp_r[:, :, T:2 * T]).then_inc(s_spis[1].h, 16)
        s_spis[1].v += 16
    yn_last_read = [None, None]

    st2 = {}
    for J in range(n_tiles + 2):
        f = J
        lb = J - 1
        tb = J - 2
        have_f = f < n_tiles
        have_lb = 0 <= lb < n_tiles
        have_tb = 0 <= tb < n_tiles

        if have_f:
            pp = f % 2
            if f >= 1 and "vynb_next" in st2.get(f - 1, {}):
                vynb = st2[f - 1]["vynb_next"]
            else:
                s_spis[pp].wait(ve)
                if f >= 1:
                    s_pe.wait(ve, st2[f - 1]["vfc_last"])
                vynb = s_v.inc(ve.tensor_copy(yn_bf[:], yn_sb[pp][:]))

        if have_lb:
            ls = st2[lb]
            lpp = lb % 2
            s_pe.wait(ve, ls["vpj"])
            for mb in range(NB):
                pt = bank2[4 + mb // 2][:, (mb % 2) * T:(mb % 2) * T + T]
                ins = ve.tensor_scalar(yc_sb[:, mb * T:(mb + 1) * T], pt, -0.5, None, ALU.add)
            vyc = s_v.inc(ins)
            s_v.wait(sc, vyc)
            if lb >= 1 and "vst2" in st2.get(lb - 1, {}):
                s_pe.wait(sc, st2[lb - 1]["vst2"])
            vs1 = s_sc.inc(sc.activation(sig_bf[:], yc_sb[:], AF.Sign, bias=cf[:, K_NH:K_NH + 1]))
            s_sc.wait(ve, vs1)
            vacc = s_v.inc(ve.tensor_copy(acc_bf[:], sig_bf[:]))
            ve.tensor_scalar(mem_sb[:], yc_sb[:], 1.9, 0.45, ALU.mult, ALU.add)
            vm = s_v.inc(ve.scalar_tensor_tensor(mem_sb[:], sig_bf[:], -0.5, mem_sb[:],
                                                 ALU.mult, ALU.add))
            lstate = {"vm": vm, "vacc": vacc, "step": 2}
        else:
            lstate = None

        def ladder_step():
            if lstate is None or lstate["step"] > 5:
                return
            k = lstate["step"]
            s_v.wait(sc, lstate["vm"])
            vsg = s_sc.inc(sc.activation(sig_bf[:], mem_sb[:], AF.Sign,
                                         bias=cf[:, K_N1:K_N1 + 1]))
            s_sc.wait(ve, vsg)
            lstate["vacc"] = s_v.inc(ve.tensor_tensor(acc_bf[:], acc_bf[:], sig_bf[:], ALU.add))
            if k < 5:
                s_sc.wait(ve, vsg)
                ve.scalar_tensor_tensor(mem_sb[:], mem_sb[:], 0.9, yc_sb[:], ALU.mult, ALU.add)
                lstate["vm"] = s_v.inc(ve.scalar_tensor_tensor(mem_sb[:], sig_bf[:], -0.5,
                                                               mem_sb[:], ALU.mult, ALU.add))
            lstate["step"] = k + 1

        def tail_part(part):
            if not have_tb:
                return
            ts_ = st2[tb]
            tpp = tb % 2
            if part == 1:
                s_pe.wait(ve, ts_["vst2"])
                ve.tensor_scalar(sm[0:1, 2 * T:3 * T], bank2[3][0:1, 0:T], 1.0 / D, None, ALU.mult)
                ve.tensor_scalar(sm[0:1, 3 * T:4 * T], bank2[3][0:1, T:2 * T], 1.0 / D,
                                 None, ALU.mult)
                ve.tensor_tensor(sm[0:1, 0:T], sm[0:1, 2 * T:3 * T], sm[0:1, 2 * T:3 * T], ALU.mult)
                vvar2 = s_v.inc(ve.tensor_tensor(sm[0:1, 3 * T:4 * T], sm[0:1, 3 * T:4 * T],
                                                 sm[0:1, 0:T], ALU.subtract))
                s_v.wait(sc, vvar2)
                sc.activation(sm[0:1, 0:T], sm[0:1, 3 * T:4 * T], AF.Ln,
                              bias=cf[0:1, K_E5:K_E5 + 1])
                vr2 = s_sc.inc(sc.activation(sm[0:1, T:2 * T], sm[0:1, 0:T], AF.Exp, scale=-0.5))
                s_sc.wait(ve, vr2)
                ve.tensor_copy(smb[0:1, T:2 * T], sm[0:1, 2 * T:3 * T])
                ts_["vmb2"] = s_v.inc(ve.tensor_copy(smb[0:1, 2 * T:3 * T], sm[0:1, T:2 * T]))
            elif part == 2:
                s_v.wait(te, ts_["vmb2"])
                te.matmul(bank2[3][:, 0:T], cbf[0:1, CC_ONESR:CC_ONESR + P], smb[0:1, T:2 * T],
                          start=True, stop=True)
                ts_["vbc2"] = s_pe.inc(te.matmul(bank2[3][:, T:2 * T],
                                                 cbf[0:1, CC_ONESR:CC_ONESR + P],
                                                 smb[0:1, 2 * T:3 * T], start=True, stop=True))
            else:
                s_pe.wait(ve, ts_["vbc2"])
                for jb in range(NB):
                    ve.tensor_tensor(o_st[tpp][:, jb * T:(jb + 1) * T],
                                     o_st[tpp][:, jb * T:(jb + 1) * T], bank2[3][:, 0:T],
                                     ALU.subtract)
                for jb in range(NB):
                    ins = ve.tensor_tensor(o_st[tpp][:, jb * T:(jb + 1) * T],
                                           o_st[tpp][:, jb * T:(jb + 1) * T],
                                           bank2[3][:, T:2 * T], ALU.mult)
                vo = s_v.inc(ins)
                ts_["vo"] = vo
                s_v.wait(sy, vo)
                sy.dma_start(out=out_r[:, :, tb * T:(tb + 1) * T],
                             in_=o_st[tpp][:].rearrange("p (jb t) -> p jb t", jb=NB)
                             ).then_inc(s_outs[tpp].h, 16)
                s_outs[tpp].v += 16

        if have_f:
            s_v.wait(te, vynb)
            if have_lb:
                s_v.wait(te, vyc)
            fc_evac = {}
            for g_ in range(16):
                bk = bank2[g_ % 3]
                if g_ >= 3:
                    fc_evac[g_ - 3][0].wait(te, fc_evac[g_ - 3][1])
                s_wfc[(2 * g_) // 8].wait(te)
                for fi in (0, 1):
                    Fb = 2 * g_ + fi
                    pt = bk[:, fi * T:fi * T + T]
                    for kb in range(NB):
                        ins = te.matmul(pt, w_fc_bf[:, kb * FF + Fb * 128:kb * FF + (Fb + 1) * 128],
                                        yn_bf[:, kb * T:(kb + 1) * T],
                                        start=(kb == 0 and fi == 0), stop=(kb == NB - 1))
                vfc = s_pe.inc(ins)
                s_pe.wait(sc, vfc)
                for fi in (0, 1):
                    pt = bk[:, fi * T:fi * T + T]
                    slot = (g_ % 3) * 2 + fi
                    sc.activation(relu_f[:], pt, AF.Relu)
                    ins = sc.activation(h2_bf[:, slot * T:(slot + 1) * T], relu_f[:],
                                        AF.Square)
                fc_evac[g_] = (s_sc, s_sc.inc(ins))
                if g_ == 1:
                    tail_part(1)
                elif g_ == 3:
                    ladder_step()
                elif g_ == 5:
                    ladder_step()
                    tail_part(2)
                elif g_ == 7:
                    ladder_step()
                elif g_ == 9:
                    ladder_step()
                    tail_part(3)
                for gp_ in ([g_ - 1] if g_ >= 1 else []) + ([15] if g_ == 15 else []):
                    fc_evac[gp_][0].wait(te, fc_evac[gp_][1])
                    s_wpj[(2 * gp_ + 1) // 8].wait(te)
                    for fi in (0, 1):
                        Fb = 2 * gp_ + fi
                        slot = (gp_ % 3) * 2 + fi
                        for mb in range(NB):
                            pt = bank2[4 + mb // 2][:, (mb % 2) * T:(mb % 2) * T + T]
                            ins = te.matmul(pt,
                                            w_pj_bf[:, Fb * 1024 + mb * 128:Fb * 1024 + (mb + 1) * 128],
                                            h2_bf[:, slot * T:(slot + 1) * T],
                                            start=(Fb == 0 and mb in (0, 2, 4, 6)),
                                            stop=(Fb == NFB - 1))
            vpj = s_pe.inc(ins)
            st2[f] = {"vpj": vpj, "vfc_last": vfc}
        else:
            while lstate is not None and lstate["step"] <= 5:
                ladder_step()
            for part_ in (1, 2, 3):
                tail_part(part_)

        if have_lb:
            while lstate["step"] <= 5:
                ladder_step()
            vacc = lstate["vacc"]
            if s_outs[lpp].v > 0:
                s_outs[lpp].wait(ve)
            vy2 = s_v.inc(ve.scalar_tensor_tensor(o_st[lpp][:], acc_bf[:], 0.1, yn_sb[lpp][:],
                                                  ALU.mult, ALU.add))
            yn_last_read[lpp] = vy2
            if lb + 2 < n_tiles:
                s_v.wait(sy, vy2)
                sy.dma_start(out=yn_sb[lpp][:].rearrange("p (jb t) -> p jb t", jb=NB),
                             in_=sp_r[:, :, (lb + 2) * T:(lb + 3) * T]).then_inc(s_spis[lpp].h, 16)
                s_spis[lpp].v += 16
            if f + 1 < n_tiles and f < n_tiles:
                s_spis[(f + 1) % 2].wait(gp)
                s_pe.wait(gp, st2[f]["vfc_last"])
                st2[f]["vynb_next"] = s_gp.inc(gp.tensor_copy(yn_bf[:], yn_sb[(f + 1) % 2][:]))
            s_v.wait(sc, vy2)
            vsq2 = s_sc.inc(sc.activation(sig_bf[:], o_st[lpp][:], AF.Square))
            vy2b = s_v.inc(ve.tensor_copy(acc_bf[:], o_st[lpp][:]))
            s_sc.wait(te, vsq2)
            s_v.wait(te, vy2b)
            if have_tb and "vo" in st2[tb]:
                s_v.wait(te, st2[tb]["vo"])
            for kb in range(NB):
                te.matmul(bank2[3][0:1, 0:T], cbf[:, CC_ONES1:CC_ONES1 + 1],
                          acc_bf[:, kb * T:(kb + 1) * T], start=(kb == 0), stop=False)
            for kb in range(NB):
                ins = te.matmul(bank2[3][0:1, T:2 * T], cbf[:, CC_ONES1:CC_ONES1 + 1],
                                sig_bf[:, kb * T:(kb + 1) * T], start=False, stop=(kb == NB - 1))
            st2[lb]["vst2"] = s_pe.inc(ins)

    s_outs[0].wait(sy)
    s_outs[1].wait(sy)
    s_sp.wait(sy)
    ph2.close()
    return nc


def make_consts(rms_w, gate_bias):
    cbf = np.zeros((P, CCORE_W), np.float32)
    cbf[:, CC_ID:CC_ID + P] = np.eye(P)
    cbf[:, CC_ONES1] = 1.0
    cbf[0, CC_ONESR:CC_ONESR + P] = 1.0
    cp1 = np.zeros((P, CP1_W), np.float32)
    for j in range(NB):
        cp1[0:64, P1_LSE + j * 16 + 2 * j] = -1.0
        cp1[64:128, P1_LSE + j * 16 + 2 * j + 1] = -1.0
        cp1[0:64, P1_WD + j * 16 + 2 * j] = 1.0
        cp1[64:128, P1_WD + j * 16 + 2 * j + 1] = 1.0
    cc = np.arange(64)
    for i in range(NT):
        for jj in range(CPT):
            m = (cc >= i * CPT + jj).astype(np.float32) * NEGBIG
            cp1[jj, P1_MASK + i * 128:P1_MASK + i * 128 + 64] = m
            cp1[jj, P1_MASK + i * 128 + 64:P1_MASK + (i + 1) * 128] = m
    tt = np.arange(T)
    for jj in range(CPT):
        cp1[jj, P1_BLK:P1_BLK + T] = (tt // CH == jj).astype(np.float32)
    for j in range(NB):
        cp1[2 * j, P1_BCH + j * 128:P1_BCH + j * 128 + 64] = 1.0
        cp1[2 * j + 1, P1_BCH + j * 128 + 64:P1_BCH + (j + 1) * 128] = 1.0

    cf = np.zeros((P, CF_W), np.float32)
    cf[:, K_RMS:K_RMS + 8] = rms_w.reshape(8, P).T
    cf[:, K_GB:K_GB + 8] = gate_bias.reshape(8, P).T
    cf[:, K_E6] = 1e-6
    cf[:, K_E5] = 1e-5
    cf[:, K_N1] = -1.0
    cf[:, K_NH] = -0.5
    return cbf.astype(ml_dtypes.bfloat16), cp1.astype(ml_dtypes.bfloat16), cf


def kernel(L0_attn_hidden_states, L0_attn_rms_weight, L0_attn_W_in, L0_attn_gate_bias,
           L0_attn_W_out, L0_ffn_W_fc, L0_ffn_W_proj):
    hs = np.asarray(L0_attn_hidden_states, np.float32)
    xTs = np.ascontiguousarray(np.transpose(hs, (0, 2, 1)))
    w_in = np.ascontiguousarray(np.asarray(L0_attn_W_in, np.float32))
    w_out = np.ascontiguousarray(np.asarray(L0_attn_W_out, np.float32))
    w_fcT = np.ascontiguousarray(np.asarray(L0_ffn_W_fc, np.float32).T)
    w_projT = np.ascontiguousarray(np.asarray(L0_ffn_W_proj, np.float32).T)
    cbf, cp1, cf = make_consts(np.asarray(L0_attn_rms_weight, np.float32),
                               np.asarray(L0_attn_gate_bias, np.float32))
    nc = build()
    in_maps = [{"xT": xTs[b], "w_in": w_in, "w_out": w_out, "w_fcT": w_fcT,
                "w_projT": w_projT, "cbf": cbf, "cp1": cp1, "cf32": cf} for b in range(8)]
    res = run_bass_kernel_spmd(nc, in_maps, core_ids=list(range(8)))
    out = np.stack([np.asarray(r["outT"], np.float32).T for r in res.results])
    return np.ascontiguousarray(out)
